# version 03
# baseline (speedup 1.0000x reference)
import math
import os
from contextlib import ExitStack

_KPH = int(os.environ.get("KPH", "99"))
_KP3G = int(os.environ.get("KP3G", "32"))
_KNOMASK = os.environ.get("KNOMASK", "0") == "1"
_KNOEXP2 = os.environ.get("KNOEXP2", "0") == "1"
_KNOEPI = os.environ.get("KNOEPI", "0") == "1"
_KNOROW = os.environ.get("KNOROW", "0") == "1"

import numpy as np
import ml_dtypes

import concourse.bass as bass
import concourse.mybir as mybir
from concourse.bass_utils import run_bass_kernel_spmd

F32 = mybir.dt.float32
BF16 = mybir.dt.bfloat16
I32 = mybir.dt.int32
AF = mybir.ActivationFunctionType
ALU = mybir.AluOpType
AX = mybir.AxisListType

B, T, D = 4, 4096, 4096
HD = 128
NHC = 16
NGC = 4
TH = T // 2
HF = 11008
DQ = 1024
HQ = 2752
HQP = 2816
SCALE = 1.0 / math.sqrt(HD)
EPS = 1e-5


class Ctr:
    def __init__(self, handle):
        self.h = handle
        self.n = 0

    def inc(self, instr, k=1):
        instr.then_inc(self.h, k)
        self.n += k
        return self.n


def sb3(sb_slice, tiles):
    return sb_slice.rearrange("p (t c) -> p t c", t=tiles)


def dr3(dr_slice):
    return dr_slice.rearrange("(t p) c -> p t c", p=128)


def build_program():
    nc = bass.Bass("TRN2", target_bir_lowering=False)

    x = nc.dram_tensor("x", [T, D], F32, kind="ExternalInput")
    xh = nc.dram_tensor("xh", [TH, D], F32, kind="ExternalInput")
    wq = nc.dram_tensor("wq", [D, NHC * HD], F32, kind="ExternalInput")
    wk = nc.dram_tensor("wk", [D, NGC * HD], F32, kind="ExternalInput")
    wv = nc.dram_tensor("wv", [D, NGC * HD], F32, kind="ExternalInput")
    wo = nc.dram_tensor("wo", [D, D], F32, kind="ExternalInput")
    w1 = nc.dram_tensor("w1", [HF, DQ], F32, kind="ExternalInput")
    w3 = nc.dram_tensor("w3", [HF, DQ], F32, kind="ExternalInput")
    w2 = nc.dram_tensor("w2", [D, HQ], F32, kind="ExternalInput")
    a1 = nc.dram_tensor("a1", [2, D], F32, kind="ExternalInput")
    b1 = nc.dram_tensor("b1", [HF, 2], F32, kind="ExternalInput")
    a3 = nc.dram_tensor("a3", [2, D], F32, kind="ExternalInput")
    b3 = nc.dram_tensor("b3", [HF, 2], F32, kind="ExternalInput")
    a2 = nc.dram_tensor("a2", [2, HF], F32, kind="ExternalInput")
    b2 = nc.dram_tensor("b2", [D, 2], F32, kind="ExternalInput")
    csa = nc.dram_tensor("csa", [128, T], BF16, kind="ExternalInput")
    csb = nc.dram_tensor("csb", [128, T], BF16, kind="ExternalInput")
    swp = nc.dram_tensor("swp", [128, 128], BF16, kind="ExternalInput")
    ident = nc.dram_tensor("ident", [128, 128], BF16, kind="ExternalInput")
    onesb = nc.dram_tensor("onesb", [128, 128], BF16, kind="ExternalInput")
    maskp = nc.dram_tensor("maskp", [128, 2048], BF16, kind="ExternalInput")
    agoff = nc.dram_tensor("agoff", [1, 1], I32, kind="ExternalInput")
    o = nc.dram_tensor("o", [TH, D], F32, kind="ExternalOutput")

    xbf = nc.dram_tensor("xbf", [T, D], BF16)
    w1bf = nc.dram_tensor("w1bf", [HF, DQ], BF16)
    w3bf = nc.dram_tensor("w3bf", [HF, DQ], BF16)
    w2bf = nc.dram_tensor("w2bf", [D, HQP], BF16)
    w1t = nc.dram_tensor("w1t", [DQ, HF], BF16)
    w3t = nc.dram_tensor("w3t", [DQ, HF], BF16)
    w2t = nc.dram_tensor("w2t", [HQP, D], BF16)
    b1td = nc.dram_tensor("b1td", [2, HF], BF16)
    b3td = nc.dram_tensor("b3td", [2, HF], BF16)
    b2td = nc.dram_tensor("b2td", [2, D], BF16)
    kT = nc.dram_tensor("kT", [NGC * HD, T], BF16)
    vS = nc.dram_tensor("vS", [T, NGC * HD], BF16)
    qT = nc.dram_tensor("qT", [NHC * HD, T], BF16)
    ctx_lo = nc.dram_tensor("ctx_lo", [NHC * HD, TH], BF16)
    ctx_hi = nc.dram_tensor("ctx_hi", [NHC * HD, TH], BF16)
    ctxAG = nc.dram_tensor("ctxAG", [2 * D, TH], BF16)
    ctxF = nc.dram_tensor("ctxF", [D, TH], BF16)
    ypre = nc.dram_tensor("ypre", [TH, D], F32)
    y32 = nc.dram_tensor("y32", [TH, D], F32)
    ybf = nc.dram_tensor("ybf", [TH, D], BF16)
    uT = nc.dram_tensor("uT", [HF, TH], BF16)
    mid2d = nc.dram_tensor("mid2d", [2, TH], BF16)
    zpre = nc.dram_tensor("zpre", [TH, D], F32)

    with ExitStack() as CTX:
        E = CTX.enter_context

        ones_sb = E(nc.sbuf_tensor("ones_sb", [128, 128], BF16))
        onesf_sb = E(nc.sbuf_tensor("onesf_sb", [128, 32], F32))
        eps_sb = E(nc.sbuf_tensor("eps_sb", [128, 1], F32))
        nc.const_aps.register(F32, EPS, eps_sb[:, 0:1])
        mneg_sb = E(nc.sbuf_tensor("mneg_sb", [128, 128], F32))
        qmax2_sb = E(nc.sbuf_tensor("qmax2_sb", [128, 128], F32))
        kmax2_sb = E(nc.sbuf_tensor("kmax2_sb", [128, NGC], F32))
        stat_sb = E(nc.sbuf_tensor("stat_sb", [128, 8], F32))
        a1T_sb = E(nc.sbuf_tensor("a1T_sb", [128, 64], BF16))
        a3T_sb = E(nc.sbuf_tensor("a3T_sb", [128, 64], BF16))
        a2T_sb = E(nc.sbuf_tensor("a2T_sb", [128, 172], BF16))
        ysum_sb = E(nc.sbuf_tensor("ysum_sb", [128, 16], F32))
        ysq_sb = E(nc.sbuf_tensor("ysq_sb", [128, 16], F32))
        zsum_sb = E(nc.sbuf_tensor("zsum_sb", [128, 16], F32))
        zsq_sb = E(nc.sbuf_tensor("zsq_sb", [128, 16], F32))
        agoff_sb = E(nc.sbuf_tensor("agoff_sb", [1, 1], I32))

        ph = Ctr(E(nc.semaphore("ph")))
        dm = [Ctr(E(nc.semaphore(f"dm{i}"))) for i in range(22)]

        def wt(eng, ev):
            eng.wait_ge(ev[0].h, ev[1])
        c1 = Ctr(E(nc.semaphore("s1")))
        c2 = Ctr(E(nc.semaphore("s2")))
        c3 = Ctr(E(nc.semaphore("s3")))
        c4 = Ctr(E(nc.semaphore("s4")))
        c5 = Ctr(E(nc.semaphore("s5")))
        c6 = Ctr(E(nc.semaphore("s6")))
        cc1 = Ctr(E(nc.semaphore("cc1")))
        cc2 = Ctr(E(nc.semaphore("cc2")))
        vch = Ctr(E(nc.semaphore("vch")))
        sch = Ctr(E(nc.semaphore("sch")))

        def endsec(eng, own=None, val=None):
            if own is not None:
                eng.wait_ge(own.h, val if val is not None else own.n)
            ph.inc(eng.nop())

        def _fin():
            ph_f = ph.n
            with nc.Block() as blk:
                @blk.sync
                def _(sy):
                    sy.wait_ge(ph.h, ph_f)
                    dm[14].inc(sy.dma_start(o.ap(), x.ap()[0:TH, :]), 16)
                    sy.wait_ge(dm[14].h, dm[14].n)

        with ExitStack() as P:
            tp0 = P.enter_context(nc.sbuf_tensor("tp0", [128, HF], BF16))
            tp1 = P.enter_context(nc.sbuf_tensor("tp1", [128, HF], BF16))
            ab_sb = P.enter_context(nc.sbuf_tensor("ab_sb", [2, HF], BF16))
            bb_sb = P.enter_context(nc.sbuf_tensor("bb_sb", [128, 172], BF16))
            btT_sb = P.enter_context(nc.sbuf_tensor("btT_sb", [2, HF], BF16))
            id_sb = P.enter_context(nc.sbuf_tensor("id_sb", [128, 128], BF16))
            ps_a = P.enter_context(nc.psum_tensor("ps_a", [128, 256], BF16))
            ps_b = P.enter_context(nc.psum_tensor("ps_b", [128, 256], BF16))
            tp = [tp0, tp1]

            ph_in = ph.n
            GEV = {
                "xbf0": (dm[0], dm[0].n + 16),
                "xbf1": (dm[1], dm[1].n + 16),
                "wcasts": (dm[2], dm[2].n + 48),
                "ab_a1": (dm[3], dm[3].n + 16),
                "ab_a3": (dm[3], dm[3].n + 32),
                "ab_a2": (dm[3], dm[3].n + 48),
                "bb_b1": (dm[4], dm[4].n + 16),
                "bb_b3": (dm[4], dm[4].n + 32),
                "bb_b2": (dm[4], dm[4].n + 48),
                "consts": (dm[5], dm[5].n + 48),
            }
            AEV = {}
            n = c2.n
            n += 1; AEV["a1"] = n
            bev1 = {}
            for t in range(86):
                n += 1; bev1[t] = n
            AEV["b1"] = n
            n += 1; AEV["a3"] = n
            bev3 = {}
            for t in range(86):
                n += 1; bev3[t] = n
            AEV["b3"] = n
            n += 1; AEV["a2"] = n
            bev2 = {}
            for t in range(32):
                n += 1; bev2[t] = n
            AEV["b2"] = n
            TEV = {"b1g": {}, "b3g": {}, "b2g": {}}
            n = c1.n
            n += 1; TEV["a1"] = n
            for t in range(86):
                n += 1; TEV["b1g"][t] = n
            n += 1; TEV["a3"] = n
            for t in range(86):
                n += 1; TEV["b3g"][t] = n
            n += 1; TEV["a2"] = n
            for t in range(32):
                n += 1; TEV["b2g"][t] = n
            SEV = {}
            tpn = [dm[6].n, dm[7].n]
            for wid, ntile in (("w1", 8), ("w3", 8), ("w2", 22)):
                for wi in range(ntile):
                    pq = wi % 2
                    tpn[pq] += 16
                    SEV[(wid + "tp", wi)] = (dm[6 + pq], tpn[pq])
                    tpn[pq] += 16
                    SEV[(wid + "st", wi)] = (dm[6 + pq], tpn[pq])
            nb = dm[8].n
            for k in ["b1td", "b3td", "b2td"]:
                nb += 16
                SEV[k] = (dm[8], nb)

            with nc.Block() as blk:
                @blk.gpsimd
                def _(g):
                    dm[0].inc(g.dma_start(xbf.ap()[0:TH, :], x.ap()[0:TH, :]), 16)
                    dm[1].inc(g.dma_start(xbf.ap()[TH:T, :], x.ap()[TH:T, :]), 16)
                    dm[2].inc(g.dma_start(w1bf.ap(), w1.ap()), 16)
                    dm[2].inc(g.dma_start(w3bf.ap(), w3.ap()), 16)
                    dm[2].inc(g.dma_start(w2bf.ap()[:, 0:HQ], w2.ap()), 16)
                    dm[3].inc(g.dma_start(ab_sb[0:2, 0:D], a1.ap()), 16)
                    dm[4].inc(g.dma_start(sb3(bb_sb[:, 0:172], 86),
                                          dr3(b1.ap())), 16)
                    g.wait_ge(c1.h, TEV["b1g"][85])
                    g.wait_ge(c1.h, TEV["a1"])
                    dm[3].inc(g.dma_start(ab_sb[0:2, 0:D], a3.ap()), 16)
                    dm[4].inc(g.dma_start(sb3(bb_sb[:, 0:172], 86),
                                          dr3(b3.ap())), 16)
                    g.wait_ge(c1.h, TEV["b3g"][85])
                    g.wait_ge(c1.h, TEV["a3"])
                    dm[3].inc(g.dma_start(ab_sb[0:2, 0:HF], a2.ap()), 16)
                    dm[4].inc(g.dma_start(sb3(bb_sb[:, 0:64], 32),
                                          dr3(b2.ap())), 16)
                    for i_ in range(5):
                        g.wait_ge(dm[i_].h, dm[i_].n)
                    ph.inc(g.nop())

                @blk.sync
                def _(sy):
                    dm[5].inc(sy.dma_start(id_sb[:, :], ident.ap()), 16)
                    dm[5].inc(sy.dma_start(ones_sb[:, :], onesb.ap()), 16)
                    dm[5].inc(sy.dma_start(agoff_sb[:, :], agoff.ap()), 16)
                    last_st = [None, None]
                    for wt_, wbf, ntile, wid in (
                            (w1t, w1bf, 8, "w1"),
                            (w3t, w3bf, 8, "w3"),
                            (w2t, w2bf, 22, "w2")):
                        width = HF if wid != "w2" else D
                        for wi in range(ntile):
                            buf = tp[wi % 2]
                            wt(sy, GEV["wcasts"])
                            if last_st[wi % 2] is not None:
                                wt(sy, last_st[wi % 2])
                            SEV[(wid + "tp", wi)][0].inc(sy.dma_start(
                                buf[:, 0:width],
                                wbf.ap()[:, wi * 128:(wi + 1) * 128],
                                transpose=True), 16)
                            wt(sy, SEV[(wid + "tp", wi)])
                            SEV[(wid + "st", wi)][0].inc(sy.dma_start(
                                wt_.ap()[wi * 128:(wi + 1) * 128, :],
                                buf[:, 0:width]), 16)
                            last_st[wi % 2] = SEV[(wid + "st", wi)]
                    sy.wait_ge(c2.h, AEV["b1"])
                    dm[8].inc(sy.dma_start(b1td.ap(), btT_sb[0:2, :]), 16)
                    sy.wait_ge(c2.h, AEV["b3"])
                    dm[8].inc(sy.dma_start(b3td.ap(), btT_sb[0:2, :]), 16)
                    sy.wait_ge(c2.h, AEV["b2"])
                    dm[8].inc(sy.dma_start(b2td.ap(), btT_sb[0:2, 0:D]), 16)
                    for i_ in (5, 6, 7, 8):
                        sy.wait_ge(dm[i_].h, dm[i_].n)
                    ph.inc(sy.nop())

                @blk.tensor
                def _(t):
                    wt(t, GEV["ab_a1"])
                    wt(t, GEV["consts"])
                    for k in range(32):
                        mm = t.transpose(ps_a[:, 2 * k:2 * k + 2],
                                         ab_sb[0:2, 128 * k:128 * (k + 1)],
                                         id_sb[0:2, 0:2])
                    c1.inc(mm)
                    wt(t, GEV["bb_b1"])
                    for k in range(86):
                        if k >= 2:
                            t.wait_ge(c2.h, bev1[k - 1])
                        c1.inc(t.transpose(
                            ps_b[0:2, 128 * (k % 2):128 * (k % 2) + 128],
                            bb_sb[:, 2 * k:2 * k + 2], id_sb[:, :]))
                    wt(t, GEV["ab_a3"])
                    t.wait_ge(c2.h, AEV["a1"])
                    for k in range(32):
                        mm = t.transpose(ps_a[:, 2 * k:2 * k + 2],
                                         ab_sb[0:2, 128 * k:128 * (k + 1)],
                                         id_sb[0:2, 0:2])
                    c1.inc(mm)
                    wt(t, GEV["bb_b3"])
                    wt(t, SEV["b1td"])
                    for k in range(86):
                        if k >= 2:
                            t.wait_ge(c2.h, bev3[k - 1])
                        c1.inc(t.transpose(
                            ps_b[0:2, 128 * (k % 2):128 * (k % 2) + 128],
                            bb_sb[:, 2 * k:2 * k + 2], id_sb[:, :]))
                    wt(t, GEV["ab_a2"])
                    t.wait_ge(c2.h, AEV["a3"])
                    for k in range(86):
                        mm = t.transpose(ps_a[:, 2 * k:2 * k + 2],
                                         ab_sb[0:2, 128 * k:128 * (k + 1)],
                                         id_sb[0:2, 0:2])
                    c1.inc(mm)
                    wt(t, GEV["bb_b2"])
                    wt(t, SEV["b3td"])
                    for k in range(32):
                        if k >= 2:
                            t.wait_ge(c2.h, bev2[k - 1])
                        c1.inc(t.transpose(
                            ps_b[0:2, 128 * (k % 2):128 * (k % 2) + 128],
                            bb_sb[:, 2 * k:2 * k + 2], id_sb[:, :]))
                    endsec(t, c1)

                @blk.scalar
                def _(sc):
                    sc.wait_ge(c1.h, TEV["a1"])
                    c2.inc(sc.copy(a1T_sb[:, :], ps_a[:, 0:64]))
                    for k in range(86):
                        sc.wait_ge(c1.h, TEV["b1g"][k])
                        c2.inc(sc.copy(
                            btT_sb[0:2, 128 * k:128 * (k + 1)],
                            ps_b[0:2, 128 * (k % 2):128 * (k % 2) + 128]))
                    sc.wait_ge(c1.h, TEV["a3"])
                    c2.inc(sc.copy(a3T_sb[:, :], ps_a[:, 0:64]))
                    for k in range(86):
                        sc.wait_ge(c1.h, TEV["b3g"][k])
                        c2.inc(sc.copy(
                            btT_sb[0:2, 128 * k:128 * (k + 1)],
                            ps_b[0:2, 128 * (k % 2):128 * (k % 2) + 128]))
                    sc.wait_ge(c1.h, TEV["a2"])
                    c2.inc(sc.copy(a2T_sb[:, :], ps_a[:, 0:172]))
                    for k in range(32):
                        sc.wait_ge(c1.h, TEV["b2g"][k])
                        c2.inc(sc.copy(
                            btT_sb[0:2, 128 * k:128 * (k + 1)],
                            ps_b[0:2, 128 * (k % 2):128 * (k % 2) + 128]))
                    endsec(sc, c2)

                @blk.vector
                def _(v):
                    c6.inc(v.memset(eps_sb[:, :], EPS))
                    c6.inc(v.memset(onesf_sb[:, :], 1.0))
                    endsec(v, c6)

            assert c2.n == AEV["b2"]
            assert c1.n == TEV["b2g"][31]
            assert ph.n == ph_in + 5

        if _KPH <= 0:
            _fin()
            return nc

        with ExitStack() as P:
            csa_sb = P.enter_context(nc.sbuf_tensor("csa_sb", [128, T], BF16))
            csb_sb = P.enter_context(nc.sbuf_tensor("csb_sb", [128, T], BF16))
            swp_sb = P.enter_context(nc.sbuf_tensor("swp_sb", [128, 128], BF16))
            qe0 = P.enter_context(nc.sbuf_tensor("qe0", [128, 512], BF16))
            qe1 = P.enter_context(nc.sbuf_tensor("qe1", [128, 512], BF16))
            xT_sb = P.enter_context(nc.sbuf_tensor("xT_sb", [128, 32 * TH], BF16))
            wkq0 = P.enter_context(nc.sbuf_tensor("wkq0", [128, 32 * 128], BF16))
            wkq1 = P.enter_context(nc.sbuf_tensor("wkq1", [128, 32 * 128], BF16))
            wvb0 = P.enter_context(nc.sbuf_tensor("wvb0", [128, 32 * 256], BF16))
            wvb1 = P.enter_context(nc.sbuf_tensor("wvb1", [128, 32 * 256], BF16))
            rp0 = P.enter_context(nc.sbuf_tensor("rp0", [128, 1024], F32))
            sq0 = P.enter_context(nc.sbuf_tensor("sq0", [128, 512], BF16))
            sq1 = P.enter_context(nc.sbuf_tensor("sq1", [128, 512], BF16))
            ko0 = P.enter_context(nc.sbuf_tensor("ko0", [128, 512], BF16))
            ko1 = P.enter_context(nc.sbuf_tensor("ko1", [128, 512], BF16))
            vo0 = P.enter_context(nc.sbuf_tensor("vo0", [128, 256], BF16))
            vo1 = P.enter_context(nc.sbuf_tensor("vo1", [128, 256], BF16))
            kmc_sb = P.enter_context(nc.sbuf_tensor("kmc_sb", [128, 128], F32))
            qeb = [qe0, qe1]
            psw0 = P.enter_context(nc.psum_tensor("psw0", [128, 512], F32))
            psw1 = P.enter_context(nc.psum_tensor("psw1", [128, 512], F32))
            psk0 = P.enter_context(nc.psum_tensor("psk0", [128, 512], F32))
            psk1 = P.enter_context(nc.psum_tensor("psk1", [128, 512], F32))
            pst0 = P.enter_context(nc.psum_tensor("pst0", [128, 512], F32))
            pst1 = P.enter_context(nc.psum_tensor("pst1", [128, 512], F32))
            wkq = [wkq0, wkq1]
            wvb = [wvb0, wvb1]
            sqb = [sq0, sq1]
            kob = [ko0, ko1]
            vob = [vo0, vo1]
            psk = [psk0, psk1]
            pst = [pst0, pst1]
            psw = [psw0, psw1]

            ph_in = ph.n

            jobs = []
            for H in (0, 1):
                for g in range(NGC):
                    for s4 in range(4):
                        jobs.append(("K", H, g, s4))
                for vc in range(2):
                    for tt in range(16):
                        jobs.append(("V", H, vc, tt))
                for h in range(NHC):
                    for s4 in range(4):
                        jobs.append(("Q", H, h, s4))
            jidx = {j: i for i, j in enumerate(jobs)}
            rjobs = [j for j in jobs if j[0] in ("K", "Q")]
            ridx = {j: i for i, j in enumerate(rjobs)}
            vjobs = [j for j in jobs if j[0] == "V"]
            vidx = {j: i for i, j in enumerate(vjobs)}

            kqloads = []
            vloads = []
            for H in (0, 1):
                for g in range(NGC):
                    kqloads.append(("K", H, g))
                for vc in range(2):
                    vloads.append(("V", H, vc))
                for h in range(NHC):
                    kqloads.append(("Q", H, h))
            kqli = {l: i for i, l in enumerate(kqloads)}
            vli = {l: i for i, l in enumerate(vloads)}

            def load_of(j):
                return (j[0], j[1], j[2])

            def last_job_of_load(l):
                kind, H, a = l
                return (kind, H, a, 3 if kind != "V" else 15)

            ldorder = []
            for H in (0, 1):
                for g in range(NGC):
                    ldorder.append(("K", H, g))
                for vc in range(2):
                    ldorder.append(("V", H, vc))
                for h in range(NHC):
                    ldorder.append(("Q", H, h))
            WEV = {}
            nkq = [dm[9].n, dm[10].n]
            nv = [dm[11].n, dm[12].n]
            for l in ldorder:
                if l[0] == "V":
                    i = vli[l]
                    nv[i % 2] += 16
                    WEV[l] = (dm[11 + i % 2], nv[i % 2])
                else:
                    i = kqli[l]
                    nkq[i % 2] += 16
                    WEV[l] = (dm[9 + i % 2], nkq[i % 2])
            CSL = (dm[5], dm[5].n + 48)
            XEV = {}
            OEV = {}
            nxt = dm[13].n
            nko = [dm[14].n, dm[15].n]
            nvo = [dm[16].n, dm[17].n]
            for H in (0, 1):
                nxt += 32 * 16
                XEV[H] = (dm[13], nxt)
                for j in [jj for jj in jobs if jj[1] == H]:
                    if j[0] == "V":
                        iv = vidx[j]
                        nvo[iv % 2] += 16
                        OEV[j] = (dm[16 + iv % 2], nvo[iv % 2])
                    else:
                        ir = ridx[j]
                        nko[ir % 2] += 16
                        OEV[j] = (dm[14 + ir % 2], nko[ir % 2])
            MEV = {}
            SEV1 = {}
            SWP1 = {}
            n = c1.n
            pend = None
            for j in jobs:
                if pend is not None:
                    n += 1
                    SWP1[pend] = n
                    n += 1
                    SEV1[pend] = n
                    pend = None
                n += 1
                MEV[j] = n
                if j[0] in ("K", "Q"):
                    pend = j
            if pend is not None:
                n += 1
                SWP1[pend] = n
                n += 1
                SEV1[pend] = n
            QEV = {}
            QE2 = {}
            n = c2.n
            for j in jobs:
                n += 1
                QEV[j] = n
                if j[0] in ("K", "Q"):
                    n += 1
                    QE2[j] = n
            REV = {}
            n = c3.n
            for j in rjobs:
                n += 1
                REV[j] = n
            PEV = {}
            n = c6.n
            for j in rjobs:
                n += 1
                PEV[j] = n

            def free_ev(j):
                if j[0] == "V":
                    return (c2, QEV[j])
                return (c6, PEV[j])

            def job_pos(j):
                kind, H, a, b = j
                if kind == "V":
                    return None
                return (H * TH + b * 512)

            with nc.Block() as blk:
                @blk.gpsimd
                def _(g):
                    g.wait_ge(ph.h, ph_in)
                    for l in ldorder:
                        kind, H, a = l
                        if kind == "V":
                            i = vli[l]
                            if i >= 2:
                                pl = vloads[i - 2]
                                g.wait_ge(c1.h, MEV[last_job_of_load(pl)])
                            srcap = wv.ap()[:, a * 256:(a + 1) * 256]
                            WEV[l][0].inc(g.dma_start(
                                sb3(wvb[i % 2][:, :], 32), dr3(srcap)), 16)
                        else:
                            i = kqli[l]
                            if i >= 2:
                                pl = kqloads[i - 2]
                                g.wait_ge(c1.h, MEV[last_job_of_load(pl)])
                            wsrc = wk if kind == "K" else wq
                            srcap = wsrc.ap()[:, a * 128:(a + 1) * 128]
                            WEV[l][0].inc(g.dma_start(
                                sb3(wkq[i % 2][:, :], 32), dr3(srcap)), 16)
                    for i_ in (9, 10, 11, 12):
                        g.wait_ge(dm[i_].h, dm[i_].n)
                    ph.inc(g.nop())

                @blk.sync
                def _(sy):
                    sy.wait_ge(ph.h, ph_in)
                    dm[5].inc(sy.dma_start(csa_sb[:, :], csa.ap()), 16)
                    dm[5].inc(sy.dma_start(csb_sb[:, :], csb.ap()), 16)
                    dm[5].inc(sy.dma_start(swp_sb[:, :], swp.ap()), 16)
                    for H in (0, 1):
                        if H == 1:
                            sy.wait_ge(c1.h, MEV[("Q", 0, NHC - 1, 3)])
                        wt(sy, GEV["xbf1" if H else "xbf0"])
                        for dc in range(32):
                            dm[13].inc(sy.dma_start(
                                xT_sb[:, dc * TH:(dc + 1) * TH],
                                xbf.ap()[H * TH:(H + 1) * TH,
                                         dc * 128:(dc + 1) * 128],
                                transpose=True), 16)
                        for j in [jj for jj in jobs if jj[1] == H]:
                            kind, _, a, b = j
                            if kind == "V":
                                iv = vidx[j]
                                sy.wait_ge(c2.h, QEV[j])
                                gt = H * TH + b * 128
                                OEV[j][0].inc(sy.dma_start(
                                    vS.ap()[gt:gt + 128,
                                            a * 256:(a + 1) * 256],
                                    vob[iv % 2][:, :]), 16)
                            else:
                                ir = ridx[j]
                                sy.wait_ge(c6.h, PEV[j])
                                ts = H * TH + b * 512
                                dst = kT if kind == "K" else qT
                                OEV[j][0].inc(sy.dma_start(
                                    dst.ap()[a * 128:(a + 1) * 128,
                                             ts:ts + 512],
                                    kob[ir % 2][:, :]), 16)
                    for i_ in (5, 13, 14, 15, 16, 17):
                        sy.wait_ge(dm[i_].h, dm[i_].n)
                    ph.inc(sy.nop())

                @blk.tensor
                def _(t):
                    t.wait_ge(ph.h, ph_in)
                    pend = None

                    def flush_stats(t):
                        nonlocal pend
                        if pend is None:
                            return
                        jp = pend
                        ip = ridx[jp]
                        wt(t, CSL)
                        t.wait_ge(c2.h, QE2[jp])
                        if ip >= 2:
                            t.wait_ge(c6.h, PEV[rjobs[ip - 2]])
                        c1.inc(t.matmul(psw[ip % 2][:, :], swp_sb[:, :],
                                        qeb[ip % 2][:, :],
                                        start=True, stop=True))
                        if ip >= 2:
                            t.wait_ge(c3.h, REV[rjobs[ip - 2]])
                        c1.inc(t.matmul(pst[ip % 2][:, :], ones_sb[:, :],
                                        sqb[ip % 2][:, :],
                                        start=True, stop=True))
                        pend = None

                    for j in jobs:
                        kind, H, a, b = j
                        i = jidx[j]
                        flush_stats(t)
                        if (kind, b) in (("K", 0), ("V", 0), ("Q", 0)) and \
                                a == 0:
                            wt(t, XEV[H])
                        if b == 0:
                            wt(t, WEV[(kind, H, a)])
                        if i >= 2:
                            sem, val = free_ev(jobs[i - 2])
                            t.wait_ge(sem.h, val)
                        pg = psk[i % 2]
                        if kind == "V":
                            ld = vli[(kind, H, a)]
                            wb = wvb[ld % 2]
                            for dc in range(32):
                                mm = t.matmul(
                                    pg[:, 0:256],
                                    xT_sb[:, dc * TH + b * 128:
                                          dc * TH + b * 128 + 128],
                                    wb[:, dc * 256:(dc + 1) * 256],
                                    start=(dc == 0), stop=(dc == 31))
                            c1.inc(mm)
                        else:
                            ld = kqli[(kind, H, a)]
                            wb = wkq[ld % 2]
                            for dc in range(32):
                                mm = t.matmul(
                                    pg[:, :],
                                    wb[:, dc * 128:(dc + 1) * 128],
                                    xT_sb[:, dc * TH + b * 512:
                                          dc * TH + b * 512 + 512],
                                    start=(dc == 0), stop=(dc == 31))
                            c1.inc(mm)
                        if kind in ("K", "Q"):
                            pend = j
                    flush_stats(t)
                    endsec(t, c1)

                @blk.scalar
                def _(sc):
                    sc.wait_ge(ph.h, ph_in)
                    for j in jobs:
                        kind, H, a, b = j
                        i = jidx[j]
                        sc.wait_ge(c1.h, MEV[j])
                        pg = psk[i % 2]
                        if kind == "V":
                            iv = vidx[j]
                            if iv >= 2:
                                wt(sc, OEV[vjobs[iv - 2]])
                            c2.inc(sc.copy(vob[iv % 2][:, :], pg[:, 0:256]))
                        else:
                            ir = ridx[j]
                            if ir >= 2:
                                sc.wait_ge(c1.h, SEV1[rjobs[ir - 2]])
                            c2.inc(sc.activation(sqb[ir % 2][:, :], pg[:, :],
                                                 AF.Square))
                            if ir >= 2:
                                sc.wait_ge(c1.h, SWP1[rjobs[ir - 2]])
                            c2.inc(sc.copy(qeb[ir % 2][:, :], pg[:, :]))
                    endsec(sc, c2)

                @blk.vector
                def _(v):
                    v.wait_ge(ph.h, ph_in)
                    for j in rjobs:
                        kind, H, a, b = j
                        ir = ridx[j]
                        i = jidx[j]
                        pg = psk[i % 2]
                        v.wait_ge(c1.h, SEV1[j])
                        if kind == "Q":
                            col = a * 8 + H * 4 + b
                            c3.inc(v.reduce_max(qmax2_sb[:, col:col + 1],
                                                pst[ir % 2][:, :], axis=AX.X))
                        else:
                            strip = H * 4 + b
                            if strip == 0:
                                c3.inc(v.reduce_max(
                                    kmax2_sb[:, a:a + 1],
                                    pst[ir % 2][:, :], axis=AX.X))
                            else:
                                vch.inc(v.reduce_max(
                                    stat_sb[:, ir % 2:ir % 2 + 1],
                                    pst[ir % 2][:, :], axis=AX.X))
                                v.wait_ge(vch.h, vch.n)
                                c3.inc(v.tensor_max(
                                    kmax2_sb[:, a:a + 1],
                                    kmax2_sb[:, a:a + 1],
                                    stat_sb[:, ir % 2:ir % 2 + 1]))
                        wt(v, CSL)
                        v.wait_ge(c2.h, QEV[j])
                        if ir >= 2:
                            wt(v, OEV[rjobs[ir - 2]])
                        ts = H * TH + b * 512
                        r = rp0
                        ko = kob[ir % 2]
                        v.wait_ge(vch.h, vch.n)
                        v.wait_ge(c6.h, c6.n)
                        v.tensor_mul(r[:, 0:512], pg[:, :],
                                     csa_sb[:, ts:ts + 512])
                        vch.inc(v.tensor_mul(r[:, 512:1024], psw[ir % 2][:, :],
                                             csb_sb[:, ts:ts + 512]))
                        v.wait_ge(vch.h, vch.n)
                        c6.inc(v.tensor_add(ko[:, :], r[:, 0:512],
                                            r[:, 512:1024]))
                    v.wait_ge(vch.h, vch.n)
                    for gk in range(NGC):
                        mmv = v.tensor_scalar_mul(
                            kmc_sb[:, gk * 32:(gk + 1) * 32],
                            onesf_sb[:, 0:32], kmax2_sb[:, gk:gk + 1])
                    vch.inc(mmv)
                    v.wait_ge(vch.h, vch.n)
                    c6.inc(v.tensor_mul(kmc_sb[:, :], kmc_sb[:, :],
                                        qmax2_sb[:, :]))
                    endsec(v, c6)

            assert c2.n == QE2[jobs[-1]]
            assert ph.n == ph_in + 5

            ph_in2 = ph.n
            with nc.Block() as blk:
                @blk.scalar
                def _(sc):
                    sc.wait_ge(ph.h, ph_in2)
                    c2.inc(sc.activation(mneg_sb[:, :], kmc_sb[:, :],
                                         AF.Sqrt))
                    endsec(sc, c2)

                @blk.vector
                def _(v):
                    v.wait_ge(c2.h, c2.n)
                    c6.inc(v.tensor_scalar_mul(mneg_sb[:, :], mneg_sb[:, :],
                                               -SCALE))
                    endsec(v, c6)
            ph_p1 = ph.n

        if _KPH <= 1:
            _fin()
            return nc

        with ExitStack() as P:
            maskp_sb = P.enter_context(nc.sbuf_tensor("maskp_sb", [128, 2048], BF16))
            qs0 = P.enter_context(nc.sbuf_tensor("qs0", [128, NHC * 512], BF16))
            qs1 = P.enter_context(nc.sbuf_tensor("qs1", [128, NHC * 512], BF16))
            ks0 = P.enter_context(nc.sbuf_tensor("ks0", [128, T], BF16))
            ks1 = P.enter_context(nc.sbuf_tensor("ks1", [128, T], BF16))
            vs0 = P.enter_context(nc.sbuf_tensor("vs0", [128, T], BF16))
            vs1 = P.enter_context(nc.sbuf_tensor("vs1", [128, T], BF16))
            pb0 = P.enter_context(nc.sbuf_tensor("pb0", [128, 1024], BF16))
            pb1 = P.enter_context(nc.sbuf_tensor("pb1", [128, 1024], BF16))
            rln_sb = P.enter_context(nc.sbuf_tensor("rln_sb", [128, 512], F32))
            ri0 = P.enter_context(nc.sbuf_tensor("ri0", [128, 512], F32))
            ri1 = P.enter_context(nc.sbuf_tensor("ri1", [128, 512], F32))
            cx0 = P.enter_context(nc.sbuf_tensor("cx0", [128, 512], BF16))
            cx1 = P.enter_context(nc.sbuf_tensor("cx1", [128, 512], BF16))
            psS0 = P.enter_context(nc.psum_tensor("psS0", [128, 1024], F32))
            psS1 = P.enter_context(nc.psum_tensor("psS1", [128, 1024], F32))
            psc0 = P.enter_context(nc.psum_tensor("psc0", [128, 512], F32))
            psc1 = P.enter_context(nc.psum_tensor("psc1", [128, 512], F32))
            psr0 = P.enter_context(nc.psum_tensor("psr0", [128, 512], F32))
            psr1 = P.enter_context(nc.psum_tensor("psr1", [128, 512], F32))
            qsb = [qs0, qs1]
            ksb = [ks0, ks1]
            vsb = [vs0, vs1]
            pbb = [pb0, pb1]
            rib = [ri0, ri1]
            cxb = [cx0, cx1]
            psS = [psS0, psS1]
            psc = [psc0, psc1]
            psr = [psr0, psr1]

            ph_in = ph.n

            groups = [(st, g) for st in range(8) for g in range(NGC)][:_KP3G]
            giof = {sg: i for i, sg in enumerate(groups)}
            heads = []
            for (st, g) in groups:
                for hh in range(4):
                    heads.append((st, g, g * 4 + hh))
            pairs = []
            hinfo = []
            for m, (st, g, h) in enumerate(heads):
                npz = 2 * (st + 1)
                p0i = len(pairs)
                for pi in range(npz):
                    pairs.append(dict(
                        m=m, st=st, g=g, h=h, pi=pi, c0=2 * pi,
                        masked=(pi >= npz - 2), j2=pi - (npz - 2),
                        first=(pi == 0), last=(pi == npz - 1)))
                hinfo.append(dict(st=st, g=g, h=h, pfirst=p0i,
                                  plast=len(pairs) - 1))
            NP = len(pairs)

            MKL = (dm[5], dm[5].n + 16)
            QL = {}
            KL = {}
            VL = {}
            CST = {}
            nq = [dm[18].n, dm[19].n]
            nk = [dm[6].n, dm[7].n]
            nvv = [dm[16].n, dm[17].n]
            nct = [dm[14].n, dm[15].n]

            def _ql(st):
                nq[st % 2] += 16
                QL[st] = (dm[18 + st % 2], nq[st % 2])

            def _kvl(gi):
                nk[gi % 2] += 16
                KL[gi] = (dm[6 + gi % 2], nk[gi % 2])
                nvv[gi % 2] += 16
                VL[gi] = (dm[16 + gi % 2], nvv[gi % 2])

            _ql(0)
            _kvl(0)
            for gi, (st, g) in enumerate(groups):
                if gi + 1 < len(groups):
                    _kvl(gi + 1)
                if g == NGC - 1 and st + 1 < 8:
                    _ql(st + 1)
                for hh in range(4):
                    m = gi * 4 + hh
                    nct[m % 2] += 16
                    CST[m] = (dm[14 + m % 2], nct[m % 2])
            s1b, s4b = c1.n, c4.n
            SC = {p: s1b + p + 1 for p in range(NP)}
            CT = {p: s4b + p + 1 for p in range(NP)}
            s2b, s5b = c2.n, c5.n
            EX = {p: s2b + p + 1 for p in range(NP)}
            RI = {m: s5b + m + 1 for m in range(len(heads))}
            s3b, s6b = c3.n, c6.n
            MK = {}
            nmk = s3b
            for p, pr in enumerate(pairs):
                if pr["masked"] and not _KNOMASK:
                    nmk += 1
                    MK[p] = nmk
            EVC = {m: s6b + m + 1 for m in range(len(heads))}

            def span_of(st):
                return (st + 1) * 512

            with nc.Block() as blk:
                @blk.sync
                def _(sy):
                    sy.wait_ge(ph.h, ph_in)
                    dm[5].inc(sy.dma_start(maskp_sb[:, :], maskp.ap()), 16)
                    QL[0][0].inc(sy.dma_start(
                        sb3(qsb[0][:, :], NHC),
                        dr3(qT.ap()[:, 0:512])), 16)
                    KL[0][0].inc(sy.dma_start(ksb[0][:, 0:512],
                                              kT.ap()[0:128, 0:512]), 16)
                    VL[0][0].inc(sy.dma_start(
                        sb3(vsb[0][:, 0:512], 4),
                        vS.ap()[0:512, 0:128].rearrange(
                            "(c p) w -> p c w", p=128)), 16)
                    for gi, (st, g) in enumerate(groups):
                        if gi + 1 < len(groups):
                            st2, g2 = groups[gi + 1]
                            sp2 = span_of(st2)
                            if gi >= 1:
                                sy.wait_ge(c4.h, CT[hinfo[gi * 4 - 1]["plast"]])
                            KL[gi + 1][0].inc(sy.dma_start(
                                ksb[(gi + 1) % 2][:, 0:sp2],
                                kT.ap()[g2 * 128:(g2 + 1) * 128, 0:sp2]), 16)
                            VL[gi + 1][0].inc(sy.dma_start(
                                sb3(vsb[(gi + 1) % 2][:, 0:sp2], sp2 // 128),
                                vS.ap()[0:sp2, g2 * 128:(g2 + 1) * 128]
                                .rearrange("(c p) w -> p c w", p=128)), 16)
                        if g == NGC - 1 and st + 1 < 8:
                            if st >= 1:
                                last_m_prev_st = (giof[(st - 1, NGC - 1)]) * 4 + 3
                                sy.wait_ge(c4.h, CT[hinfo[last_m_prev_st]["plast"]])
                            QL[st + 1][0].inc(sy.dma_start(
                                sb3(qsb[(st + 1) % 2][:, :], NHC),
                                dr3(qT.ap()[:, (st + 1) * 512:(st + 2) * 512])),
                                16)
                        for hh in range(4):
                            m = gi * 4 + hh
                            st_, _, h_ = heads[m]
                            sy.wait_ge(c6.h, EVC[m])
                            dst = ctx_lo if st_ < 4 else ctx_hi
                            cofs = (st_ % 4) * 512
                            CST[m][0].inc(sy.dma_start(
                                dst.ap()[h_ * 128:(h_ + 1) * 128,
                                         cofs:cofs + 512],
                                cxb[m % 2][:, :]), 16)
                    for i_ in (5, 6, 7, 14, 15, 16, 17, 18, 19):
                        sy.wait_ge(dm[i_].h, dm[i_].n)
                    ph.inc(sy.nop())

                @blk.tensor
                def _(t):
                    t.wait_ge(ph.h, ph_in)

                    def consume(q):
                        pq = pairs[q]
                        mq = pq["m"]
                        gi = giof[(pq["st"], pq["g"])]
                        if pq["masked"] and not _KNOMASK:
                            t.wait_ge(c3.h, MK[q])
                        else:
                            t.wait_ge(c2.h, EX[q])
                        for half in (0, 1):
                            c = pq["c0"] + half
                            stt = pq["first"] and half == 0
                            spp = pq["last"] and half == 1
                            mm = t.matmul(
                                psc[mq % 2][:, :],
                                vsb[gi % 2][:, c * 128:(c + 1) * 128],
                                pbb[q % 2][:, half * 512:(half + 1) * 512],
                                start=stt, stop=spp)
                            if not _KNOROW:
                                mm = t.matmul(
                                    psr[mq % 2][:, :], ones_sb[:, :],
                                    pbb[q % 2][:, half * 512:(half + 1) * 512],
                                    start=stt, stop=spp)
                        c4.inc(mm)

                    for p, pr in enumerate(pairs):
                        m = pr["m"]
                        gi = giof[(pr["st"], pr["g"])]
                        if pr["first"]:
                            if pr["h"] == pr["g"] * 4:
                                wt(t, KL[gi])
                                wt(t, VL[gi])
                                if pr["g"] == 0:
                                    wt(t, QL[pr["st"]])
                            if m >= 2:
                                t.wait_ge(c6.h, EVC[m - 2])
                                t.wait_ge(c5.h, RI[m - 2])
                        for half in (0, 1):
                            c = pr["c0"] + half
                            mm = t.matmul(
                                psS[p % 2][:, half * 512:(half + 1) * 512],
                                ksb[gi % 2][:, c * 128:(c + 1) * 128],
                                qsb[pr["st"] % 2][:, pr["h"] * 512:
                                                  (pr["h"] + 1) * 512],
                                start=True, stop=True)
                        c1.inc(mm)
                        if p >= 1:
                            consume(p - 1)
                    consume(NP - 1)
                    endsec(t, c4)

                @blk.scalar
                def _(sc):
                    sc.wait_ge(ph.h, ph_in)
                    for p, pr in enumerate(pairs):
                        m = pr["m"]
                        col = pr["h"] * 8 + pr["st"]
                        sc.wait_ge(c1.h, SC[p])
                        if p >= 2:
                            sc.wait_ge(c4.h, CT[p - 2])
                        if _KNOEXP2:
                            sc.activation(pbb[p % 2][:, 0:512],
                                          psS[p % 2][:, 0:512], AF.Exp,
                                          bias=mneg_sb[:, col:col + 1],
                                          scale=SCALE)
                            c2.inc(sc.activation(pbb[p % 2][:, 512:1024],
                                                 psS[p % 2][:, 512:1024],
                                                 AF.Exp,
                                                 bias=mneg_sb[:, col:col + 1],
                                                 scale=SCALE))
                        else:
                            c2.inc(sc.activation(pbb[p % 2][:, :],
                                                 psS[p % 2][:, :], AF.Exp,
                                                 bias=mneg_sb[:, col:col + 1],
                                                 scale=SCALE))
                        if pr["last"]:
                            sc.wait_ge(c4.h, CT[p])
                            if m >= 2:
                                sc.wait_ge(c6.h, EVC[m - 2])
                            if m >= 2:
                                sc.wait_ge(c5.h, RI[m - 2])
                            if _KNOEPI:
                                c5.inc(sc.memzero(rib[m % 2][:, :]))
                            else:
                                sch.inc(sc.activation(rln_sb[:, :],
                                                      psr[m % 2][:, :], AF.Ln))
                                sc.wait_ge(sch.h, sch.n)
                                c5.inc(sc.activation(
                                    rib[m % 2][:, :], rln_sb[:, :],
                                    AF.Exp, scale=-1.0))
                    endsec(sc, c5)

                @blk.vector
                def _(v):
                    v.wait_ge(ph.h, ph_in)
                    wt(v, MKL)
                    for p, pr in enumerate(pairs):
                        if pr["masked"] and not _KNOMASK:
                            v.wait_ge(c2.h, EX[p])
                            j2 = pr["j2"]
                            c3.inc(v.tensor_mul(
                                pbb[p % 2][:, :], pbb[p % 2][:, :],
                                maskp_sb[:, j2 * 1024:(j2 + 1) * 1024]))
                        if pr["last"]:
                            m = pr["m"]
                            v.wait_ge(c5.h, RI[m])
                            if m >= 2:
                                wt(v, CST[m - 2])
                            c6.inc(v.tensor_mul(cxb[m % 2][:, :],
                                                psc[m % 2][:, :],
                                                rib[m % 2][:, :]))
                    endsec(v, c6)

            assert c1.n == SC[NP - 1]
            assert c4.n == CT[NP - 1]
            assert c2.n == EX[NP - 1]
            assert ph.n == ph_in + 4

        if _KPH <= 3:
            _fin()
            return nc

        ph_in = ph.n
        with nc.Block() as blk:
            @blk.gpsimd
            def _(g):
                g.wait_ge(ph.h, ph_in)
                cc1.inc(g.collective_compute(
                    "AllGather", ALU.bypass,
                    replica_groups=[[0, 1], [2, 3], [4, 5], [6, 7]],
                    ins=[ctx_lo.ap().opt()],
                    outs=[ctxAG.ap()[0:D, :].opt()]))
                cc2.inc(g.collective_compute(
                    "AllGather", ALU.bypass,
                    replica_groups=[[0, 1], [2, 3], [4, 5], [6, 7]],
                    ins=[ctx_hi.ap().opt()],
                    outs=[ctxAG.ap()[D:2 * D, :].opt()]))
                g.wait_ge(cc1.h, 1)
                g.wait_ge(cc2.h, 1)
                wt(g, GEV["consts"])
                with g.register("ro") as ro:
                    g.reg_load(ro, agoff_sb[0:1, 0:1])
                    dm[20].inc(g.dma_start(
                        ctxF.ap(),
                        bass.AP(ctxAG, ro, [[TH, D], [1, TH]])), 16)
                g.wait_ge(dm[20].h, dm[20].n)
                ph.inc(g.nop())
        ph_in = ph.n

        if _KPH <= 4:
            _fin()
            return nc

        with ExitStack() as P:
            cxs = P.enter_context(nc.sbuf_tensor("cxs", [128, 32 * 1024], BF16))
            woc0 = P.enter_context(nc.sbuf_tensor("woc0", [128, 32 * 256], BF16))
            woc1 = P.enter_context(nc.sbuf_tensor("woc1", [128, 32 * 256], BF16))
            xhb0 = P.enter_context(nc.sbuf_tensor("xhb0", [128, 8 * 256], F32))
            xhb1 = P.enter_context(nc.sbuf_tensor("xhb1", [128, 8 * 256], F32))
            stg0 = P.enter_context(nc.sbuf_tensor("stg0", [128, 8 * 256], F32))
            stg1 = P.enter_context(nc.sbuf_tensor("stg1", [128, 8 * 256], F32))
            trash_sb = P.enter_context(nc.sbuf_tensor("trash_sb", [128, 256], F32))
            pso0 = P.enter_context(nc.psum_tensor("pso0", [128, 256], F32))
            pso1 = P.enter_context(nc.psum_tensor("pso1", [128, 256], F32))
            pso2 = P.enter_context(nc.psum_tensor("pso2", [128, 256], F32))
            woc = [woc0, woc1]
            xhb = [xhb0, xhb1]
            stg = [stg0, stg1]
            pso = [pso0, pso1, pso2]

            ph_in = ph.n
            chunks = [(p_, oc) for p_ in (0, 1) for oc in range(16)]
            jobs4 = [(p_, oc, tt) for (p_, oc) in chunks for tt in range(8)]
            j4 = {jj: i for i, jj in enumerate(jobs4)}
            ci_of = {ch: i for i, ch in enumerate(chunks)}

            WOE = {}
            nwo = [dm[9].n, dm[10].n]
            for ch in chunks:
                ci = ci_of[ch]
                nwo[ci % 2] += 16
                WOE[ch] = (dm[9 + ci % 2], nwo[ci % 2])
            CXE = {}
            XHE = {}
            YPE = {}
            ncx = dm[13].n
            nxh = [dm[16].n, dm[17].n]
            nyp = [dm[14].n, dm[15].n]
            for p_ in (0, 1):
                ncx += 16
                CXE[p_] = (dm[13], ncx)
                for oc in range(16):
                    ci = ci_of[(p_, oc)]
                    nxh[ci % 2] += 16
                    XHE[(p_, oc)] = (dm[16 + ci % 2], nxh[ci % 2])
                    nyp[ci % 2] += 16
                    YPE[(p_, oc)] = (dm[14 + ci % 2], nyp[ci % 2])
            MM4 = {}
            n = c1.n
            for jj in jobs4:
                n += 1
                MM4[jj] = n
            EV4 = {}
            n = c6.n
            for jj in jobs4:
                n += 1
                EV4[jj] = n
            SQ4 = {}
            n = c2.n
            for jj in jobs4:
                n += 1
                SQ4[jj] = n

            with nc.Block() as blk:
                @blk.gpsimd
                def _(g):
                    g.wait_ge(ph.h, ph_in)
                    for ch in chunks:
                        p_, oc = ch
                        ci = ci_of[ch]
                        if ci >= 2:
                            g.wait_ge(c1.h, MM4[chunks[ci - 2] + (7,)])
                        WOE[ch][0].inc(g.dma_start(
                            sb3(woc[ci % 2][:, :], 32),
                            dr3(wo.ap()[:, oc * 256:(oc + 1) * 256])), 16)
                    for i_ in (9, 10):
                        g.wait_ge(dm[i_].h, dm[i_].n)
                    ph.inc(g.nop())

                @blk.sync
                def _(sy):
                    sy.wait_ge(ph.h, ph_in)
                    for p_ in (0, 1):
                        if p_ == 1:
                            sy.wait_ge(c1.h, MM4[(0, 15, 7)])
                        dm[13].inc(sy.dma_start(
                            sb3(cxs[:, :], 32),
                            dr3(ctxF.ap()[:, p_ * 1024:(p_ + 1) * 1024])), 16)
                        for oc in range(16):
                            ci = ci_of[(p_, oc)]
                            if ci >= 2:
                                sy.wait_ge(c6.h, EV4[chunks[ci - 2] + (7,)])
                            XHE[(p_, oc)][0].inc(sy.dma_start(
                                sb3(xhb[ci % 2][:, :], 8),
                                dr3(xh.ap()[p_ * 1024:(p_ + 1) * 1024,
                                            oc * 256:(oc + 1) * 256])), 16)
                            sy.wait_ge(c6.h, EV4[(p_, oc, 7)])
                            YPE[(p_, oc)][0].inc(sy.dma_start(
                                dr3(ypre.ap()[p_ * 1024:(p_ + 1) * 1024,
                                              oc * 256:(oc + 1) * 256]),
                                sb3(stg[ci % 2][:, :], 8)), 16)
                    for i_ in (13, 14, 15, 16, 17):
                        sy.wait_ge(dm[i_].h, dm[i_].n)
                    ph.inc(sy.nop())

                @blk.tensor
                def _(t):
                    t.wait_ge(ph.h, ph_in)
                    for jj in jobs4:
                        p_, oc, tt = jj
                        ci = ci_of[(p_, oc)]
                        i = j4[jj]
                        if tt == 0:
                            wt(t, CXE[p_])
                            wt(t, WOE[(p_, oc)])
                        if i >= 3:
                            t.wait_ge(c6.h, EV4[jobs4[i - 3]])
                        for dc in range(32):
                            mm = t.matmul(
                                pso[i % 3][:, :],
                                cxs[:, dc * 1024 + tt * 128:
                                    dc * 1024 + tt * 128 + 128],
                                woc[ci % 2][:, dc * 256:(dc + 1) * 256],
                                start=(dc == 0), stop=(dc == 31))
                        c1.inc(mm)
                    endsec(t, c1)

                @blk.vector
                def _(v):
                    v.wait_ge(ph.h, ph_in)
                    for jj in jobs4:
                        p_, oc, tt = jj
                        ci = ci_of[(p_, oc)]
                        i = j4[jj]
                        v.wait_ge(c1.h, MM4[jj])
                        if tt == 0:
                            wt(v, XHE[(p_, oc)])
                            if ci >= 2:
                                wt(v, YPE[chunks[ci - 2]])
                                v.wait_ge(c2.h, SQ4[chunks[ci - 2] + (7,)])
                        c6.inc(v.scalar_tensor_tensor(
                            stg[ci % 2][:, tt * 256:(tt + 1) * 256],
                            pso[i % 3][:, :], 1.0,
                            xhb[ci % 2][:, tt * 256:(tt + 1) * 256],
                            ALU.mult, ALU.add,
                            accum_out=stat_sb[:, 2 + i % 2:3 + i % 2]))
                        col = p_ * 8 + tt
                        v.wait_ge(c6.h, EV4[jj])
                        v.wait_ge(vch.h, vch.n)
                        if oc == 0:
                            vch.inc(v.tensor_copy(
                                ysum_sb[:, col:col + 1],
                                stat_sb[:, 2 + i % 2:3 + i % 2]))
                        else:
                            vch.inc(v.tensor_add(
                                ysum_sb[:, col:col + 1],
                                ysum_sb[:, col:col + 1],
                                stat_sb[:, 2 + i % 2:3 + i % 2]))
                        if i >= 1:
                            pj = jobs4[i - 1]
                            pcol = pj[0] * 8 + pj[2]
                            v.wait_ge(c2.h, SQ4[pj])
                            v.wait_ge(vch.h, vch.n)
                            if pj[1] == 0:
                                vch.inc(v.tensor_copy(
                                    ysq_sb[:, pcol:pcol + 1],
                                    stat_sb[:, 4 + (i - 1) % 2:
                                            5 + (i - 1) % 2]))
                            else:
                                vch.inc(v.tensor_add(
                                    ysq_sb[:, pcol:pcol + 1],
                                    ysq_sb[:, pcol:pcol + 1],
                                    stat_sb[:, 4 + (i - 1) % 2:
                                            5 + (i - 1) % 2]))
                    pj = jobs4[-1]
                    i = len(jobs4)
                    pcol = pj[0] * 8 + pj[2]
                    v.wait_ge(c2.h, SQ4[pj])
                    v.wait_ge(vch.h, vch.n)
                    vch.inc(v.tensor_add(
                        ysq_sb[:, pcol:pcol + 1],
                        ysq_sb[:, pcol:pcol + 1],
                        stat_sb[:, 4 + (i - 1) % 2:5 + (i - 1) % 2]))
                    endsec(v, c6)

                @blk.scalar
                def _(sc):
                    sc.wait_ge(ph.h, ph_in)
                    for jj in jobs4:
                        p_, oc, tt = jj
                        ci = ci_of[(p_, oc)]
                        i = j4[jj]
                        sc.wait_ge(c6.h, EV4[jj])
                        if i >= 1:
                            sc.wait_ge(c2.h, SQ4[jobs4[i - 1]])
                        c2.inc(sc.activation(
                            trash_sb[:, :],
                            stg[ci % 2][:, tt * 256:(tt + 1) * 256],
                            AF.Square,
                            accum_out=stat_sb[:, 4 + i % 2:5 + i % 2]))
                    endsec(sc, c2)

            assert c1.n == MM4[jobs4[-1]]
            assert ph.n == ph_in + 5

        if _KPH <= 5:
            _fin()
            return nc

        with ExitStack() as P:
            yp0 = P.enter_context(nc.sbuf_tensor("yp0", [128, D], F32))
            yp1 = P.enter_context(nc.sbuf_tensor("yp1", [128, D], F32))
            yo0 = P.enter_context(nc.sbuf_tensor("yo0", [128, D], F32))
            yo1 = P.enter_context(nc.sbuf_tensor("yo1", [128, D], F32))
            ypb = [yp0, yp1]
            yob = [yo0, yo1]

            ph_in = ph.n
            LY = {}
            SY = {}
            GY = {}
            nly = [dm[6].n, dm[7].n]
            nsy = [dm[14].n, dm[15].n]
            ngy = [dm[20].n, dm[21].n]
            for tt in range(16):
                nly[tt % 2] += 16
                LY[tt] = (dm[6 + tt % 2], nly[tt % 2])
                nsy[tt % 2] += 16
                SY[tt] = (dm[14 + tt % 2], nsy[tt % 2])
                ngy[tt % 2] += 16
                GY[tt] = (dm[20 + tt % 2], ngy[tt % 2])
            VA = {tt: c3.n + tt + 1 for tt in range(16)}
            RS = {tt: c2.n + tt + 1 for tt in range(16)}
            NM = {tt: c6.n + tt + 1 for tt in range(16)}

            with nc.Block() as blk:
                @blk.sync
                def _(sy):
                    sy.wait_ge(ph.h, ph_in)
                    for tt in range(16):
                        if tt >= 2:
                            sy.wait_ge(c6.h, NM[tt - 2])
                        LY[tt][0].inc(sy.dma_start(
                            ypb[tt % 2][:, :],
                            ypre.ap()[tt * 128:(tt + 1) * 128, :]), 16)
                        sy.wait_ge(c6.h, NM[tt])
                        SY[tt][0].inc(sy.dma_start(
                            y32.ap()[tt * 128:(tt + 1) * 128, :],
                            yob[tt % 2][:, :]), 16)
                    for i_ in (6, 7, 14, 15):
                        sy.wait_ge(dm[i_].h, dm[i_].n)
                    ph.inc(sy.nop())

                @blk.gpsimd
                def _(g):
                    g.wait_ge(ph.h, ph_in)
                    for tt in range(16):
                        g.wait_ge(c6.h, NM[tt])
                        GY[tt][0].inc(g.dma_start(
                            ybf.ap()[tt * 128:(tt + 1) * 128, :],
                            yob[tt % 2][:, :]), 16)
                    for i_ in (20, 21):
                        g.wait_ge(dm[i_].h, dm[i_].n)
                    ph.inc(g.nop())

                @blk.vector
                def _(v):
                    v.wait_ge(ph.h, ph_in)
                    for tt in range(16):
                        p = tt % 2
                        if tt >= 2:
                            v.wait_ge(c2.h, RS[tt - 2])
                            v.wait_ge(c6.h, NM[tt - 2])
                        v.wait_ge(vch.h, vch.n)
                        vch.inc(v.tensor_scalar_mul(
                            stat_sb[:, p:p + 1],
                            ysum_sb[:, tt:tt + 1], 1.0 / D))
                        v.wait_ge(vch.h, vch.n)
                        vch.inc(v.tensor_mul(stat_sb[:, 6 + p:7 + p],
                                             stat_sb[:, p:p + 1],
                                             stat_sb[:, p:p + 1]))
                        v.wait_ge(vch.h, vch.n)
                        c3.inc(v.scalar_tensor_tensor(
                            stat_sb[:, 2 + p:3 + p],
                            ysq_sb[:, tt:tt + 1], 1.0 / D,
                            stat_sb[:, 6 + p:7 + p],
                            ALU.mult, ALU.subtract))
                        if tt >= 1:
                            pp = (tt - 1) % 2
                            v.wait_ge(c2.h, RS[tt - 1])
                            if tt - 1 >= 2:
                                wt(v, SY[tt - 3])
                                wt(v, GY[tt - 3])
                            wt(v, LY[tt - 1])
                            c6.inc(v.tensor_scalar(
                                yob[pp][:, :], ypb[pp][:, :],
                                stat_sb[:, pp:pp + 1],
                                stat_sb[:, 4 + pp:5 + pp],
                                ALU.subtract, ALU.mult))
                    pp = 15 % 2
                    v.wait_ge(c2.h, RS[15])
                    wt(v, SY[13])
                    wt(v, GY[13])
                    wt(v, LY[15])
                    c6.inc(v.tensor_scalar(
                        yob[pp][:, :], ypb[pp][:, :],
                        stat_sb[:, pp:pp + 1],
                        stat_sb[:, 4 + pp:5 + pp],
                        ALU.subtract, ALU.mult))
                    endsec(v, c6)

                @blk.scalar
                def _(sc):
                    sc.wait_ge(ph.h, ph_in)
                    for tt in range(16):
                        p = tt % 2
                        sc.wait_ge(c3.h, VA[tt])
                        sch.inc(sc.activation(stat_sb[:, 6 + p:7 + p],
                                              stat_sb[:, 2 + p:3 + p], AF.Ln,
                                              bias=EPS))
                        sc.wait_ge(sch.h, sch.n)
                        c2.inc(sc.activation(stat_sb[:, 4 + p:5 + p],
                                             stat_sb[:, 6 + p:7 + p],
                                             AF.Exp, scale=-0.5))
                    endsec(sc, c2)

            assert ph.n == ph_in + 4

        if _KPH <= 6:
            _fin()
            return nc

        with ExitStack() as P6:
            xs_sb = P6.enter_context(nc.sbuf_tensor("xs_sb", [128, 8 * TH], BF16))
            mid1f = P6.enter_context(nc.sbuf_tensor("mid1f", [2, TH], BF16))
            mid3f = P6.enter_context(nc.sbuf_tensor("mid3f", [2, TH], BF16))
            mid2f = P6.enter_context(nc.sbuf_tensor("mid2f", [2, TH], BF16))

            with ExitStack() as P:
                yT0 = P.enter_context(nc.sbuf_tensor("yT0", [128, 32 * 512], BF16))
                yT1 = P.enter_context(nc.sbuf_tensor("yT1", [128, 32 * 512], BF16))
                xtmp = P.enter_context(nc.sbuf_tensor("xtmp", [128, 512], F32))
                pm10 = P.enter_context(nc.psum_tensor("pm10", [2, 512], F32))
                pm11 = P.enter_context(nc.psum_tensor("pm11", [2, 512], F32))
                pm30 = P.enter_context(nc.psum_tensor("pm30", [2, 512], F32))
                pm31 = P.enter_context(nc.psum_tensor("pm31", [2, 512], F32))
                yTb = [yT0, yT1]
                pm1 = [pm10, pm11]
                pm3 = [pm30, pm31]

                ph_in = ph.n
                YT = {}
                nyt = [dm[6].n, dm[7].n]
                for s_ in range(4):
                    nyt[s_ % 2] += 32 * 16
                    YT[s_] = (dm[6 + s_ % 2], nyt[s_ % 2])
                MID = {}
                n = c1.n
                for s_ in range(4):
                    for l_ in (0, 1):
                        n += 1
                        MID[(s_, l_)] = n
                MIDE = {}
                n = c2.n
                for s_ in range(4):
                    for l_ in (0, 1):
                        n += 1
                        MIDE[(s_, l_)] = n
                XSD = {s_: c6.n + s_ + 1 for s_ in range(4)}

                with nc.Block() as blk:
                    @blk.sync
                    def _(sy):
                        sy.wait_ge(ph.h, ph_in)
                        for s_ in range(4):
                            if s_ >= 2:
                                sy.wait_ge(c6.h, XSD[s_ - 2])
                                sy.wait_ge(c1.h, MID[(s_ - 2, 1)])
                            for dc in range(32):
                                dm[6 + s_ % 2].inc(sy.dma_start(
                                    yTb[s_ % 2][:, dc * 512:(dc + 1) * 512],
                                    ybf.ap()[s_ * 512:(s_ + 1) * 512,
                                             dc * 128:(dc + 1) * 128],
                                    transpose=True), 16)
                        for i_ in (6, 7):
                            sy.wait_ge(dm[i_].h, dm[i_].n)
                        ph.inc(sy.nop())

                    @blk.tensor
                    def _(t):
                        t.wait_ge(ph.h, ph_in)
                        for s_ in range(4):
                            wt(t, YT[s_])
                            for l_, (aT, pm) in enumerate(
                                    ((a1T_sb, pm1), (a3T_sb, pm3))):
                                if s_ >= 2:
                                    t.wait_ge(c2.h, MIDE[(s_ - 2, l_)])
                                for dc in range(32):
                                    mm = t.matmul(
                                        pm[s_ % 2][:, :],
                                        aT[:, dc * 2:dc * 2 + 2],
                                        yTb[s_ % 2][:, dc * 512:(dc + 1) * 512],
                                        start=(dc == 0), stop=(dc == 31))
                                c1.inc(mm)
                        endsec(t, c1)

                    @blk.scalar
                    def _(sc):
                        sc.wait_ge(ph.h, ph_in)
                        for s_ in range(4):
                            for l_, (mf, pm) in enumerate(
                                    ((mid1f, pm1), (mid3f, pm3))):
                                sc.wait_ge(c1.h, MID[(s_, l_)])
                                c2.inc(sc.mul(
                                    mf[0:2, s_ * 512:(s_ + 1) * 512],
                                    pm[s_ % 2][:, :], 0.5))
                        endsec(sc, c2)

                    @blk.vector
                    def _(v):
                        v.wait_ge(ph.h, ph_in)
                        for s_ in range(4):
                            wt(v, YT[s_])
                            v.wait_ge(c6.h, c6.n)
                            for j in range(8):
                                yb = yTb[s_ % 2]
                                v.wait_ge(vch.h, vch.n)
                                vch.inc(v.tensor_add(
                                    xtmp[:, :],
                                    yb[:, j * 512:(j + 1) * 512],
                                    yb[:, (8 + j) * 512:(9 + j) * 512]))
                                v.wait_ge(vch.h, vch.n)
                                vch.inc(v.tensor_add(
                                    xtmp[:, :], xtmp[:, :],
                                    yb[:, (16 + j) * 512:(17 + j) * 512]))
                                v.wait_ge(vch.h, vch.n)
                                mm = v.tensor_add(
                                    xs_sb[:, j * TH + s_ * 512:
                                          j * TH + s_ * 512 + 512],
                                    xtmp[:, :],
                                    yb[:, (24 + j) * 512:(25 + j) * 512])
                                if j < 7:
                                    vch.inc(mm)
                            c6.inc(mm)
                        endsec(v, c6)

                assert ph.n == ph_in + 4

            if _KPH <= 7:
                _fin()
                return nc

            with ExitStack() as P:
                b1T_s = P.enter_context(nc.sbuf_tensor("b1T_s", [2, HF], BF16))
                b3T_s = P.enter_context(nc.sbuf_tensor("b3T_s", [2, HF], BF16))
                w1c0 = P.enter_context(nc.sbuf_tensor("w1c0", [128, 8 * 128], BF16))
                w1c1 = P.enter_context(nc.sbuf_tensor("w1c1", [128, 8 * 128], BF16))
                w3c0 = P.enter_context(nc.sbuf_tensor("w3c0", [128, 8 * 128], BF16))
                w3c1 = P.enter_context(nc.sbuf_tensor("w3c1", [128, 8 * 128], BF16))
                u0 = P.enter_context(nc.sbuf_tensor("u0", [128, TH], BF16))
                u1 = P.enter_context(nc.sbuf_tensor("u1", [128, TH], BF16))
                sil0 = P.enter_context(nc.sbuf_tensor("sil0", [128, 512], F32))
                sil1 = P.enter_context(nc.sbuf_tensor("sil1", [128, 512], F32))
                ph10 = P.enter_context(nc.psum_tensor("ph10", [128, 512], F32))
                ph11 = P.enter_context(nc.psum_tensor("ph11", [128, 512], F32))
                ph30 = P.enter_context(nc.psum_tensor("ph30", [128, 512], F32))
                ph31 = P.enter_context(nc.psum_tensor("ph31", [128, 512], F32))
                pq0 = P.enter_context(nc.psum_tensor("pq0", [2, 512], F32))
                pq1 = P.enter_context(nc.psum_tensor("pq1", [2, 512], F32))
                pq2 = P.enter_context(nc.psum_tensor("pq2", [2, 512], F32))
                pq3 = P.enter_context(nc.psum_tensor("pq3", [2, 512], F32))
                w1cb = [w1c0, w1c1]
                w3cb = [w3c0, w3c1]
                ub = [u0, u1]
                silb = [sil0, sil1]
                ph1 = [ph10, ph11]
                ph3 = [ph30, ph31]
                pqm = [pq0, pq1, pq2, pq3]

                ph_in = ph.n
                js_of = lambda hc, s_: hc * 4 + s_
                BL = (dm[13], dm[13].n + 32)
                W1L = {}
                W3L = {}
                UST = {}
                nw1 = [dm[6].n, dm[7].n]
                nw3 = [dm[18].n, dm[19].n]
                nus = [dm[14].n, dm[15].n]
                for hc in range(86):
                    nw1[hc % 2] += 16
                    W1L[hc] = (dm[6 + hc % 2], nw1[hc % 2])
                    nw3[hc % 2] += 16
                    W3L[hc] = (dm[18 + hc % 2], nw3[hc % 2])
                    nus[hc % 2] += 16
                    UST[hc] = (dm[14 + hc % 2], nus[hc % 2])
                H1 = {}
                H3 = {}
                MID2 = {}
                n = c1.n
                pend6 = None
                for hc in range(86):
                    for s_ in range(4):
                        n += 1
                        H1[(hc, s_)] = n
                        n += 1
                        H3[(hc, s_)] = n
                        if pend6 is not None:
                            n += 1
                            MID2[pend6] = n
                        pend6 = (hc, s_)
                n += 1
                MID2[pend6] = n
                SIL = {}
                n = c2.n
                for hc in range(86):
                    for s_ in range(4):
                        n += 1
                        SIL[(hc, s_)] = n
                M2E = {}
                for s_ in range(4):
                    n += 1
                    M2E[s_] = n
                UD = {}
                n = c3.n
                for hc in range(86):
                    for s_ in range(4):
                        n += 1
                        UD[(hc, s_)] = n

                with nc.Block() as blk:
                    @blk.sync
                    def _(sy):
                        sy.wait_ge(ph.h, ph_in)
                        dm[13].inc(sy.dma_start(b1T_s[0:2, :], b1td.ap()), 16)
                        dm[13].inc(sy.dma_start(b3T_s[0:2, :], b3td.ap()), 16)
                        for hc in range(86):
                            if hc >= 2:
                                sy.wait_ge(c1.h, H3[(hc - 2, 3)])
                            W1L[hc][0].inc(sy.dma_start(
                                sb3(w1cb[hc % 2][:, :], 8),
                                dr3(w1t.ap()[:, hc * 128:(hc + 1) * 128])), 16)
                            W3L[hc][0].inc(sy.dma_start(
                                sb3(w3cb[hc % 2][:, :], 8),
                                dr3(w3t.ap()[:, hc * 128:(hc + 1) * 128])), 16)
                            sy.wait_ge(c3.h, UD[(hc, 3)])
                            UST[hc][0].inc(sy.dma_start(
                                uT.ap()[hc * 128:(hc + 1) * 128, :],
                                ub[hc % 2][:, :]), 16)
                        sy.wait_ge(c2.h, M2E[3])
                        dm[13].inc(sy.dma_start(mid2d.ap(), mid2f[0:2, :]), 16)
                        for i_ in (6, 7, 13, 14, 15, 18, 19):
                            sy.wait_ge(dm[i_].h, dm[i_].n)
                        ph.inc(sy.nop())

                    @blk.tensor
                    def _(t):
                        t.wait_ge(ph.h, ph_in)
                        pend6 = None
                        for hc in range(86):
                            for s_ in range(4):
                                js = js_of(hc, s_)
                                if s_ == 0:
                                    wt(t, W1L[hc])
                                    wt(t, W3L[hc])
                                    if hc == 0:
                                        wt(t, BL)
                                if js >= 2:
                                    pj = (js - 2) // 4, (js - 2) % 4
                                    t.wait_ge(c3.h, UD[pj])
                                for wcb, bT, mf, php in (
                                        (w1cb, b1T_s, mid1f, ph1),
                                        (w3cb, b3T_s, mid3f, ph3)):
                                    for dc in range(8):
                                        t.matmul(
                                            php[s_ % 2][:, :],
                                            wcb[hc % 2][:, dc * 128:(dc + 1) * 128],
                                            xs_sb[:, dc * TH + s_ * 512:
                                                  dc * TH + s_ * 512 + 512],
                                            start=(dc == 0), stop=False)
                                    mm = t.matmul(
                                        php[s_ % 2][:, :],
                                        bT[0:2, hc * 128:(hc + 1) * 128],
                                        mf[0:2, s_ * 512:(s_ + 1) * 512],
                                        start=False, stop=True)
                                    c1.inc(mm)
                                if pend6 is not None:
                                    hp, sp = pend6
                                    t.wait_ge(c3.h, UD[pend6])
                                    c1.inc(t.matmul(
                                        pqm[sp][:, :],
                                        a2T_sb[:, hp * 2:hp * 2 + 2],
                                        ub[hp % 2][:, sp * 512:(sp + 1) * 512],
                                        start=(hp == 0), stop=(hp == 85)))
                                pend6 = (hc, s_)
                        hp, sp = pend6
                        t.wait_ge(c3.h, UD[pend6])
                        c1.inc(t.matmul(
                            pqm[sp][:, :],
                            a2T_sb[:, hp * 2:hp * 2 + 2],
                            ub[hp % 2][:, sp * 512:(sp + 1) * 512],
                            start=(hp == 0), stop=(hp == 85)))
                        endsec(t, c1)

                    @blk.scalar
                    def _(sc):
                        sc.wait_ge(ph.h, ph_in)
                        for hc in range(86):
                            for s_ in range(4):
                                js = js_of(hc, s_)
                                sc.wait_ge(c1.h, H1[(hc, s_)])
                                if js >= 2:
                                    sc.wait_ge(c3.h, UD[((js - 2) // 4,
                                                        (js - 2) % 4)])
                                c2.inc(sc.activation(
                                    silb[js % 2][:, :],
                                    ph1[s_ % 2][:, :], AF.Silu))
                        sc.wait_ge(c1.h, MID2[(85, 3)])
                        for s_ in range(4):
                            c2.inc(sc.mul(mid2f[0:2, s_ * 512:(s_ + 1) * 512],
                                          pqm[s_][:, :], 0.5))
                        endsec(sc, c2)

                    @blk.vector
                    def _(v):
                        v.wait_ge(ph.h, ph_in)
                        for hc in range(86):
                            for s_ in range(4):
                                js = js_of(hc, s_)
                                v.wait_ge(c2.h, SIL[(hc, s_)])
                                v.wait_ge(c1.h, H3[(hc, s_)])
                                if s_ == 0 and hc >= 2:
                                    wt(v, UST[hc - 2])
                                    v.wait_ge(c1.h, MID2[(hc - 2, 3)])
                                c3.inc(v.tensor_mul(
                                    ub[hc % 2][:, s_ * 512:(s_ + 1) * 512],
                                    silb[js % 2][:, :],
                                    ph3[s_ % 2][:, :]))
                        endsec(v, c3)

                assert ph.n == ph_in + 4

        if _KPH <= 8:
            _fin()
            return nc

        with ExitStack() as P:
            us_sb = P.enter_context(nc.sbuf_tensor("us_sb", [128, 22 * TH], BF16))
            with ExitStack() as Pa:
                ul0 = Pa.enter_context(nc.sbuf_tensor("ul0", [128, 4 * TH], BF16))
                ul1 = Pa.enter_context(nc.sbuf_tensor("ul1", [128, 4 * TH], BF16))
                utmp = Pa.enter_context(nc.sbuf_tensor("utmp", [128, TH], F32))
                ulb = [ul0, ul1]

                ph_in = ph.n
                UL = {}
                nul = [dm[6].n, dm[7].n]
                for j in range(22):
                    nul[j % 2] += 4 * 16
                    UL[j] = (dm[6 + j % 2], nul[j % 2])
                FD = {j: c6.n + j + 1 for j in range(22)}

                with nc.Block() as blk:
                    @blk.sync
                    def _(sy):
                        sy.wait_ge(ph.h, ph_in)
                        for j in range(22):
                            rows = 128 if j < 21 else 64
                            if j >= 2:
                                sy.wait_ge(c6.h, FD[j - 2])
                            for c_ in range(4):
                                r0 = c_ * HQ + j * 128
                                dm[6 + j % 2].inc(sy.dma_start(
                                    ulb[j % 2][0:rows, c_ * TH:c_ * TH + TH],
                                    uT.ap()[r0:r0 + rows, :]), 16)
                        for i_ in (6, 7):
                            sy.wait_ge(dm[i_].h, dm[i_].n)
                        ph.inc(sy.nop())

                    @blk.vector
                    def _(v):
                        v.wait_ge(ph.h, ph_in)
                        for j in range(22):
                            rows = 128 if j < 21 else 64
                            wt(v, UL[j])
                            ub_ = ulb[j % 2]
                            v.wait_ge(vch.h, vch.n)
                            v.wait_ge(c6.h, c6.n)
                            vch.inc(v.tensor_add(utmp[0:rows, :],
                                                 ub_[0:rows, 0:TH],
                                                 ub_[0:rows, TH:2 * TH]))
                            v.wait_ge(vch.h, vch.n)
                            vch.inc(v.tensor_add(utmp[0:rows, :],
                                                 utmp[0:rows, :],
                                                 ub_[0:rows, 2 * TH:3 * TH]))
                            v.wait_ge(vch.h, vch.n)
                            c6.inc(v.tensor_add(
                                us_sb[0:rows, j * TH:(j + 1) * TH],
                                utmp[0:rows, :],
                                ub_[0:rows, 3 * TH:4 * TH]))
                        endsec(v, c6)

                assert ph.n == ph_in + 2

            with ExitStack() as P:
                w2c0 = P.enter_context(nc.sbuf_tensor("w2c0", [128, 22 * 256], BF16))
                w2c1 = P.enter_context(nc.sbuf_tensor("w2c1", [128, 22 * 256], BF16))
                yr0 = P.enter_context(nc.sbuf_tensor("yr0", [128, 16 * 256], F32))
                yr1 = P.enter_context(nc.sbuf_tensor("yr1", [128, 16 * 256], F32))
                zt0 = P.enter_context(nc.sbuf_tensor("zt0", [128, 16 * 256], F32))
                zt1 = P.enter_context(nc.sbuf_tensor("zt1", [128, 16 * 256], F32))
                b2T_s = P.enter_context(nc.sbuf_tensor("b2T_s", [2, D], BF16))
                mid2s = P.enter_context(nc.sbuf_tensor("mid2s", [2, TH], BF16))
                trs2 = P.enter_context(nc.sbuf_tensor("trs2", [128, 256], F32))
                pz0 = P.enter_context(nc.psum_tensor("pz0", [128, 256], F32))
                pz1 = P.enter_context(nc.psum_tensor("pz1", [128, 256], F32))
                pz2 = P.enter_context(nc.psum_tensor("pz2", [128, 256], F32))
                w2cb = [w2c0, w2c1]
                yrb = [yr0, yr1]
                ztb = [zt0, zt1]
                pz = [pz0, pz1, pz2]

                ph_in = ph.n
                jobs7 = [(oc, tt) for oc in range(16) for tt in range(16)]
                j7 = {jj: i for i, jj in enumerate(jobs7)}
                BM2L = (dm[13], dm[13].n + 32)
                W2L = {}
                YRL = {}
                ZST = {}
                nw2 = [dm[6].n, dm[7].n]
                nyr = [dm[18].n, dm[19].n]
                nzs = [dm[14].n, dm[15].n]
                for oc in range(16):
                    nw2[oc % 2] += 16
                    W2L[oc] = (dm[6 + oc % 2], nw2[oc % 2])
                    nyr[oc % 2] += 16
                    YRL[oc] = (dm[18 + oc % 2], nyr[oc % 2])
                    nzs[oc % 2] += 16
                    ZST[oc] = (dm[14 + oc % 2], nzs[oc % 2])
                ZM = {}
                n = c1.n
                for jj in jobs7:
                    n += 1
                    ZM[jj] = n
                ZEV = {}
                n = c6.n
                for jj in jobs7:
                    n += 1
                    ZEV[jj] = n
                ZSQ = {}
                n = c2.n
                for jj in jobs7:
                    n += 1
                    ZSQ[jj] = n

                with nc.Block() as blk:
                    @blk.sync
                    def _(sy):
                        sy.wait_ge(ph.h, ph_in)
                        dm[13].inc(sy.dma_start(b2T_s[0:2, :], b2td.ap()), 16)
                        dm[13].inc(sy.dma_start(mid2s[0:2, :], mid2d.ap()), 16)
                        for oc in range(16):
                            if oc >= 2:
                                sy.wait_ge(c1.h, ZM[(oc - 2, 15)])
                            W2L[oc][0].inc(sy.dma_start(
                                sb3(w2cb[oc % 2][:, :], 22),
                                dr3(w2t.ap()[:, oc * 256:(oc + 1) * 256])), 16)
                            if oc >= 2:
                                sy.wait_ge(c6.h, ZEV[(oc - 2, 15)])
                            YRL[oc][0].inc(sy.dma_start(
                                sb3(yrb[oc % 2][:, :], 16),
                                dr3(y32.ap()[:, oc * 256:(oc + 1) * 256])), 16)
                            sy.wait_ge(c6.h, ZEV[(oc, 15)])
                            ZST[oc][0].inc(sy.dma_start(
                                dr3(zpre.ap()[:, oc * 256:(oc + 1) * 256]),
                                sb3(ztb[oc % 2][:, :], 16)), 16)
                        for i_ in (6, 7, 13, 14, 15, 18, 19):
                            sy.wait_ge(dm[i_].h, dm[i_].n)
                        ph.inc(sy.nop())

                    @blk.tensor
                    def _(t):
                        t.wait_ge(ph.h, ph_in)
                        for jj in jobs7:
                            oc, tt = jj
                            i = j7[jj]
                            if tt == 0:
                                wt(t, W2L[oc])
                                if oc == 0:
                                    wt(t, BM2L)
                            if i >= 3:
                                t.wait_ge(c6.h, ZEV[jobs7[i - 3]])
                            for j in range(22):
                                rows = 128 if j < 21 else 64
                                t.matmul(
                                    pz[i % 3][:, :],
                                    us_sb[0:rows, j * TH + tt * 128:
                                          j * TH + tt * 128 + 128],
                                    w2cb[oc % 2][0:rows,
                                                 j * 256:(j + 1) * 256],
                                    start=(j == 0), stop=False)
                            mm = t.matmul(
                                pz[i % 3][:, :],
                                mid2s[0:2, tt * 128:(tt + 1) * 128],
                                b2T_s[0:2, oc * 256:(oc + 1) * 256],
                                start=False, stop=True)
                            c1.inc(mm)
                        endsec(t, c1)

                    @blk.vector
                    def _(v):
                        v.wait_ge(ph.h, ph_in)
                        for jj in jobs7:
                            oc, tt = jj
                            i = j7[jj]
                            v.wait_ge(c1.h, ZM[jj])
                            if tt == 0:
                                wt(v, YRL[oc])
                                if oc >= 2:
                                    wt(v, ZST[oc - 2])
                                    v.wait_ge(c2.h, ZSQ[(oc - 2, 15)])
                            c6.inc(v.scalar_tensor_tensor(
                                ztb[oc % 2][:, tt * 256:(tt + 1) * 256],
                                pz[i % 3][:, :], 1.0,
                                yrb[oc % 2][:, tt * 256:(tt + 1) * 256],
                                ALU.mult, ALU.add,
                                accum_out=stat_sb[:, 2 + i % 2:3 + i % 2]))
                            v.wait_ge(c6.h, ZEV[jj])
                            v.wait_ge(vch.h, vch.n)
                            if oc == 0:
                                vch.inc(v.tensor_copy(
                                    zsum_sb[:, tt:tt + 1],
                                    stat_sb[:, 2 + i % 2:3 + i % 2]))
                            else:
                                vch.inc(v.tensor_add(
                                    zsum_sb[:, tt:tt + 1],
                                    zsum_sb[:, tt:tt + 1],
                                    stat_sb[:, 2 + i % 2:3 + i % 2]))
                            if i >= 1:
                                poc, ptt = jobs7[i - 1]
                                v.wait_ge(c2.h, ZSQ[jobs7[i - 1]])
                                v.wait_ge(vch.h, vch.n)
                                cell = stat_sb[:, 4 + (i - 1) % 2:
                                               5 + (i - 1) % 2]
                                if poc == 0:
                                    vch.inc(v.tensor_copy(
                                        zsq_sb[:, ptt:ptt + 1], cell))
                                else:
                                    vch.inc(v.tensor_add(
                                        zsq_sb[:, ptt:ptt + 1],
                                        zsq_sb[:, ptt:ptt + 1], cell))
                        i = len(jobs7)
                        poc, ptt = jobs7[-1]
                        v.wait_ge(c2.h, ZSQ[jobs7[-1]])
                        v.wait_ge(vch.h, vch.n)
                        vch.inc(v.tensor_add(
                            zsq_sb[:, ptt:ptt + 1],
                            zsq_sb[:, ptt:ptt + 1],
                            stat_sb[:, 4 + (i - 1) % 2:5 + (i - 1) % 2]))
                        endsec(v, c6)

                    @blk.scalar
                    def _(sc):
                        sc.wait_ge(ph.h, ph_in)
                        for jj in jobs7:
                            oc, tt = jj
                            i = j7[jj]
                            sc.wait_ge(c6.h, ZEV[jj])
                            if i >= 1:
                                sc.wait_ge(c2.h, ZSQ[jobs7[i - 1]])
                            c2.inc(sc.activation(
                                trs2[:, :],
                                ztb[oc % 2][:, tt * 256:(tt + 1) * 256],
                                AF.Square,
                                accum_out=stat_sb[:, 4 + i % 2:5 + i % 2]))
                        endsec(sc, c2)

                assert ph.n == ph_in + 4

        if _KPH <= 10:
            _fin()
            return nc

        with ExitStack() as P:
            zp0 = P.enter_context(nc.sbuf_tensor("zp0", [128, D], F32))
            zp1 = P.enter_context(nc.sbuf_tensor("zp1", [128, D], F32))
            oo0 = P.enter_context(nc.sbuf_tensor("oo0", [128, D], F32))
            oo1 = P.enter_context(nc.sbuf_tensor("oo1", [128, D], F32))
            zpb = [zp0, zp1]
            oob = [oo0, oo1]

            ph_in = ph.n
            ZL = {}
            OST = {}
            nzl = [dm[6].n, dm[7].n]
            nos = [dm[14].n, dm[15].n]
            for tt in range(16):
                nzl[tt % 2] += 16
                ZL[tt] = (dm[6 + tt % 2], nzl[tt % 2])
                nos[tt % 2] += 16
                OST[tt] = (dm[14 + tt % 2], nos[tt % 2])
            VA8 = {tt: c3.n + tt + 1 for tt in range(16)}
            RS8 = {tt: c2.n + tt + 1 for tt in range(16)}
            NM8 = {tt: c6.n + tt + 1 for tt in range(16)}

            with nc.Block() as blk:
                @blk.sync
                def _(sy):
                    sy.wait_ge(ph.h, ph_in)
                    for tt in range(16):
                        if tt >= 2:
                            sy.wait_ge(c6.h, NM8[tt - 2])
                        ZL[tt][0].inc(sy.dma_start(
                            zpb[tt % 2][:, :],
                            zpre.ap()[tt * 128:(tt + 1) * 128, :]), 16)
                        sy.wait_ge(c6.h, NM8[tt])
                        OST[tt][0].inc(sy.dma_start(
                            o.ap()[tt * 128:(tt + 1) * 128, :],
                            oob[tt % 2][:, :]), 16)
                    sy.wait_ge(dm[14].h, dm[14].n)
                    sy.wait_ge(dm[15].h, dm[15].n)
                    ph.inc(sy.nop())

                @blk.vector
                def _(v):
                    v.wait_ge(ph.h, ph_in)
                    for tt in range(16):
                        p = tt % 2
                        if tt >= 2:
                            v.wait_ge(c2.h, RS8[tt - 2])
                            v.wait_ge(c6.h, NM8[tt - 2])
                        v.wait_ge(vch.h, vch.n)
                        vch.inc(v.tensor_scalar_mul(
                            stat_sb[:, p:p + 1],
                            zsum_sb[:, tt:tt + 1], 1.0 / D))
                        v.wait_ge(vch.h, vch.n)
                        vch.inc(v.tensor_mul(stat_sb[:, 6 + p:7 + p],
                                             stat_sb[:, p:p + 1],
                                             stat_sb[:, p:p + 1]))
                        v.wait_ge(vch.h, vch.n)
                        c3.inc(v.scalar_tensor_tensor(
                            stat_sb[:, 2 + p:3 + p],
                            zsq_sb[:, tt:tt + 1], 1.0 / D,
                            stat_sb[:, 6 + p:7 + p],
                            ALU.mult, ALU.subtract))
                        if tt >= 1:
                            pp = (tt - 1) % 2
                            v.wait_ge(c2.h, RS8[tt - 1])
                            if tt - 1 >= 2:
                                wt(v, OST[tt - 3])
                            wt(v, ZL[tt - 1])
                            c6.inc(v.tensor_scalar(
                                oob[pp][:, :], zpb[pp][:, :],
                                stat_sb[:, pp:pp + 1],
                                stat_sb[:, 4 + pp:5 + pp],
                                ALU.subtract, ALU.mult))
                    pp = 15 % 2
                    v.wait_ge(c2.h, RS8[15])
                    wt(v, OST[13])
                    wt(v, ZL[15])
                    c6.inc(v.tensor_scalar(
                        oob[pp][:, :], zpb[pp][:, :],
                        stat_sb[:, pp:pp + 1],
                        stat_sb[:, 4 + pp:5 + pp],
                        ALU.subtract, ALU.mult))
                    endsec(v, c6)

                @blk.scalar
                def _(sc):
                    sc.wait_ge(ph.h, ph_in)
                    for tt in range(16):
                        p = tt % 2
                        sc.wait_ge(c3.h, VA8[tt])
                        sch.inc(sc.activation(stat_sb[:, 6 + p:7 + p],
                                              stat_sb[:, 2 + p:3 + p], AF.Ln,
                                              bias=EPS))
                        sc.wait_ge(sch.h, sch.n)
                        c2.inc(sc.activation(stat_sb[:, 4 + p:5 + p],
                                             stat_sb[:, 6 + p:7 + p],
                                             AF.Exp, scale=-0.5))
                    endsec(sc, c2)

    return nc


_NC_CACHE = []


def _get_program():
    if not _NC_CACHE:
        _NC_CACHE.append(build_program())
    return _NC_CACHE[0]


def _consts():
    inv_freq = (1.0 / (500000.0 **
                       (np.arange(0, HD, 2, dtype=np.float32) / HD))
                ).astype(np.float32)
    t = np.arange(T, dtype=np.float32)
    freqs = np.outer(t, inv_freq).astype(np.float32)
    cosT = np.cos(freqs).T.astype(np.float32)
    sinT = np.sin(freqs).T.astype(np.float32)
    csa_c = np.empty((128, T), dtype=ml_dtypes.bfloat16)
    csa_c[0:64] = cosT
    csa_c[64:128] = cosT
    csb_c = np.empty((128, T), dtype=ml_dtypes.bfloat16)
    csb_c[0:64] = -sinT
    csb_c[64:128] = sinT
    swp_c = np.zeros((128, 128), dtype=ml_dtypes.bfloat16)
    for m in range(128):
        swp_c[(m + 64) % 128, m] = 1.0
    ident_c = np.eye(128, dtype=ml_dtypes.bfloat16)
    ones_c = np.ones((128, 128), dtype=ml_dtypes.bfloat16)
    kt = np.arange(128)[:, None]
    qt = np.arange(512)[None, :]
    maskp_c = np.zeros((128, 2048), dtype=ml_dtypes.bfloat16)
    for j2 in range(2):
        for half in range(2):
            jj = 2 * j2 + half
            m = (jj * 128 + kt <= qt).astype(ml_dtypes.bfloat16)
            maskp_c[:, j2 * 1024 + half * 512:
                    j2 * 1024 + half * 512 + 512] = m
    return csa_c, csb_c, swp_c, ident_c, ones_c, maskp_c


def kernel(L0_attn_x, L0_attn_Wq, L0_attn_Wk, L0_attn_Wv, L0_attn_Wo,
           L0_ffn_w1q, L0_ffn_a1, L0_ffn_b1, L0_ffn_w3q, L0_ffn_a3,
           L0_ffn_b3, L0_ffn_w2q, L0_ffn_a2, L0_ffn_b2):
    xall = np.ascontiguousarray(np.asarray(L0_attn_x, dtype=np.float32))
    Wq = np.asarray(L0_attn_Wq, dtype=np.float32)
    Wk = np.asarray(L0_attn_Wk, dtype=np.float32)
    Wv = np.asarray(L0_attn_Wv, dtype=np.float32)
    Wo = np.ascontiguousarray(np.asarray(L0_attn_Wo, dtype=np.float32))
    w1q = np.ascontiguousarray(np.asarray(L0_ffn_w1q, dtype=np.float32))
    w3q = np.ascontiguousarray(np.asarray(L0_ffn_w3q, dtype=np.float32))
    w2q = np.ascontiguousarray(np.asarray(L0_ffn_w2q, dtype=np.float32))
    a1_ = np.ascontiguousarray(np.asarray(L0_ffn_a1, dtype=np.float32))
    b1_ = np.ascontiguousarray(np.asarray(L0_ffn_b1, dtype=np.float32))
    a3_ = np.ascontiguousarray(np.asarray(L0_ffn_a3, dtype=np.float32))
    b3_ = np.ascontiguousarray(np.asarray(L0_ffn_b3, dtype=np.float32))
    a2_ = np.ascontiguousarray(np.asarray(L0_ffn_a2, dtype=np.float32))
    b2_ = np.ascontiguousarray(np.asarray(L0_ffn_b2, dtype=np.float32))

    csa_c, csb_c, swp_c, ident_c, ones_c, maskp_c = _consts()

    in_maps = []
    for c in range(8):
        b = c // 2
        s = c % 2
        in_maps.append({
            "x": xall[b],
            "xh": np.ascontiguousarray(xall[b][s * TH:(s + 1) * TH]),
            "wq": np.ascontiguousarray(Wq[:, s * 2048:(s + 1) * 2048]),
            "wk": np.ascontiguousarray(Wk[:, s * 512:(s + 1) * 512]),
            "wv": np.ascontiguousarray(Wv[:, s * 512:(s + 1) * 512]),
            "wo": Wo,
            "w1": w1q, "w3": w3q, "w2": w2q,
            "a1": a1_, "b1": b1_, "a3": a3_, "b3": b3_,
            "a2": a2_, "b2": b2_,
            "csa": csa_c, "csb": csb_c, "swp": swp_c,
            "ident": ident_c, "onesb": ones_c,
            "maskp": maskp_c,
            "agoff": np.array([[s * D * TH]], dtype=np.int32),
        })

    nc = _get_program()
    res = run_bass_kernel_spmd(nc, in_maps, core_ids=list(range(8)))
    out = np.empty((B, T, D), dtype=np.float32)
    for c in range(8):
        b = c // 2
        s = c % 2
        out[b, s * TH:(s + 1) * TH] = np.asarray(res.results[c]["o"])
    return out


# version 04 (final)
# speedup vs baseline: 1.2414x; 1.2414x over previous
import math
import os
from contextlib import ExitStack

_KPH = int(os.environ.get("KPH", "99"))
_KP3G = int(os.environ.get("KP3G", "32"))
_KNOMASK = os.environ.get("KNOMASK", "0") == "1"
_KNOEXP2 = os.environ.get("KNOEXP2", "0") == "1"
_KNOEPI = os.environ.get("KNOEPI", "0") == "1"
_KNOROW = os.environ.get("KNOROW", "0") == "1"

import numpy as np
import ml_dtypes

import concourse.bass as bass
import concourse.mybir as mybir
from concourse.bass_utils import run_bass_kernel_spmd

F32 = mybir.dt.float32
BF16 = mybir.dt.bfloat16
I32 = mybir.dt.int32
AF = mybir.ActivationFunctionType
ALU = mybir.AluOpType
AX = mybir.AxisListType

B, T, D = 4, 4096, 4096
HD = 128
NHC = 16
NGC = 4
TH = T // 2
HF = 11008
DQ = 1024
HQ = 2752
HQP = 2816
SCALE = 1.0 / math.sqrt(HD)
EPS = 1e-5


class Ctr:
    def __init__(self, handle):
        self.h = handle
        self.n = 0

    def inc(self, instr, k=1):
        instr.then_inc(self.h, k)
        self.n += k
        return self.n


def sb3(sb_slice, tiles):
    return sb_slice.rearrange("p (t c) -> p t c", t=tiles)


def dr3(dr_slice):
    return dr_slice.rearrange("(t p) c -> p t c", p=128)


def build_program():
    nc = bass.Bass("TRN2", target_bir_lowering=False)

    x = nc.dram_tensor("x", [T, D], F32, kind="ExternalInput")
    xh = nc.dram_tensor("xh", [TH, D], F32, kind="ExternalInput")
    wq = nc.dram_tensor("wq", [D, NHC * HD], F32, kind="ExternalInput")
    wk = nc.dram_tensor("wk", [D, NGC * HD], F32, kind="ExternalInput")
    wv = nc.dram_tensor("wv", [D, NGC * HD], F32, kind="ExternalInput")
    wo = nc.dram_tensor("wo", [D, D], F32, kind="ExternalInput")
    w1 = nc.dram_tensor("w1", [HF, DQ], F32, kind="ExternalInput")
    w3 = nc.dram_tensor("w3", [HF, DQ], F32, kind="ExternalInput")
    w2 = nc.dram_tensor("w2", [D, HQ], F32, kind="ExternalInput")
    a1 = nc.dram_tensor("a1", [2, D], F32, kind="ExternalInput")
    b1 = nc.dram_tensor("b1", [HF, 2], F32, kind="ExternalInput")
    a3 = nc.dram_tensor("a3", [2, D], F32, kind="ExternalInput")
    b3 = nc.dram_tensor("b3", [HF, 2], F32, kind="ExternalInput")
    a2 = nc.dram_tensor("a2", [2, HF], F32, kind="ExternalInput")
    b2 = nc.dram_tensor("b2", [D, 2], F32, kind="ExternalInput")
    csa = nc.dram_tensor("csa", [128, T], BF16, kind="ExternalInput")
    csb = nc.dram_tensor("csb", [128, T], BF16, kind="ExternalInput")
    swp = nc.dram_tensor("swp", [128, 128], BF16, kind="ExternalInput")
    ident = nc.dram_tensor("ident", [128, 128], BF16, kind="ExternalInput")
    onesb = nc.dram_tensor("onesb", [128, 128], BF16, kind="ExternalInput")
    maskp = nc.dram_tensor("maskp", [128, 2048], BF16, kind="ExternalInput")
    agoff = nc.dram_tensor("agoff", [1, 1], I32, kind="ExternalInput")
    o = nc.dram_tensor("o", [TH, D], F32, kind="ExternalOutput")

    xbf = nc.dram_tensor("xbf", [T, D], BF16)
    w1bf = nc.dram_tensor("w1bf", [HF, DQ], BF16)
    w3bf = nc.dram_tensor("w3bf", [HF, DQ], BF16)
    w2bf = nc.dram_tensor("w2bf", [D, HQP], BF16)
    w1t = nc.dram_tensor("w1t", [DQ, HF], BF16)
    w3t = nc.dram_tensor("w3t", [DQ, HF], BF16)
    w2t = nc.dram_tensor("w2t", [HQP, D], BF16)
    b1td = nc.dram_tensor("b1td", [2, HF], BF16)
    b3td = nc.dram_tensor("b3td", [2, HF], BF16)
    b2td = nc.dram_tensor("b2td", [2, D], BF16)
    kT = nc.dram_tensor("kT", [NGC * HD, T], BF16)
    vS = nc.dram_tensor("vS", [T, NGC * HD], BF16)
    qT = nc.dram_tensor("qT", [NHC * HD, T], BF16)
    ctx_lo = nc.dram_tensor("ctx_lo", [NHC * HD, TH], BF16)
    ctx_hi = nc.dram_tensor("ctx_hi", [NHC * HD, TH], BF16)
    ctxAG = nc.dram_tensor("ctxAG", [2 * D, TH], BF16)
    ctxF = nc.dram_tensor("ctxF", [D, TH], BF16)
    ypre = nc.dram_tensor("ypre", [TH, D], F32)
    y32 = nc.dram_tensor("y32", [TH, D], F32)
    ybf = nc.dram_tensor("ybf", [TH, D], BF16)
    uT = nc.dram_tensor("uT", [HF, TH], BF16)
    mid2d = nc.dram_tensor("mid2d", [2, TH], BF16)
    zpre = nc.dram_tensor("zpre", [TH, D], F32)

    with ExitStack() as CTX:
        E = CTX.enter_context

        ones_sb = E(nc.sbuf_tensor("ones_sb", [128, 128], BF16))
        onesf_sb = E(nc.sbuf_tensor("onesf_sb", [128, 32], F32))
        eps_sb = E(nc.sbuf_tensor("eps_sb", [128, 1], F32))
        nc.const_aps.register(F32, EPS, eps_sb[:, 0:1])
        mneg_sb = E(nc.sbuf_tensor("mneg_sb", [128, 128], F32))
        qmax2_sb = E(nc.sbuf_tensor("qmax2_sb", [128, 128], F32))
        kmax2_sb = E(nc.sbuf_tensor("kmax2_sb", [128, NGC], F32))
        stat_sb = E(nc.sbuf_tensor("stat_sb", [128, 8], F32))
        a1T_sb = E(nc.sbuf_tensor("a1T_sb", [128, 64], BF16))
        a3T_sb = E(nc.sbuf_tensor("a3T_sb", [128, 64], BF16))
        a2T_sb = E(nc.sbuf_tensor("a2T_sb", [128, 172], BF16))
        ysum_sb = E(nc.sbuf_tensor("ysum_sb", [128, 16], F32))
        ysq_sb = E(nc.sbuf_tensor("ysq_sb", [128, 16], F32))
        zsum_sb = E(nc.sbuf_tensor("zsum_sb", [128, 16], F32))
        zsq_sb = E(nc.sbuf_tensor("zsq_sb", [128, 16], F32))
        agoff_sb = E(nc.sbuf_tensor("agoff_sb", [1, 1], I32))

        ph = Ctr(E(nc.semaphore("ph")))
        dm = [Ctr(E(nc.semaphore(f"dm{i}"))) for i in range(22)]

        def wt(eng, ev):
            eng.wait_ge(ev[0].h, ev[1])
        c1 = Ctr(E(nc.semaphore("s1")))
        c2 = Ctr(E(nc.semaphore("s2")))
        c3 = Ctr(E(nc.semaphore("s3")))
        c4 = Ctr(E(nc.semaphore("s4")))
        c5 = Ctr(E(nc.semaphore("s5")))
        c6 = Ctr(E(nc.semaphore("s6")))
        cc1 = Ctr(E(nc.semaphore("cc1")))
        cc2 = Ctr(E(nc.semaphore("cc2")))
        vch = Ctr(E(nc.semaphore("vch")))
        sch = Ctr(E(nc.semaphore("sch")))

        def endsec(eng, own=None, val=None):
            if own is not None:
                eng.wait_ge(own.h, val if val is not None else own.n)
            ph.inc(eng.nop())

        def _fin():
            ph_f = ph.n
            with nc.Block() as blk:
                @blk.sync
                def _(sy):
                    sy.wait_ge(ph.h, ph_f)
                    dm[14].inc(sy.dma_start(o.ap(), x.ap()[0:TH, :]), 16)
                    sy.wait_ge(dm[14].h, dm[14].n)

        with ExitStack() as P:
            tp0 = P.enter_context(nc.sbuf_tensor("tp0", [128, HF], BF16))
            tp1 = P.enter_context(nc.sbuf_tensor("tp1", [128, HF], BF16))
            ab_sb = P.enter_context(nc.sbuf_tensor("ab_sb", [2, HF], BF16))
            bb_sb = P.enter_context(nc.sbuf_tensor("bb_sb", [128, 172], BF16))
            btT_sb = P.enter_context(nc.sbuf_tensor("btT_sb", [2, HF], BF16))
            id_sb = P.enter_context(nc.sbuf_tensor("id_sb", [128, 128], BF16))
            ps_a = P.enter_context(nc.psum_tensor("ps_a", [128, 256], BF16))
            ps_b = P.enter_context(nc.psum_tensor("ps_b", [128, 256], BF16))
            tp = [tp0, tp1]

            ph_in = ph.n
            GEV = {
                "xbf0": (dm[0], dm[0].n + 16),
                "xbf1": (dm[1], dm[1].n + 16),
                "wcasts": (dm[2], dm[2].n + 48),
                "ab_a1": (dm[3], dm[3].n + 16),
                "ab_a3": (dm[3], dm[3].n + 32),
                "ab_a2": (dm[3], dm[3].n + 48),
                "bb_b1": (dm[4], dm[4].n + 16),
                "bb_b3": (dm[4], dm[4].n + 32),
                "bb_b2": (dm[4], dm[4].n + 48),
                "consts": (dm[5], dm[5].n + 48),
            }
            AEV = {}
            n = c2.n
            n += 1; AEV["a1"] = n
            bev1 = {}
            for t in range(86):
                n += 1; bev1[t] = n
            AEV["b1"] = n
            n += 1; AEV["a3"] = n
            bev3 = {}
            for t in range(86):
                n += 1; bev3[t] = n
            AEV["b3"] = n
            n += 1; AEV["a2"] = n
            bev2 = {}
            for t in range(32):
                n += 1; bev2[t] = n
            AEV["b2"] = n
            TEV = {"b1g": {}, "b3g": {}, "b2g": {}}
            n = c1.n
            n += 1; TEV["a1"] = n
            for t in range(86):
                n += 1; TEV["b1g"][t] = n
            n += 1; TEV["a3"] = n
            for t in range(86):
                n += 1; TEV["b3g"][t] = n
            n += 1; TEV["a2"] = n
            for t in range(32):
                n += 1; TEV["b2g"][t] = n
            SEV = {}
            tpn = [dm[6].n, dm[7].n]
            for wid, ntile in (("w1", 8), ("w3", 8), ("w2", 22)):
                for wi in range(ntile):
                    pq = wi % 2
                    tpn[pq] += 16
                    SEV[(wid + "tp", wi)] = (dm[6 + pq], tpn[pq])
                    tpn[pq] += 16
                    SEV[(wid + "st", wi)] = (dm[6 + pq], tpn[pq])
            nb = dm[8].n
            for k in ["b1td", "b3td", "b2td"]:
                nb += 16
                SEV[k] = (dm[8], nb)

            with nc.Block() as blk:
                @blk.gpsimd
                def _(g):
                    dm[0].inc(g.dma_start(xbf.ap()[0:TH, :], x.ap()[0:TH, :]), 16)
                    dm[1].inc(g.dma_start(xbf.ap()[TH:T, :], x.ap()[TH:T, :]), 16)
                    dm[2].inc(g.dma_start(w1bf.ap(), w1.ap()), 16)
                    dm[2].inc(g.dma_start(w3bf.ap(), w3.ap()), 16)
                    dm[2].inc(g.dma_start(w2bf.ap()[:, 0:HQ], w2.ap()), 16)
                    dm[3].inc(g.dma_start(ab_sb[0:2, 0:D], a1.ap()), 16)
                    dm[4].inc(g.dma_start(sb3(bb_sb[:, 0:172], 86),
                                          dr3(b1.ap())), 16)
                    g.wait_ge(c1.h, TEV["b1g"][85])
                    g.wait_ge(c1.h, TEV["a1"])
                    dm[3].inc(g.dma_start(ab_sb[0:2, 0:D], a3.ap()), 16)
                    dm[4].inc(g.dma_start(sb3(bb_sb[:, 0:172], 86),
                                          dr3(b3.ap())), 16)
                    g.wait_ge(c1.h, TEV["b3g"][85])
                    g.wait_ge(c1.h, TEV["a3"])
                    dm[3].inc(g.dma_start(ab_sb[0:2, 0:HF], a2.ap()), 16)
                    dm[4].inc(g.dma_start(sb3(bb_sb[:, 0:64], 32),
                                          dr3(b2.ap())), 16)
                    for i_ in range(5):
                        g.wait_ge(dm[i_].h, dm[i_].n)
                    ph.inc(g.nop())

                @blk.sync
                def _(sy):
                    dm[5].inc(sy.dma_start(id_sb[:, :], ident.ap()), 16)
                    dm[5].inc(sy.dma_start(ones_sb[:, :], onesb.ap()), 16)
                    dm[5].inc(sy.dma_start(agoff_sb[:, :], agoff.ap()), 16)
                    last_st = [None, None]
                    for wt_, wbf, ntile, wid in (
                            (w1t, w1bf, 8, "w1"),
                            (w3t, w3bf, 8, "w3"),
                            (w2t, w2bf, 22, "w2")):
                        width = HF if wid != "w2" else D
                        for wi in range(ntile):
                            buf = tp[wi % 2]
                            wt(sy, GEV["wcasts"])
                            if last_st[wi % 2] is not None:
                                wt(sy, last_st[wi % 2])
                            SEV[(wid + "tp", wi)][0].inc(sy.dma_start(
                                buf[:, 0:width],
                                wbf.ap()[:, wi * 128:(wi + 1) * 128],
                                transpose=True), 16)
                            wt(sy, SEV[(wid + "tp", wi)])
                            SEV[(wid + "st", wi)][0].inc(sy.dma_start(
                                wt_.ap()[wi * 128:(wi + 1) * 128, :],
                                buf[:, 0:width]), 16)
                            last_st[wi % 2] = SEV[(wid + "st", wi)]
                    sy.wait_ge(c2.h, AEV["b1"])
                    dm[8].inc(sy.dma_start(b1td.ap(), btT_sb[0:2, :]), 16)
                    sy.wait_ge(c2.h, AEV["b3"])
                    dm[8].inc(sy.dma_start(b3td.ap(), btT_sb[0:2, :]), 16)
                    sy.wait_ge(c2.h, AEV["b2"])
                    dm[8].inc(sy.dma_start(b2td.ap(), btT_sb[0:2, 0:D]), 16)
                    for i_ in (5, 6, 7, 8):
                        sy.wait_ge(dm[i_].h, dm[i_].n)
                    ph.inc(sy.nop())

                @blk.tensor
                def _(t):
                    wt(t, GEV["ab_a1"])
                    wt(t, GEV["consts"])
                    for k in range(32):
                        mm = t.transpose(ps_a[:, 2 * k:2 * k + 2],
                                         ab_sb[0:2, 128 * k:128 * (k + 1)],
                                         id_sb[0:2, 0:2])
                    c1.inc(mm)
                    wt(t, GEV["bb_b1"])
                    for k in range(86):
                        if k >= 2:
                            t.wait_ge(c2.h, bev1[k - 1])
                        c1.inc(t.transpose(
                            ps_b[0:2, 128 * (k % 2):128 * (k % 2) + 128],
                            bb_sb[:, 2 * k:2 * k + 2], id_sb[:, :]))
                    wt(t, GEV["ab_a3"])
                    t.wait_ge(c2.h, AEV["a1"])
                    for k in range(32):
                        mm = t.transpose(ps_a[:, 2 * k:2 * k + 2],
                                         ab_sb[0:2, 128 * k:128 * (k + 1)],
                                         id_sb[0:2, 0:2])
                    c1.inc(mm)
                    wt(t, GEV["bb_b3"])
                    wt(t, SEV["b1td"])
                    for k in range(86):
                        if k >= 2:
                            t.wait_ge(c2.h, bev3[k - 1])
                        c1.inc(t.transpose(
                            ps_b[0:2, 128 * (k % 2):128 * (k % 2) + 128],
                            bb_sb[:, 2 * k:2 * k + 2], id_sb[:, :]))
                    wt(t, GEV["ab_a2"])
                    t.wait_ge(c2.h, AEV["a3"])
                    for k in range(86):
                        mm = t.transpose(ps_a[:, 2 * k:2 * k + 2],
                                         ab_sb[0:2, 128 * k:128 * (k + 1)],
                                         id_sb[0:2, 0:2])
                    c1.inc(mm)
                    wt(t, GEV["bb_b2"])
                    wt(t, SEV["b3td"])
                    for k in range(32):
                        if k >= 2:
                            t.wait_ge(c2.h, bev2[k - 1])
                        c1.inc(t.transpose(
                            ps_b[0:2, 128 * (k % 2):128 * (k % 2) + 128],
                            bb_sb[:, 2 * k:2 * k + 2], id_sb[:, :]))
                    endsec(t, c1)

                @blk.scalar
                def _(sc):
                    sc.wait_ge(c1.h, TEV["a1"])
                    c2.inc(sc.copy(a1T_sb[:, :], ps_a[:, 0:64]))
                    for k in range(86):
                        sc.wait_ge(c1.h, TEV["b1g"][k])
                        c2.inc(sc.copy(
                            btT_sb[0:2, 128 * k:128 * (k + 1)],
                            ps_b[0:2, 128 * (k % 2):128 * (k % 2) + 128]))
                    sc.wait_ge(c1.h, TEV["a3"])
                    c2.inc(sc.copy(a3T_sb[:, :], ps_a[:, 0:64]))
                    for k in range(86):
                        sc.wait_ge(c1.h, TEV["b3g"][k])
                        c2.inc(sc.copy(
                            btT_sb[0:2, 128 * k:128 * (k + 1)],
                            ps_b[0:2, 128 * (k % 2):128 * (k % 2) + 128]))
                    sc.wait_ge(c1.h, TEV["a2"])
                    c2.inc(sc.copy(a2T_sb[:, :], ps_a[:, 0:172]))
                    for k in range(32):
                        sc.wait_ge(c1.h, TEV["b2g"][k])
                        c2.inc(sc.copy(
                            btT_sb[0:2, 128 * k:128 * (k + 1)],
                            ps_b[0:2, 128 * (k % 2):128 * (k % 2) + 128]))
                    endsec(sc, c2)

                @blk.vector
                def _(v):
                    c6.inc(v.memset(eps_sb[:, :], EPS))
                    c6.inc(v.memset(onesf_sb[:, :], 1.0))
                    endsec(v, c6)

            assert c2.n == AEV["b2"]
            assert c1.n == TEV["b2g"][31]
            assert ph.n == ph_in + 5

        if _KPH <= 0:
            _fin()
            return nc

        with ExitStack() as P:
            csa_sb = P.enter_context(nc.sbuf_tensor("csa_sb", [128, T], BF16))
            csb_sb = P.enter_context(nc.sbuf_tensor("csb_sb", [128, T], BF16))
            swp_sb = P.enter_context(nc.sbuf_tensor("swp_sb", [128, 128], BF16))
            qe0 = P.enter_context(nc.sbuf_tensor("qe0", [128, 512], BF16))
            qe1 = P.enter_context(nc.sbuf_tensor("qe1", [128, 512], BF16))
            xT_sb = P.enter_context(nc.sbuf_tensor("xT_sb", [128, 32 * TH], BF16))
            wkq0 = P.enter_context(nc.sbuf_tensor("wkq0", [128, 32 * 128], BF16))
            wkq1 = P.enter_context(nc.sbuf_tensor("wkq1", [128, 32 * 128], BF16))
            wvb0 = P.enter_context(nc.sbuf_tensor("wvb0", [128, 32 * 256], BF16))
            wvb1 = P.enter_context(nc.sbuf_tensor("wvb1", [128, 32 * 256], BF16))
            rp0 = P.enter_context(nc.sbuf_tensor("rp0", [128, 1024], F32))
            sq0 = P.enter_context(nc.sbuf_tensor("sq0", [128, 512], BF16))
            sq1 = P.enter_context(nc.sbuf_tensor("sq1", [128, 512], BF16))
            ko0 = P.enter_context(nc.sbuf_tensor("ko0", [128, 512], BF16))
            ko1 = P.enter_context(nc.sbuf_tensor("ko1", [128, 512], BF16))
            vo0 = P.enter_context(nc.sbuf_tensor("vo0", [128, 256], BF16))
            vo1 = P.enter_context(nc.sbuf_tensor("vo1", [128, 256], BF16))
            kmc_sb = P.enter_context(nc.sbuf_tensor("kmc_sb", [128, 128], F32))
            qeb = [qe0, qe1]
            psw0 = P.enter_context(nc.psum_tensor("psw0", [128, 512], F32))
            psw1 = P.enter_context(nc.psum_tensor("psw1", [128, 512], F32))
            psk0 = P.enter_context(nc.psum_tensor("psk0", [128, 512], F32))
            psk1 = P.enter_context(nc.psum_tensor("psk1", [128, 512], F32))
            psk2 = P.enter_context(nc.psum_tensor("psk2", [128, 512], F32))
            pst0 = P.enter_context(nc.psum_tensor("pst0", [128, 512], F32))
            pst1 = P.enter_context(nc.psum_tensor("pst1", [128, 512], F32))
            wkq = [wkq0, wkq1]
            wvb = [wvb0, wvb1]
            sqb = [sq0, sq1]
            kob = [ko0, ko1]
            vob = [vo0, vo1]
            psk = [psk0, psk1, psk2]
            pst = [pst0, pst1]
            psw = [psw0, psw1]

            ph_in = ph.n

            jobs = []
            for H in (0, 1):
                for g in range(NGC):
                    for s4 in range(4):
                        jobs.append(("K", H, g, s4))
                for vc in range(2):
                    for tt in range(16):
                        jobs.append(("V", H, vc, tt))
                for h in range(NHC):
                    for s4 in range(4):
                        jobs.append(("Q", H, h, s4))
            jidx = {j: i for i, j in enumerate(jobs)}
            rjobs = [j for j in jobs if j[0] in ("K", "Q")]
            ridx = {j: i for i, j in enumerate(rjobs)}
            vjobs = [j for j in jobs if j[0] == "V"]
            vidx = {j: i for i, j in enumerate(vjobs)}

            kqloads = []
            vloads = []
            for H in (0, 1):
                for g in range(NGC):
                    kqloads.append(("K", H, g))
                for vc in range(2):
                    vloads.append(("V", H, vc))
                for h in range(NHC):
                    kqloads.append(("Q", H, h))
            kqli = {l: i for i, l in enumerate(kqloads)}
            vli = {l: i for i, l in enumerate(vloads)}

            def load_of(j):
                return (j[0], j[1], j[2])

            def last_job_of_load(l):
                kind, H, a = l
                return (kind, H, a, 3 if kind != "V" else 15)

            ldorder = []
            for H in (0, 1):
                for g in range(NGC):
                    ldorder.append(("K", H, g))
                for vc in range(2):
                    ldorder.append(("V", H, vc))
                for h in range(NHC):
                    ldorder.append(("Q", H, h))
            WEV = {}
            nkq = [dm[9].n, dm[10].n]
            nv = [dm[11].n, dm[12].n]
            for l in ldorder:
                if l[0] == "V":
                    i = vli[l]
                    nv[i % 2] += 16
                    WEV[l] = (dm[11 + i % 2], nv[i % 2])
                else:
                    i = kqli[l]
                    nkq[i % 2] += 16
                    WEV[l] = (dm[9 + i % 2], nkq[i % 2])
            CSL = (dm[5], dm[5].n + 48)
            XEV = {}
            OEV = {}
            nxt = dm[13].n
            nko = [dm[14].n, dm[15].n]
            nvo = [dm[16].n, dm[17].n]
            for H in (0, 1):
                nxt += 32 * 16
                XEV[H] = (dm[13], nxt)
                for j in [jj for jj in jobs if jj[1] == H]:
                    if j[0] == "V":
                        iv = vidx[j]
                        nvo[iv % 2] += 16
                        OEV[j] = (dm[16 + iv % 2], nvo[iv % 2])
                    else:
                        ir = ridx[j]
                        nko[ir % 2] += 16
                        OEV[j] = (dm[14 + ir % 2], nko[ir % 2])
            MEV = {}
            SEV1 = {}
            SWP1 = {}
            n = c1.n
            pendq = []
            for j in jobs:
                while pendq and (len(pendq) >= 2 or j[0] == "V"
                                 or j == ("K", 1, 0, 0)):
                    jp = pendq.pop(0)
                    n += 1
                    SWP1[jp] = n
                    n += 1
                    SEV1[jp] = n
                n += 1
                MEV[j] = n
                if j[0] in ("K", "Q"):
                    pendq.append(j)
            for jp in pendq:
                n += 1
                SWP1[jp] = n
                n += 1
                SEV1[jp] = n
            QEV = {}
            QE2 = {}
            n = c2.n
            for j in jobs:
                n += 1
                QEV[j] = n
                if j[0] in ("K", "Q"):
                    n += 1
                    QE2[j] = n
            REV = {}
            n = c3.n
            for j in rjobs:
                n += 1
                REV[j] = n
            PEV = {}
            n = c6.n
            for j in rjobs:
                n += 1
                PEV[j] = n

            def free_ev(j):
                if j[0] == "V":
                    return (c2, QEV[j])
                return (c6, PEV[j])

            def job_pos(j):
                kind, H, a, b = j
                if kind == "V":
                    return None
                return (H * TH + b * 512)

            with nc.Block() as blk:
                @blk.gpsimd
                def _(g):
                    g.wait_ge(ph.h, ph_in)
                    for l in ldorder:
                        kind, H, a = l
                        if kind == "V":
                            i = vli[l]
                            if i >= 2:
                                pl = vloads[i - 2]
                                g.wait_ge(c1.h, MEV[last_job_of_load(pl)])
                            srcap = wv.ap()[:, a * 256:(a + 1) * 256]
                            WEV[l][0].inc(g.dma_start(
                                sb3(wvb[i % 2][:, :], 32), dr3(srcap)), 16)
                        else:
                            i = kqli[l]
                            if i >= 2:
                                pl = kqloads[i - 2]
                                g.wait_ge(c1.h, MEV[last_job_of_load(pl)])
                            wsrc = wk if kind == "K" else wq
                            srcap = wsrc.ap()[:, a * 128:(a + 1) * 128]
                            WEV[l][0].inc(g.dma_start(
                                sb3(wkq[i % 2][:, :], 32), dr3(srcap)), 16)
                    for i_ in (9, 10, 11, 12):
                        g.wait_ge(dm[i_].h, dm[i_].n)
                    ph.inc(g.nop())

                @blk.sync
                def _(sy):
                    sy.wait_ge(ph.h, ph_in)
                    dm[5].inc(sy.dma_start(csa_sb[:, :], csa.ap()), 16)
                    dm[5].inc(sy.dma_start(csb_sb[:, :], csb.ap()), 16)
                    dm[5].inc(sy.dma_start(swp_sb[:, :], swp.ap()), 16)
                    for H in (0, 1):
                        if H == 1:
                            sy.wait_ge(c1.h, MEV[("Q", 0, NHC - 1, 3)])
                        wt(sy, GEV["xbf1" if H else "xbf0"])
                        for dc in range(32):
                            dm[13].inc(sy.dma_start(
                                xT_sb[:, dc * TH:(dc + 1) * TH],
                                xbf.ap()[H * TH:(H + 1) * TH,
                                         dc * 128:(dc + 1) * 128],
                                transpose=True), 16)
                        for j in [jj for jj in jobs if jj[1] == H]:
                            kind, _, a, b = j
                            if kind == "V":
                                iv = vidx[j]
                                sy.wait_ge(c2.h, QEV[j])
                                gt = H * TH + b * 128
                                OEV[j][0].inc(sy.dma_start(
                                    vS.ap()[gt:gt + 128,
                                            a * 256:(a + 1) * 256],
                                    vob[iv % 2][:, :]), 16)
                            else:
                                ir = ridx[j]
                                sy.wait_ge(c6.h, PEV[j])
                                ts = H * TH + b * 512
                                dst = kT if kind == "K" else qT
                                OEV[j][0].inc(sy.dma_start(
                                    dst.ap()[a * 128:(a + 1) * 128,
                                             ts:ts + 512],
                                    kob[ir % 2][:, :]), 16)
                    for i_ in (5, 13, 14, 15, 16, 17):
                        sy.wait_ge(dm[i_].h, dm[i_].n)
                    ph.inc(sy.nop())

                @blk.tensor
                def _(t):
                    t.wait_ge(ph.h, ph_in)
                    pendq = []

                    def flush_stats(t, force=False):
                        if not pendq or (not force and len(pendq) < 2):
                            return
                        jp = pendq.pop(0)
                        ip = ridx[jp]
                        wt(t, CSL)
                        t.wait_ge(c2.h, QE2[jp])
                        if ip >= 2:
                            t.wait_ge(c6.h, PEV[rjobs[ip - 2]])
                        c1.inc(t.matmul(psw[ip % 2][:, :], swp_sb[:, :],
                                        qeb[ip % 2][:, :],
                                        start=True, stop=True))
                        if ip >= 2:
                            t.wait_ge(c3.h, REV[rjobs[ip - 2]])
                        c1.inc(t.matmul(pst[ip % 2][:, :], ones_sb[:, :],
                                        sqb[ip % 2][:, :],
                                        start=True, stop=True))

                    for j in jobs:
                        kind, H, a, b = j
                        i = jidx[j]
                        drain = (kind == "V") or j == ("K", 1, 0, 0)
                        flush_stats(t, force=drain)
                        if drain:
                            flush_stats(t, force=True)
                        if (kind, b) in (("K", 0), ("V", 0), ("Q", 0)) and \
                                a == 0:
                            wt(t, XEV[H])
                        if b == 0:
                            wt(t, WEV[(kind, H, a)])
                        if i >= 3:
                            sem, val = free_ev(jobs[i - 3])
                            t.wait_ge(sem.h, val)
                        pg = psk[i % 3]
                        if kind == "V":
                            ld = vli[(kind, H, a)]
                            wb = wvb[ld % 2]
                            for dc in range(32):
                                mm = t.matmul(
                                    pg[:, 0:256],
                                    xT_sb[:, dc * TH + b * 128:
                                          dc * TH + b * 128 + 128],
                                    wb[:, dc * 256:(dc + 1) * 256],
                                    start=(dc == 0), stop=(dc == 31))
                            c1.inc(mm)
                        else:
                            ld = kqli[(kind, H, a)]
                            wb = wkq[ld % 2]
                            for dc in range(32):
                                mm = t.matmul(
                                    pg[:, :],
                                    wb[:, dc * 128:(dc + 1) * 128],
                                    xT_sb[:, dc * TH + b * 512:
                                          dc * TH + b * 512 + 512],
                                    start=(dc == 0), stop=(dc == 31))
                            c1.inc(mm)
                        if kind in ("K", "Q"):
                            pendq.append(j)
                    while pendq:
                        flush_stats(t, force=True)
                    endsec(t, c1)

                @blk.scalar
                def _(sc):
                    sc.wait_ge(ph.h, ph_in)
                    for j in jobs:
                        kind, H, a, b = j
                        i = jidx[j]
                        sc.wait_ge(c1.h, MEV[j])
                        pg = psk[i % 3]
                        if kind == "V":
                            iv = vidx[j]
                            if iv >= 2:
                                wt(sc, OEV[vjobs[iv - 2]])
                            c2.inc(sc.copy(vob[iv % 2][:, :], pg[:, 0:256]))
                        else:
                            ir = ridx[j]
                            if ir >= 2:
                                sc.wait_ge(c1.h, SEV1[rjobs[ir - 2]])
                            c2.inc(sc.activation(sqb[ir % 2][:, :], pg[:, :],
                                                 AF.Square))
                            if ir >= 2:
                                sc.wait_ge(c1.h, SWP1[rjobs[ir - 2]])
                            c2.inc(sc.copy(qeb[ir % 2][:, :], pg[:, :]))
                    endsec(sc, c2)

                @blk.vector
                def _(v):
                    v.wait_ge(ph.h, ph_in)
                    for j in rjobs:
                        kind, H, a, b = j
                        ir = ridx[j]
                        i = jidx[j]
                        pg = psk[i % 3]
                        v.wait_ge(c1.h, SEV1[j])
                        if kind == "Q":
                            col = a * 8 + H * 4 + b
                            c3.inc(v.reduce_max(qmax2_sb[:, col:col + 1],
                                                pst[ir % 2][:, :], axis=AX.X))
                        else:
                            strip = H * 4 + b
                            if strip == 0:
                                c3.inc(v.reduce_max(
                                    kmax2_sb[:, a:a + 1],
                                    pst[ir % 2][:, :], axis=AX.X))
                            else:
                                vch.inc(v.reduce_max(
                                    stat_sb[:, ir % 2:ir % 2 + 1],
                                    pst[ir % 2][:, :], axis=AX.X))
                                v.wait_ge(vch.h, vch.n)
                                c3.inc(v.tensor_max(
                                    kmax2_sb[:, a:a + 1],
                                    kmax2_sb[:, a:a + 1],
                                    stat_sb[:, ir % 2:ir % 2 + 1]))
                        wt(v, CSL)
                        v.wait_ge(c2.h, QEV[j])
                        if ir >= 2:
                            wt(v, OEV[rjobs[ir - 2]])
                        ts = H * TH + b * 512
                        r = rp0
                        ko = kob[ir % 2]
                        v.wait_ge(vch.h, vch.n)
                        v.wait_ge(c6.h, c6.n)
                        v.tensor_mul(r[:, 0:512], pg[:, :],
                                     csa_sb[:, ts:ts + 512])
                        vch.inc(v.tensor_mul(r[:, 512:1024], psw[ir % 2][:, :],
                                             csb_sb[:, ts:ts + 512]))
                        v.wait_ge(vch.h, vch.n)
                        c6.inc(v.tensor_add(ko[:, :], r[:, 0:512],
                                            r[:, 512:1024]))
                    v.wait_ge(vch.h, vch.n)
                    for gk in range(NGC):
                        mmv = v.tensor_scalar_mul(
                            kmc_sb[:, gk * 32:(gk + 1) * 32],
                            onesf_sb[:, 0:32], kmax2_sb[:, gk:gk + 1])
                    vch.inc(mmv)
                    v.wait_ge(vch.h, vch.n)
                    c6.inc(v.tensor_mul(kmc_sb[:, :], kmc_sb[:, :],
                                        qmax2_sb[:, :]))
                    endsec(v, c6)

            assert c2.n == QE2[jobs[-1]]
            assert ph.n == ph_in + 5

            ph_in2 = ph.n
            with nc.Block() as blk:
                @blk.scalar
                def _(sc):
                    sc.wait_ge(ph.h, ph_in2)
                    c2.inc(sc.activation(mneg_sb[:, :], kmc_sb[:, :],
                                         AF.Sqrt))
                    endsec(sc, c2)

                @blk.vector
                def _(v):
                    v.wait_ge(c2.h, c2.n)
                    c6.inc(v.tensor_scalar_mul(mneg_sb[:, :], mneg_sb[:, :],
                                               -SCALE))
                    endsec(v, c6)
            ph_p1 = ph.n

        if _KPH <= 1:
            _fin()
            return nc

        with ExitStack() as P:
            maskp_sb = P.enter_context(nc.sbuf_tensor("maskp_sb", [128, 2048], BF16))
            qs0 = P.enter_context(nc.sbuf_tensor("qs0", [128, NHC * 512], BF16))
            qs1 = P.enter_context(nc.sbuf_tensor("qs1", [128, NHC * 512], BF16))
            ks0 = P.enter_context(nc.sbuf_tensor("ks0", [128, T], BF16))
            ks1 = P.enter_context(nc.sbuf_tensor("ks1", [128, T], BF16))
            vs0 = P.enter_context(nc.sbuf_tensor("vs0", [128, T], BF16))
            vs1 = P.enter_context(nc.sbuf_tensor("vs1", [128, T], BF16))
            pb0 = P.enter_context(nc.sbuf_tensor("pb0", [128, 1024], BF16))
            pb1 = P.enter_context(nc.sbuf_tensor("pb1", [128, 1024], BF16))
            rln_sb = P.enter_context(nc.sbuf_tensor("rln_sb", [128, 512], F32))
            ri0 = P.enter_context(nc.sbuf_tensor("ri0", [128, 512], F32))
            ri1 = P.enter_context(nc.sbuf_tensor("ri1", [128, 512], F32))
            cx0 = P.enter_context(nc.sbuf_tensor("cx0", [128, 512], BF16))
            cx1 = P.enter_context(nc.sbuf_tensor("cx1", [128, 512], BF16))
            psS0 = P.enter_context(nc.psum_tensor("psS0", [128, 1024], F32))
            psS1 = P.enter_context(nc.psum_tensor("psS1", [128, 1024], F32))
            psc0 = P.enter_context(nc.psum_tensor("psc0", [128, 512], F32))
            psc1 = P.enter_context(nc.psum_tensor("psc1", [128, 512], F32))
            psr0 = P.enter_context(nc.psum_tensor("psr0", [128, 512], F32))
            psr1 = P.enter_context(nc.psum_tensor("psr1", [128, 512], F32))
            qsb = [qs0, qs1]
            ksb = [ks0, ks1]
            vsb = [vs0, vs1]
            pbb = [pb0, pb1]
            rib = [ri0, ri1]
            cxb = [cx0, cx1]
            psS = [psS0, psS1]
            psc = [psc0, psc1]
            psr = [psr0, psr1]

            ph_in = ph.n

            groups = [(st, g) for st in range(8) for g in range(NGC)][:_KP3G]
            giof = {sg: i for i, sg in enumerate(groups)}
            heads = []
            for (st, g) in groups:
                for hh in range(4):
                    heads.append((st, g, g * 4 + hh))
            pairs = []
            hinfo = []
            for m, (st, g, h) in enumerate(heads):
                npz = 2 * (st + 1)
                p0i = len(pairs)
                for pi in range(npz):
                    pairs.append(dict(
                        m=m, st=st, g=g, h=h, pi=pi, c0=2 * pi,
                        masked=(pi >= npz - 2), j2=pi - (npz - 2),
                        first=(pi == 0), last=(pi == npz - 1)))
                hinfo.append(dict(st=st, g=g, h=h, pfirst=p0i,
                                  plast=len(pairs) - 1))
            NP = len(pairs)

            MKL = (dm[5], dm[5].n + 16)
            QL = {}
            KL = {}
            VL = {}
            CST = {}
            nq = [dm[18].n, dm[19].n]
            nk = [dm[6].n, dm[7].n]
            nvv = [dm[16].n, dm[17].n]
            nct = [dm[14].n, dm[15].n]

            def _ql(st):
                nq[st % 2] += 16
                QL[st] = (dm[18 + st % 2], nq[st % 2])

            def _kvl(gi):
                nk[gi % 2] += 16
                KL[gi] = (dm[6 + gi % 2], nk[gi % 2])
                nvv[gi % 2] += 16
                VL[gi] = (dm[16 + gi % 2], nvv[gi % 2])

            _ql(0)
            _kvl(0)
            for gi, (st, g) in enumerate(groups):
                if gi + 1 < len(groups):
                    _kvl(gi + 1)
                if g == NGC - 1 and st + 1 < 8:
                    _ql(st + 1)
                for hh in range(4):
                    m = gi * 4 + hh
                    nct[m % 2] += 16
                    CST[m] = (dm[14 + m % 2], nct[m % 2])
            s1b, s4b = c1.n, c4.n
            SC = {p: s1b + p + 1 for p in range(NP)}
            CT = {p: s4b + p + 1 for p in range(NP)}
            s2b, s5b = c2.n, c5.n
            EX = {p: s2b + p + 1 for p in range(NP)}
            RI = {m: s5b + m + 1 for m in range(len(heads))}
            s3b, s6b = c3.n, c6.n
            MK = {}
            nmk = s3b
            for p, pr in enumerate(pairs):
                if pr["masked"] and not _KNOMASK:
                    nmk += 1
                    MK[p] = nmk
            EVC = {m: s6b + m + 1 for m in range(len(heads))}

            def span_of(st):
                return (st + 1) * 512

            with nc.Block() as blk:
                @blk.sync
                def _(sy):
                    sy.wait_ge(ph.h, ph_in)
                    dm[5].inc(sy.dma_start(maskp_sb[:, :], maskp.ap()), 16)
                    QL[0][0].inc(sy.dma_start(
                        sb3(qsb[0][:, :], NHC),
                        dr3(qT.ap()[:, 0:512])), 16)
                    KL[0][0].inc(sy.dma_start(ksb[0][:, 0:512],
                                              kT.ap()[0:128, 0:512]), 16)
                    VL[0][0].inc(sy.dma_start(
                        sb3(vsb[0][:, 0:512], 4),
                        vS.ap()[0:512, 0:128].rearrange(
                            "(c p) w -> p c w", p=128)), 16)
                    for gi, (st, g) in enumerate(groups):
                        if gi + 1 < len(groups):
                            st2, g2 = groups[gi + 1]
                            sp2 = span_of(st2)
                            if gi >= 1:
                                sy.wait_ge(c4.h, CT[hinfo[gi * 4 - 1]["plast"]])
                            KL[gi + 1][0].inc(sy.dma_start(
                                ksb[(gi + 1) % 2][:, 0:sp2],
                                kT.ap()[g2 * 128:(g2 + 1) * 128, 0:sp2]), 16)
                            VL[gi + 1][0].inc(sy.dma_start(
                                sb3(vsb[(gi + 1) % 2][:, 0:sp2], sp2 // 128),
                                vS.ap()[0:sp2, g2 * 128:(g2 + 1) * 128]
                                .rearrange("(c p) w -> p c w", p=128)), 16)
                        if g == NGC - 1 and st + 1 < 8:
                            if st >= 1:
                                last_m_prev_st = (giof[(st - 1, NGC - 1)]) * 4 + 3
                                sy.wait_ge(c4.h, CT[hinfo[last_m_prev_st]["plast"]])
                            QL[st + 1][0].inc(sy.dma_start(
                                sb3(qsb[(st + 1) % 2][:, :], NHC),
                                dr3(qT.ap()[:, (st + 1) * 512:(st + 2) * 512])),
                                16)
                        for hh in range(4):
                            m = gi * 4 + hh
                            st_, _, h_ = heads[m]
                            sy.wait_ge(c6.h, EVC[m])
                            dst = ctx_lo if st_ < 4 else ctx_hi
                            cofs = (st_ % 4) * 512
                            CST[m][0].inc(sy.dma_start(
                                dst.ap()[h_ * 128:(h_ + 1) * 128,
                                         cofs:cofs + 512],
                                cxb[m % 2][:, :]), 16)
                    for i_ in (5, 6, 7, 14, 15, 16, 17, 18, 19):
                        sy.wait_ge(dm[i_].h, dm[i_].n)
                    ph.inc(sy.nop())

                @blk.tensor
                def _(t):
                    t.wait_ge(ph.h, ph_in)

                    def consume(q):
                        pq = pairs[q]
                        mq = pq["m"]
                        gi = giof[(pq["st"], pq["g"])]
                        if pq["masked"] and not _KNOMASK:
                            t.wait_ge(c3.h, MK[q])
                        else:
                            t.wait_ge(c2.h, EX[q])
                        for half in (0, 1):
                            c = pq["c0"] + half
                            stt = pq["first"] and half == 0
                            spp = pq["last"] and half == 1
                            mm = t.matmul(
                                psc[mq % 2][:, :],
                                vsb[gi % 2][:, c * 128:(c + 1) * 128],
                                pbb[q % 2][:, half * 512:(half + 1) * 512],
                                start=stt, stop=spp)
                            if not _KNOROW:
                                mm = t.matmul(
                                    psr[mq % 2][:, :], ones_sb[:, :],
                                    pbb[q % 2][:, half * 512:(half + 1) * 512],
                                    start=stt, stop=spp)
                        c4.inc(mm)

                    for p, pr in enumerate(pairs):
                        m = pr["m"]
                        gi = giof[(pr["st"], pr["g"])]
                        if pr["first"]:
                            if pr["h"] == pr["g"] * 4:
                                wt(t, KL[gi])
                                wt(t, VL[gi])
                                if pr["g"] == 0:
                                    wt(t, QL[pr["st"]])
                            if m >= 2:
                                t.wait_ge(c6.h, EVC[m - 2])
                                t.wait_ge(c5.h, RI[m - 2])
                        for half in (0, 1):
                            c = pr["c0"] + half
                            mm = t.matmul(
                                psS[p % 2][:, half * 512:(half + 1) * 512],
                                ksb[gi % 2][:, c * 128:(c + 1) * 128],
                                qsb[pr["st"] % 2][:, pr["h"] * 512:
                                                  (pr["h"] + 1) * 512],
                                start=True, stop=True)
                        c1.inc(mm)
                        if p >= 1:
                            consume(p - 1)
                    consume(NP - 1)
                    endsec(t, c4)

                @blk.scalar
                def _(sc):
                    sc.wait_ge(ph.h, ph_in)
                    pend_epi = None
                    for p, pr in enumerate(pairs):
                        m = pr["m"]
                        col = pr["h"] * 8 + pr["st"]
                        sc.wait_ge(c1.h, SC[p])
                        if p >= 2:
                            sc.wait_ge(c4.h, CT[p - 2])
                        if _KNOEXP2:
                            sc.activation(pbb[p % 2][:, 0:512],
                                          psS[p % 2][:, 0:512], AF.Exp,
                                          bias=mneg_sb[:, col:col + 1],
                                          scale=SCALE)
                            c2.inc(sc.activation(pbb[p % 2][:, 512:1024],
                                                 psS[p % 2][:, 512:1024],
                                                 AF.Exp,
                                                 bias=mneg_sb[:, col:col + 1],
                                                 scale=SCALE))
                        else:
                            c2.inc(sc.activation(pbb[p % 2][:, :],
                                                 psS[p % 2][:, :], AF.Exp,
                                                 bias=mneg_sb[:, col:col + 1],
                                                 scale=SCALE))
                        if pend_epi is not None:
                            me, pe = pend_epi
                            pend_epi = None
                            sc.wait_ge(c4.h, CT[pe])
                            if me >= 2:
                                sc.wait_ge(c6.h, EVC[me - 2])
                                sc.wait_ge(c5.h, RI[me - 2])
                            if _KNOEPI:
                                c5.inc(sc.memzero(rib[me % 2][:, :]))
                            else:
                                sch.inc(sc.activation(rln_sb[:, :],
                                                      psr[me % 2][:, :],
                                                      AF.Ln))
                                sc.wait_ge(sch.h, sch.n)
                                c5.inc(sc.activation(
                                    rib[me % 2][:, :], rln_sb[:, :],
                                    AF.Exp, scale=-1.0))
                        if pr["last"]:
                            pend_epi = (m, p)
                    me, pe = pend_epi
                    sc.wait_ge(c4.h, CT[pe])
                    if me >= 2:
                        sc.wait_ge(c6.h, EVC[me - 2])
                        sc.wait_ge(c5.h, RI[me - 2])
                    if _KNOEPI:
                        c5.inc(sc.memzero(rib[me % 2][:, :]))
                    else:
                        sch.inc(sc.activation(rln_sb[:, :],
                                              psr[me % 2][:, :], AF.Ln))
                        sc.wait_ge(sch.h, sch.n)
                        c5.inc(sc.activation(rib[me % 2][:, :], rln_sb[:, :],
                                             AF.Exp, scale=-1.0))
                    endsec(sc, c5)

                @blk.vector
                def _(v):
                    v.wait_ge(ph.h, ph_in)
                    wt(v, MKL)
                    for p, pr in enumerate(pairs):
                        if pr["masked"] and not _KNOMASK:
                            v.wait_ge(c2.h, EX[p])
                            j2 = pr["j2"]
                            c3.inc(v.tensor_mul(
                                pbb[p % 2][:, :], pbb[p % 2][:, :],
                                maskp_sb[:, j2 * 1024:(j2 + 1) * 1024]))
                        if pr["last"]:
                            m = pr["m"]
                            v.wait_ge(c5.h, RI[m])
                            if m >= 2:
                                wt(v, CST[m - 2])
                            c6.inc(v.tensor_mul(cxb[m % 2][:, :],
                                                psc[m % 2][:, :],
                                                rib[m % 2][:, :]))
                    endsec(v, c6)

            assert c1.n == SC[NP - 1]
            assert c4.n == CT[NP - 1]
            assert c2.n == EX[NP - 1]
            assert ph.n == ph_in + 4

        if _KPH <= 3:
            _fin()
            return nc

        ph_in = ph.n
        with nc.Block() as blk:
            @blk.gpsimd
            def _(g):
                g.wait_ge(ph.h, ph_in)
                cc1.inc(g.collective_compute(
                    "AllGather", ALU.bypass,
                    replica_groups=[[0, 1], [2, 3], [4, 5], [6, 7]],
                    ins=[ctx_lo.ap().opt()],
                    outs=[ctxAG.ap()[0:D, :].opt()]))
                cc2.inc(g.collective_compute(
                    "AllGather", ALU.bypass,
                    replica_groups=[[0, 1], [2, 3], [4, 5], [6, 7]],
                    ins=[ctx_hi.ap().opt()],
                    outs=[ctxAG.ap()[D:2 * D, :].opt()]))
                g.wait_ge(cc1.h, 1)
                g.wait_ge(cc2.h, 1)
                wt(g, GEV["consts"])
                with g.register("ro") as ro:
                    g.reg_load(ro, agoff_sb[0:1, 0:1])
                    dm[20].inc(g.dma_start(
                        ctxF.ap(),
                        bass.AP(ctxAG, ro, [[TH, D], [1, TH]])), 16)
                g.wait_ge(dm[20].h, dm[20].n)
                ph.inc(g.nop())
        ph_in = ph.n

        if _KPH <= 4:
            _fin()
            return nc

        with ExitStack() as P:
            cxs = P.enter_context(nc.sbuf_tensor("cxs", [128, 32 * 1024], BF16))
            woc0 = P.enter_context(nc.sbuf_tensor("woc0", [128, 32 * 256], BF16))
            woc1 = P.enter_context(nc.sbuf_tensor("woc1", [128, 32 * 256], BF16))
            xhb0 = P.enter_context(nc.sbuf_tensor("xhb0", [128, 8 * 256], F32))
            xhb1 = P.enter_context(nc.sbuf_tensor("xhb1", [128, 8 * 256], F32))
            stg0 = P.enter_context(nc.sbuf_tensor("stg0", [128, 8 * 256], F32))
            stg1 = P.enter_context(nc.sbuf_tensor("stg1", [128, 8 * 256], F32))
            trash_sb = P.enter_context(nc.sbuf_tensor("trash_sb", [128, 256], F32))
            pso0 = P.enter_context(nc.psum_tensor("pso0", [128, 256], F32))
            pso1 = P.enter_context(nc.psum_tensor("pso1", [128, 256], F32))
            pso2 = P.enter_context(nc.psum_tensor("pso2", [128, 256], F32))
            woc = [woc0, woc1]
            xhb = [xhb0, xhb1]
            stg = [stg0, stg1]
            pso = [pso0, pso1, pso2]

            ph_in = ph.n
            chunks = [(p_, oc) for p_ in (0, 1) for oc in range(16)]
            jobs4 = [(p_, oc, tt) for (p_, oc) in chunks for tt in range(8)]
            j4 = {jj: i for i, jj in enumerate(jobs4)}
            ci_of = {ch: i for i, ch in enumerate(chunks)}

            WOE = {}
            nwo = [dm[9].n, dm[10].n]
            for ch in chunks:
                ci = ci_of[ch]
                nwo[ci % 2] += 16
                WOE[ch] = (dm[9 + ci % 2], nwo[ci % 2])
            CXE = {}
            XHE = {}
            YPE = {}
            ncx = dm[13].n
            nxh = [dm[16].n, dm[17].n]
            nyp = [dm[14].n, dm[15].n]
            for p_ in (0, 1):
                ncx += 16
                CXE[p_] = (dm[13], ncx)
                for oc in range(16):
                    ci = ci_of[(p_, oc)]
                    nxh[ci % 2] += 16
                    XHE[(p_, oc)] = (dm[16 + ci % 2], nxh[ci % 2])
            for ch in chunks:
                ci = ci_of[ch]
                nyp[ci % 2] += 16
                YPE[ch] = (dm[14 + ci % 2], nyp[ci % 2])
            MM4 = {}
            n = c1.n
            for jj in jobs4:
                n += 1
                MM4[jj] = n
            EV4 = {}
            n = c6.n
            for jj in jobs4:
                n += 1
                EV4[jj] = n
            SQ4 = {}
            n = c2.n
            for jj in jobs4:
                n += 1
                SQ4[jj] = n

            with nc.Block() as blk:
                @blk.gpsimd
                def _(g):
                    g.wait_ge(ph.h, ph_in)
                    for ch in chunks:
                        p_, oc = ch
                        ci = ci_of[ch]
                        if ci >= 2:
                            g.wait_ge(c1.h, MM4[chunks[ci - 2] + (7,)])
                        WOE[ch][0].inc(g.dma_start(
                            sb3(woc[ci % 2][:, :], 32),
                            dr3(wo.ap()[:, oc * 256:(oc + 1) * 256])), 16)
                    for i_ in (9, 10):
                        g.wait_ge(dm[i_].h, dm[i_].n)
                    ph.inc(g.nop())

                @blk.sync
                def _(sy):
                    sy.wait_ge(ph.h, ph_in)
                    for p_ in (0, 1):
                        if p_ == 1:
                            sy.wait_ge(c1.h, MM4[(0, 15, 7)])
                        dm[13].inc(sy.dma_start(
                            sb3(cxs[:, :], 32),
                            dr3(ctxF.ap()[:, p_ * 1024:(p_ + 1) * 1024])), 16)
                        for oc in range(16):
                            ci = ci_of[(p_, oc)]
                            if ci >= 2:
                                sy.wait_ge(c6.h, EV4[chunks[ci - 2] + (7,)])
                            XHE[(p_, oc)][0].inc(sy.dma_start(
                                sb3(xhb[ci % 2][:, :], 8),
                                dr3(xh.ap()[p_ * 1024:(p_ + 1) * 1024,
                                            oc * 256:(oc + 1) * 256])), 16)
                            if ci >= 1:
                                pch = chunks[ci - 1]
                                sy.wait_ge(c6.h, EV4[pch + (7,)])
                                YPE[pch][0].inc(sy.dma_start(
                                    dr3(ypre.ap()[pch[0] * 1024:
                                                  pch[0] * 1024 + 1024,
                                                  pch[1] * 256:
                                                  pch[1] * 256 + 256]),
                                    sb3(stg[(ci - 1) % 2][:, :], 8)), 16)
                    pch = chunks[-1]
                    sy.wait_ge(c6.h, EV4[pch + (7,)])
                    YPE[pch][0].inc(sy.dma_start(
                        dr3(ypre.ap()[pch[0] * 1024:pch[0] * 1024 + 1024,
                                      pch[1] * 256:pch[1] * 256 + 256]),
                        sb3(stg[31 % 2][:, :], 8)), 16)
                    for i_ in (13, 14, 15, 16, 17):
                        sy.wait_ge(dm[i_].h, dm[i_].n)
                    ph.inc(sy.nop())

                @blk.tensor
                def _(t):
                    t.wait_ge(ph.h, ph_in)
                    for jj in jobs4:
                        p_, oc, tt = jj
                        ci = ci_of[(p_, oc)]
                        i = j4[jj]
                        if tt == 0:
                            wt(t, CXE[p_])
                            wt(t, WOE[(p_, oc)])
                        if i >= 3:
                            t.wait_ge(c6.h, EV4[jobs4[i - 3]])
                        for dc in range(32):
                            mm = t.matmul(
                                pso[i % 3][:, :],
                                cxs[:, dc * 1024 + tt * 128:
                                    dc * 1024 + tt * 128 + 128],
                                woc[ci % 2][:, dc * 256:(dc + 1) * 256],
                                start=(dc == 0), stop=(dc == 31))
                        c1.inc(mm)
                    endsec(t, c1)

                @blk.vector
                def _(v):
                    v.wait_ge(ph.h, ph_in)
                    for jj in jobs4:
                        p_, oc, tt = jj
                        ci = ci_of[(p_, oc)]
                        i = j4[jj]
                        v.wait_ge(c1.h, MM4[jj])
                        if tt == 0:
                            wt(v, XHE[(p_, oc)])
                            if ci >= 2:
                                wt(v, YPE[chunks[ci - 2]])
                                v.wait_ge(c2.h, SQ4[chunks[ci - 2] + (7,)])
                        c6.inc(v.scalar_tensor_tensor(
                            stg[ci % 2][:, tt * 256:(tt + 1) * 256],
                            pso[i % 3][:, :], 1.0,
                            xhb[ci % 2][:, tt * 256:(tt + 1) * 256],
                            ALU.mult, ALU.add,
                            accum_out=stat_sb[:, 2 + i % 2:3 + i % 2]))
                        col = p_ * 8 + tt
                        v.wait_ge(c6.h, EV4[jj])
                        v.wait_ge(vch.h, vch.n)
                        if oc == 0:
                            vch.inc(v.tensor_copy(
                                ysum_sb[:, col:col + 1],
                                stat_sb[:, 2 + i % 2:3 + i % 2]))
                        else:
                            vch.inc(v.tensor_add(
                                ysum_sb[:, col:col + 1],
                                ysum_sb[:, col:col + 1],
                                stat_sb[:, 2 + i % 2:3 + i % 2]))
                        if i >= 1:
                            pj = jobs4[i - 1]
                            pcol = pj[0] * 8 + pj[2]
                            v.wait_ge(c2.h, SQ4[pj])
                            v.wait_ge(vch.h, vch.n)
                            if pj[1] == 0:
                                vch.inc(v.tensor_copy(
                                    ysq_sb[:, pcol:pcol + 1],
                                    stat_sb[:, 4 + (i - 1) % 2:
                                            5 + (i - 1) % 2]))
                            else:
                                vch.inc(v.tensor_add(
                                    ysq_sb[:, pcol:pcol + 1],
                                    ysq_sb[:, pcol:pcol + 1],
                                    stat_sb[:, 4 + (i - 1) % 2:
                                            5 + (i - 1) % 2]))
                    pj = jobs4[-1]
                    i = len(jobs4)
                    pcol = pj[0] * 8 + pj[2]
                    v.wait_ge(c2.h, SQ4[pj])
                    v.wait_ge(vch.h, vch.n)
                    vch.inc(v.tensor_add(
                        ysq_sb[:, pcol:pcol + 1],
                        ysq_sb[:, pcol:pcol + 1],
                        stat_sb[:, 4 + (i - 1) % 2:5 + (i - 1) % 2]))
                    endsec(v, c6)

                @blk.scalar
                def _(sc):
                    sc.wait_ge(ph.h, ph_in)
                    for jj in jobs4:
                        p_, oc, tt = jj
                        ci = ci_of[(p_, oc)]
                        i = j4[jj]
                        sc.wait_ge(c6.h, EV4[jj])
                        if i >= 1:
                            sc.wait_ge(c2.h, SQ4[jobs4[i - 1]])
                        c2.inc(sc.activation(
                            trash_sb[:, :],
                            stg[ci % 2][:, tt * 256:(tt + 1) * 256],
                            AF.Square,
                            accum_out=stat_sb[:, 4 + i % 2:5 + i % 2]))
                    endsec(sc, c2)

            assert c1.n == MM4[jobs4[-1]]
            assert ph.n == ph_in + 5

        if _KPH <= 5:
            _fin()
            return nc

        with ExitStack() as P:
            yp0 = P.enter_context(nc.sbuf_tensor("yp0", [128, D], F32))
            yp1 = P.enter_context(nc.sbuf_tensor("yp1", [128, D], F32))
            yo0 = P.enter_context(nc.sbuf_tensor("yo0", [128, D], F32))
            yo1 = P.enter_context(nc.sbuf_tensor("yo1", [128, D], F32))
            ypb = [yp0, yp1]
            yob = [yo0, yo1]

            ph_in = ph.n
            LY = {}
            SY = {}
            GY = {}
            nly = [dm[6].n, dm[7].n]
            nsy = [dm[14].n, dm[15].n]
            ngy = [dm[20].n, dm[21].n]
            for tt in range(16):
                nly[tt % 2] += 16
                LY[tt] = (dm[6 + tt % 2], nly[tt % 2])
                nsy[tt % 2] += 16
                SY[tt] = (dm[14 + tt % 2], nsy[tt % 2])
                ngy[tt % 2] += 16
                GY[tt] = (dm[20 + tt % 2], ngy[tt % 2])
            VA = {tt: c3.n + tt + 1 for tt in range(16)}
            RS = {tt: c2.n + tt + 1 for tt in range(16)}
            NM = {tt: c6.n + tt + 1 for tt in range(16)}

            with nc.Block() as blk:
                @blk.sync
                def _(sy):
                    sy.wait_ge(ph.h, ph_in)
                    for tt in range(16):
                        if tt >= 2:
                            sy.wait_ge(c6.h, NM[tt - 2])
                        LY[tt][0].inc(sy.dma_start(
                            ypb[tt % 2][:, :],
                            ypre.ap()[tt * 128:(tt + 1) * 128, :]), 16)
                        sy.wait_ge(c6.h, NM[tt])
                        SY[tt][0].inc(sy.dma_start(
                            y32.ap()[tt * 128:(tt + 1) * 128, :],
                            yob[tt % 2][:, :]), 16)
                    for i_ in (6, 7, 14, 15):
                        sy.wait_ge(dm[i_].h, dm[i_].n)
                    ph.inc(sy.nop())

                @blk.gpsimd
                def _(g):
                    g.wait_ge(ph.h, ph_in)
                    for tt in range(16):
                        g.wait_ge(c6.h, NM[tt])
                        GY[tt][0].inc(g.dma_start(
                            ybf.ap()[tt * 128:(tt + 1) * 128, :],
                            yob[tt % 2][:, :]), 16)
                    for i_ in (20, 21):
                        g.wait_ge(dm[i_].h, dm[i_].n)
                    ph.inc(g.nop())

                @blk.vector
                def _(v):
                    v.wait_ge(ph.h, ph_in)
                    for tt in range(16):
                        p = tt % 2
                        if tt >= 2:
                            v.wait_ge(c2.h, RS[tt - 2])
                            v.wait_ge(c6.h, NM[tt - 2])
                        v.wait_ge(vch.h, vch.n)
                        vch.inc(v.tensor_scalar_mul(
                            stat_sb[:, p:p + 1],
                            ysum_sb[:, tt:tt + 1], 1.0 / D))
                        v.wait_ge(vch.h, vch.n)
                        vch.inc(v.tensor_mul(stat_sb[:, 6 + p:7 + p],
                                             stat_sb[:, p:p + 1],
                                             stat_sb[:, p:p + 1]))
                        v.wait_ge(vch.h, vch.n)
                        c3.inc(v.scalar_tensor_tensor(
                            stat_sb[:, 2 + p:3 + p],
                            ysq_sb[:, tt:tt + 1], 1.0 / D,
                            stat_sb[:, 6 + p:7 + p],
                            ALU.mult, ALU.subtract))
                        if tt >= 1:
                            pp = (tt - 1) % 2
                            v.wait_ge(c2.h, RS[tt - 1])
                            if tt - 1 >= 2:
                                wt(v, SY[tt - 3])
                                wt(v, GY[tt - 3])
                            wt(v, LY[tt - 1])
                            c6.inc(v.tensor_scalar(
                                yob[pp][:, :], ypb[pp][:, :],
                                stat_sb[:, pp:pp + 1],
                                stat_sb[:, 4 + pp:5 + pp],
                                ALU.subtract, ALU.mult))
                    pp = 15 % 2
                    v.wait_ge(c2.h, RS[15])
                    wt(v, SY[13])
                    wt(v, GY[13])
                    wt(v, LY[15])
                    c6.inc(v.tensor_scalar(
                        yob[pp][:, :], ypb[pp][:, :],
                        stat_sb[:, pp:pp + 1],
                        stat_sb[:, 4 + pp:5 + pp],
                        ALU.subtract, ALU.mult))
                    endsec(v, c6)

                @blk.scalar
                def _(sc):
                    sc.wait_ge(ph.h, ph_in)
                    for tt in range(16):
                        p = tt % 2
                        sc.wait_ge(c3.h, VA[tt])
                        sch.inc(sc.activation(stat_sb[:, 6 + p:7 + p],
                                              stat_sb[:, 2 + p:3 + p], AF.Ln,
                                              bias=EPS))
                        sc.wait_ge(sch.h, sch.n)
                        c2.inc(sc.activation(stat_sb[:, 4 + p:5 + p],
                                             stat_sb[:, 6 + p:7 + p],
                                             AF.Exp, scale=-0.5))
                    endsec(sc, c2)

            assert ph.n == ph_in + 4

        if _KPH <= 6:
            _fin()
            return nc

        with ExitStack() as P6:
            xs_sb = P6.enter_context(nc.sbuf_tensor("xs_sb", [128, 8 * TH], BF16))
            mid1f = P6.enter_context(nc.sbuf_tensor("mid1f", [2, TH], BF16))
            mid3f = P6.enter_context(nc.sbuf_tensor("mid3f", [2, TH], BF16))
            mid2f = P6.enter_context(nc.sbuf_tensor("mid2f", [2, TH], BF16))

            with ExitStack() as P:
                yT0 = P.enter_context(nc.sbuf_tensor("yT0", [128, 32 * 512], BF16))
                yT1 = P.enter_context(nc.sbuf_tensor("yT1", [128, 32 * 512], BF16))
                xtmp = P.enter_context(nc.sbuf_tensor("xtmp", [128, 512], F32))
                pm10 = P.enter_context(nc.psum_tensor("pm10", [2, 512], F32))
                pm11 = P.enter_context(nc.psum_tensor("pm11", [2, 512], F32))
                pm30 = P.enter_context(nc.psum_tensor("pm30", [2, 512], F32))
                pm31 = P.enter_context(nc.psum_tensor("pm31", [2, 512], F32))
                yTb = [yT0, yT1]
                pm1 = [pm10, pm11]
                pm3 = [pm30, pm31]

                ph_in = ph.n
                YT = {}
                nyt = [dm[6].n, dm[7].n]
                for s_ in range(4):
                    nyt[s_ % 2] += 32 * 16
                    YT[s_] = (dm[6 + s_ % 2], nyt[s_ % 2])
                MID = {}
                n = c1.n
                for s_ in range(4):
                    for l_ in (0, 1):
                        n += 1
                        MID[(s_, l_)] = n
                MIDE = {}
                n = c2.n
                for s_ in range(4):
                    for l_ in (0, 1):
                        n += 1
                        MIDE[(s_, l_)] = n
                XSD = {s_: c6.n + s_ + 1 for s_ in range(4)}

                with nc.Block() as blk:
                    @blk.sync
                    def _(sy):
                        sy.wait_ge(ph.h, ph_in)
                        for s_ in range(4):
                            if s_ >= 2:
                                sy.wait_ge(c6.h, XSD[s_ - 2])
                                sy.wait_ge(c1.h, MID[(s_ - 2, 1)])
                            for dc in range(32):
                                dm[6 + s_ % 2].inc(sy.dma_start(
                                    yTb[s_ % 2][:, dc * 512:(dc + 1) * 512],
                                    ybf.ap()[s_ * 512:(s_ + 1) * 512,
                                             dc * 128:(dc + 1) * 128],
                                    transpose=True), 16)
                        for i_ in (6, 7):
                            sy.wait_ge(dm[i_].h, dm[i_].n)
                        ph.inc(sy.nop())

                    @blk.tensor
                    def _(t):
                        t.wait_ge(ph.h, ph_in)
                        for s_ in range(4):
                            wt(t, YT[s_])
                            for l_, (aT, pm) in enumerate(
                                    ((a1T_sb, pm1), (a3T_sb, pm3))):
                                if s_ >= 2:
                                    t.wait_ge(c2.h, MIDE[(s_ - 2, l_)])
                                for dc in range(32):
                                    mm = t.matmul(
                                        pm[s_ % 2][:, :],
                                        aT[:, dc * 2:dc * 2 + 2],
                                        yTb[s_ % 2][:, dc * 512:(dc + 1) * 512],
                                        start=(dc == 0), stop=(dc == 31))
                                c1.inc(mm)
                        endsec(t, c1)

                    @blk.scalar
                    def _(sc):
                        sc.wait_ge(ph.h, ph_in)
                        for s_ in range(4):
                            for l_, (mf, pm) in enumerate(
                                    ((mid1f, pm1), (mid3f, pm3))):
                                sc.wait_ge(c1.h, MID[(s_, l_)])
                                c2.inc(sc.mul(
                                    mf[0:2, s_ * 512:(s_ + 1) * 512],
                                    pm[s_ % 2][:, :], 0.5))
                        endsec(sc, c2)

                    @blk.vector
                    def _(v):
                        v.wait_ge(ph.h, ph_in)
                        for s_ in range(4):
                            wt(v, YT[s_])
                            v.wait_ge(c6.h, c6.n)
                            for j in range(8):
                                yb = yTb[s_ % 2]
                                v.wait_ge(vch.h, vch.n)
                                vch.inc(v.tensor_add(
                                    xtmp[:, :],
                                    yb[:, j * 512:(j + 1) * 512],
                                    yb[:, (8 + j) * 512:(9 + j) * 512]))
                                v.wait_ge(vch.h, vch.n)
                                vch.inc(v.tensor_add(
                                    xtmp[:, :], xtmp[:, :],
                                    yb[:, (16 + j) * 512:(17 + j) * 512]))
                                v.wait_ge(vch.h, vch.n)
                                mm = v.tensor_add(
                                    xs_sb[:, j * TH + s_ * 512:
                                          j * TH + s_ * 512 + 512],
                                    xtmp[:, :],
                                    yb[:, (24 + j) * 512:(25 + j) * 512])
                                if j < 7:
                                    vch.inc(mm)
                            c6.inc(mm)
                        endsec(v, c6)

                assert ph.n == ph_in + 4

            if _KPH <= 7:
                _fin()
                return nc

            with ExitStack() as P:
                b1T_s = P.enter_context(nc.sbuf_tensor("b1T_s", [2, HF], BF16))
                b3T_s = P.enter_context(nc.sbuf_tensor("b3T_s", [2, HF], BF16))
                w1c0 = P.enter_context(nc.sbuf_tensor("w1c0", [128, 8 * 128], BF16))
                w1c1 = P.enter_context(nc.sbuf_tensor("w1c1", [128, 8 * 128], BF16))
                w3c0 = P.enter_context(nc.sbuf_tensor("w3c0", [128, 8 * 128], BF16))
                w3c1 = P.enter_context(nc.sbuf_tensor("w3c1", [128, 8 * 128], BF16))
                u0 = P.enter_context(nc.sbuf_tensor("u0", [128, TH], BF16))
                u1 = P.enter_context(nc.sbuf_tensor("u1", [128, TH], BF16))
                sil0 = P.enter_context(nc.sbuf_tensor("sil0", [128, 512], F32))
                sil1 = P.enter_context(nc.sbuf_tensor("sil1", [128, 512], F32))
                ph10 = P.enter_context(nc.psum_tensor("ph10", [128, 512], F32))
                ph11 = P.enter_context(nc.psum_tensor("ph11", [128, 512], F32))
                ph30 = P.enter_context(nc.psum_tensor("ph30", [128, 512], F32))
                ph31 = P.enter_context(nc.psum_tensor("ph31", [128, 512], F32))
                pq0 = P.enter_context(nc.psum_tensor("pq0", [2, 512], F32))
                pq1 = P.enter_context(nc.psum_tensor("pq1", [2, 512], F32))
                pq2 = P.enter_context(nc.psum_tensor("pq2", [2, 512], F32))
                pq3 = P.enter_context(nc.psum_tensor("pq3", [2, 512], F32))
                w1cb = [w1c0, w1c1]
                w3cb = [w3c0, w3c1]
                ub = [u0, u1]
                silb = [sil0, sil1]
                ph1 = [ph10, ph11]
                ph3 = [ph30, ph31]
                pqm = [pq0, pq1, pq2, pq3]

                ph_in = ph.n
                js_of = lambda hc, s_: hc * 4 + s_
                BL = (dm[13], dm[13].n + 32)
                W1L = {}
                W3L = {}
                UST = {}
                nw1 = [dm[6].n, dm[7].n]
                nw3 = [dm[18].n, dm[19].n]
                nus = [dm[14].n, dm[15].n]
                for hc in range(86):
                    nw1[hc % 2] += 16
                    W1L[hc] = (dm[6 + hc % 2], nw1[hc % 2])
                    nw3[hc % 2] += 16
                    W3L[hc] = (dm[18 + hc % 2], nw3[hc % 2])
                for hc in range(86):
                    nus[hc % 2] += 16
                    UST[hc] = (dm[14 + hc % 2], nus[hc % 2])
                H1 = {}
                H3 = {}
                MID2 = {}
                n = c1.n
                pq6 = []
                for hc in range(86):
                    for s_ in range(4):
                        if len(pq6) >= 2:
                            jp6 = pq6.pop(0)
                            n += 1
                            MID2[jp6] = n
                        n += 1
                        H1[(hc, s_)] = n
                        n += 1
                        H3[(hc, s_)] = n
                        pq6.append((hc, s_))
                for jp6 in pq6:
                    n += 1
                    MID2[jp6] = n
                SIL = {}
                n = c2.n
                for hc in range(86):
                    for s_ in range(4):
                        n += 1
                        SIL[(hc, s_)] = n
                M2E = {}
                for s_ in range(4):
                    n += 1
                    M2E[s_] = n
                UD = {}
                n = c3.n
                for hc in range(86):
                    for s_ in range(4):
                        n += 1
                        UD[(hc, s_)] = n

                with nc.Block() as blk:
                    @blk.sync
                    def _(sy):
                        sy.wait_ge(ph.h, ph_in)
                        dm[13].inc(sy.dma_start(b1T_s[0:2, :], b1td.ap()), 16)
                        dm[13].inc(sy.dma_start(b3T_s[0:2, :], b3td.ap()), 16)
                        for hc in range(86):
                            if hc >= 2:
                                sy.wait_ge(c1.h, H3[(hc - 2, 3)])
                            W1L[hc][0].inc(sy.dma_start(
                                sb3(w1cb[hc % 2][:, :], 8),
                                dr3(w1t.ap()[:, hc * 128:(hc + 1) * 128])), 16)
                            W3L[hc][0].inc(sy.dma_start(
                                sb3(w3cb[hc % 2][:, :], 8),
                                dr3(w3t.ap()[:, hc * 128:(hc + 1) * 128])), 16)
                            if hc >= 1:
                                sy.wait_ge(c3.h, UD[(hc - 1, 3)])
                                UST[hc - 1][0].inc(sy.dma_start(
                                    uT.ap()[(hc - 1) * 128:hc * 128, :],
                                    ub[(hc - 1) % 2][:, :]), 16)
                        sy.wait_ge(c3.h, UD[(85, 3)])
                        UST[85][0].inc(sy.dma_start(
                            uT.ap()[85 * 128:86 * 128, :],
                            ub[85 % 2][:, :]), 16)
                        sy.wait_ge(c2.h, M2E[3])
                        dm[13].inc(sy.dma_start(mid2d.ap(), mid2f[0:2, :]), 16)
                        for i_ in (6, 7, 13, 14, 15, 18, 19):
                            sy.wait_ge(dm[i_].h, dm[i_].n)
                        ph.inc(sy.nop())

                    @blk.tensor
                    def _(t):
                        t.wait_ge(ph.h, ph_in)
                        pq6 = []

                        def flush6(t, force=False):
                            if not pq6 or (not force and len(pq6) < 2):
                                return
                            hp, sp = pq6.pop(0)
                            t.wait_ge(c3.h, UD[(hp, sp)])
                            c1.inc(t.matmul(
                                pqm[sp][:, :],
                                a2T_sb[:, hp * 2:hp * 2 + 2],
                                ub[hp % 2][:, sp * 512:(sp + 1) * 512],
                                start=(hp == 0), stop=(hp == 85)))

                        for hc in range(86):
                            for s_ in range(4):
                                js = js_of(hc, s_)
                                flush6(t)
                                if s_ == 0:
                                    wt(t, W1L[hc])
                                    wt(t, W3L[hc])
                                    if hc == 0:
                                        wt(t, BL)
                                if js >= 2:
                                    pj = (js - 2) // 4, (js - 2) % 4
                                    t.wait_ge(c3.h, UD[pj])
                                for wcb, bT, mf, php in (
                                        (w1cb, b1T_s, mid1f, ph1),
                                        (w3cb, b3T_s, mid3f, ph3)):
                                    for dc in range(8):
                                        t.matmul(
                                            php[s_ % 2][:, :],
                                            wcb[hc % 2][:, dc * 128:(dc + 1) * 128],
                                            xs_sb[:, dc * TH + s_ * 512:
                                                  dc * TH + s_ * 512 + 512],
                                            start=(dc == 0), stop=False)
                                    mm = t.matmul(
                                        php[s_ % 2][:, :],
                                        bT[0:2, hc * 128:(hc + 1) * 128],
                                        mf[0:2, s_ * 512:(s_ + 1) * 512],
                                        start=False, stop=True)
                                    c1.inc(mm)
                                pq6.append((hc, s_))
                        while pq6:
                            flush6(t, force=True)
                        endsec(t, c1)

                    @blk.scalar
                    def _(sc):
                        sc.wait_ge(ph.h, ph_in)
                        for hc in range(86):
                            for s_ in range(4):
                                js = js_of(hc, s_)
                                sc.wait_ge(c1.h, H1[(hc, s_)])
                                if js >= 2:
                                    sc.wait_ge(c3.h, UD[((js - 2) // 4,
                                                        (js - 2) % 4)])
                                c2.inc(sc.activation(
                                    silb[js % 2][:, :],
                                    ph1[s_ % 2][:, :], AF.Silu))
                        sc.wait_ge(c1.h, MID2[(85, 3)])
                        for s_ in range(4):
                            c2.inc(sc.mul(mid2f[0:2, s_ * 512:(s_ + 1) * 512],
                                          pqm[s_][:, :], 0.5))
                        endsec(sc, c2)

                    @blk.vector
                    def _(v):
                        v.wait_ge(ph.h, ph_in)
                        for hc in range(86):
                            for s_ in range(4):
                                js = js_of(hc, s_)
                                v.wait_ge(c2.h, SIL[(hc, s_)])
                                v.wait_ge(c1.h, H3[(hc, s_)])
                                if s_ == 0 and hc >= 2:
                                    wt(v, UST[hc - 2])
                                    v.wait_ge(c1.h, MID2[(hc - 2, 3)])
                                c3.inc(v.tensor_mul(
                                    ub[hc % 2][:, s_ * 512:(s_ + 1) * 512],
                                    silb[js % 2][:, :],
                                    ph3[s_ % 2][:, :]))
                        endsec(v, c3)

                assert ph.n == ph_in + 4

        if _KPH <= 8:
            _fin()
            return nc

        with ExitStack() as P:
            us_sb = P.enter_context(nc.sbuf_tensor("us_sb", [128, 22 * TH], BF16))
            with ExitStack() as Pa:
                ul0 = Pa.enter_context(nc.sbuf_tensor("ul0", [128, 4 * TH], BF16))
                ul1 = Pa.enter_context(nc.sbuf_tensor("ul1", [128, 4 * TH], BF16))
                utmp = Pa.enter_context(nc.sbuf_tensor("utmp", [128, TH], F32))
                ulb = [ul0, ul1]

                ph_in = ph.n
                UL = {}
                nul = [dm[6].n, dm[7].n]
                for j in range(22):
                    nul[j % 2] += 4 * 16
                    UL[j] = (dm[6 + j % 2], nul[j % 2])
                FD = {j: c6.n + j + 1 for j in range(22)}

                with nc.Block() as blk:
                    @blk.sync
                    def _(sy):
                        sy.wait_ge(ph.h, ph_in)
                        for j in range(22):
                            rows = 128 if j < 21 else 64
                            if j >= 2:
                                sy.wait_ge(c6.h, FD[j - 2])
                            for c_ in range(4):
                                r0 = c_ * HQ + j * 128
                                dm[6 + j % 2].inc(sy.dma_start(
                                    ulb[j % 2][0:rows, c_ * TH:c_ * TH + TH],
                                    uT.ap()[r0:r0 + rows, :]), 16)
                        for i_ in (6, 7):
                            sy.wait_ge(dm[i_].h, dm[i_].n)
                        ph.inc(sy.nop())

                    @blk.vector
                    def _(v):
                        v.wait_ge(ph.h, ph_in)
                        for j in range(22):
                            rows = 128 if j < 21 else 64
                            wt(v, UL[j])
                            ub_ = ulb[j % 2]
                            v.wait_ge(vch.h, vch.n)
                            v.wait_ge(c6.h, c6.n)
                            vch.inc(v.tensor_add(utmp[0:rows, :],
                                                 ub_[0:rows, 0:TH],
                                                 ub_[0:rows, TH:2 * TH]))
                            v.wait_ge(vch.h, vch.n)
                            vch.inc(v.tensor_add(utmp[0:rows, :],
                                                 utmp[0:rows, :],
                                                 ub_[0:rows, 2 * TH:3 * TH]))
                            v.wait_ge(vch.h, vch.n)
                            c6.inc(v.tensor_add(
                                us_sb[0:rows, j * TH:(j + 1) * TH],
                                utmp[0:rows, :],
                                ub_[0:rows, 3 * TH:4 * TH]))
                        endsec(v, c6)

                assert ph.n == ph_in + 2

            with ExitStack() as P:
                w2c0 = P.enter_context(nc.sbuf_tensor("w2c0", [128, 22 * 256], BF16))
                w2c1 = P.enter_context(nc.sbuf_tensor("w2c1", [128, 22 * 256], BF16))
                yr0 = P.enter_context(nc.sbuf_tensor("yr0", [128, 16 * 256], F32))
                yr1 = P.enter_context(nc.sbuf_tensor("yr1", [128, 16 * 256], F32))
                zt0 = P.enter_context(nc.sbuf_tensor("zt0", [128, 16 * 256], F32))
                zt1 = P.enter_context(nc.sbuf_tensor("zt1", [128, 16 * 256], F32))
                b2T_s = P.enter_context(nc.sbuf_tensor("b2T_s", [2, D], BF16))
                mid2s = P.enter_context(nc.sbuf_tensor("mid2s", [2, TH], BF16))
                trs2 = P.enter_context(nc.sbuf_tensor("trs2", [128, 256], F32))
                pz0 = P.enter_context(nc.psum_tensor("pz0", [128, 256], F32))
                pz1 = P.enter_context(nc.psum_tensor("pz1", [128, 256], F32))
                pz2 = P.enter_context(nc.psum_tensor("pz2", [128, 256], F32))
                w2cb = [w2c0, w2c1]
                yrb = [yr0, yr1]
                ztb = [zt0, zt1]
                pz = [pz0, pz1, pz2]

                ph_in = ph.n
                jobs7 = [(oc, tt) for oc in range(16) for tt in range(16)]
                j7 = {jj: i for i, jj in enumerate(jobs7)}
                BM2L = (dm[13], dm[13].n + 32)
                W2L = {}
                YRL = {}
                ZST = {}
                nw2 = [dm[6].n, dm[7].n]
                nyr = [dm[18].n, dm[19].n]
                nzs = [dm[14].n, dm[15].n]
                for oc in range(16):
                    nw2[oc % 2] += 16
                    W2L[oc] = (dm[6 + oc % 2], nw2[oc % 2])
                    nyr[oc % 2] += 16
                    YRL[oc] = (dm[18 + oc % 2], nyr[oc % 2])
                for oc in range(16):
                    nzs[oc % 2] += 16
                    ZST[oc] = (dm[14 + oc % 2], nzs[oc % 2])
                ZM = {}
                n = c1.n
                for jj in jobs7:
                    n += 1
                    ZM[jj] = n
                ZEV = {}
                n = c6.n
                for jj in jobs7:
                    n += 1
                    ZEV[jj] = n
                ZSQ = {}
                n = c2.n
                for jj in jobs7:
                    n += 1
                    ZSQ[jj] = n

                with nc.Block() as blk:
                    @blk.sync
                    def _(sy):
                        sy.wait_ge(ph.h, ph_in)
                        dm[13].inc(sy.dma_start(b2T_s[0:2, :], b2td.ap()), 16)
                        dm[13].inc(sy.dma_start(mid2s[0:2, :], mid2d.ap()), 16)
                        for oc in range(16):
                            if oc >= 2:
                                sy.wait_ge(c1.h, ZM[(oc - 2, 15)])
                            W2L[oc][0].inc(sy.dma_start(
                                sb3(w2cb[oc % 2][:, :], 22),
                                dr3(w2t.ap()[:, oc * 256:(oc + 1) * 256])), 16)
                            if oc >= 2:
                                sy.wait_ge(c6.h, ZEV[(oc - 2, 15)])
                            YRL[oc][0].inc(sy.dma_start(
                                sb3(yrb[oc % 2][:, :], 16),
                                dr3(y32.ap()[:, oc * 256:(oc + 1) * 256])), 16)
                            if oc >= 1:
                                sy.wait_ge(c6.h, ZEV[(oc - 1, 15)])
                                ZST[oc - 1][0].inc(sy.dma_start(
                                    dr3(zpre.ap()[:, (oc - 1) * 256:
                                                  oc * 256]),
                                    sb3(ztb[(oc - 1) % 2][:, :], 16)), 16)
                        sy.wait_ge(c6.h, ZEV[(15, 15)])
                        ZST[15][0].inc(sy.dma_start(
                            dr3(zpre.ap()[:, 15 * 256:16 * 256]),
                            sb3(ztb[15 % 2][:, :], 16)), 16)
                        for i_ in (6, 7, 13, 14, 15, 18, 19):
                            sy.wait_ge(dm[i_].h, dm[i_].n)
                        ph.inc(sy.nop())

                    @blk.tensor
                    def _(t):
                        t.wait_ge(ph.h, ph_in)
                        for jj in jobs7:
                            oc, tt = jj
                            i = j7[jj]
                            if tt == 0:
                                wt(t, W2L[oc])
                                if oc == 0:
                                    wt(t, BM2L)
                            if i >= 3:
                                t.wait_ge(c6.h, ZEV[jobs7[i - 3]])
                            for j in range(22):
                                rows = 128 if j < 21 else 64
                                t.matmul(
                                    pz[i % 3][:, :],
                                    us_sb[0:rows, j * TH + tt * 128:
                                          j * TH + tt * 128 + 128],
                                    w2cb[oc % 2][0:rows,
                                                 j * 256:(j + 1) * 256],
                                    start=(j == 0), stop=False)
                            mm = t.matmul(
                                pz[i % 3][:, :],
                                mid2s[0:2, tt * 128:(tt + 1) * 128],
                                b2T_s[0:2, oc * 256:(oc + 1) * 256],
                                start=False, stop=True)
                            c1.inc(mm)
                        endsec(t, c1)

                    @blk.vector
                    def _(v):
                        v.wait_ge(ph.h, ph_in)
                        for jj in jobs7:
                            oc, tt = jj
                            i = j7[jj]
                            v.wait_ge(c1.h, ZM[jj])
                            if tt == 0:
                                wt(v, YRL[oc])
                                if oc >= 2:
                                    wt(v, ZST[oc - 2])
                                    v.wait_ge(c2.h, ZSQ[(oc - 2, 15)])
                            c6.inc(v.scalar_tensor_tensor(
                                ztb[oc % 2][:, tt * 256:(tt + 1) * 256],
                                pz[i % 3][:, :], 1.0,
                                yrb[oc % 2][:, tt * 256:(tt + 1) * 256],
                                ALU.mult, ALU.add,
                                accum_out=stat_sb[:, 2 + i % 2:3 + i % 2]))
                            v.wait_ge(c6.h, ZEV[jj])
                            v.wait_ge(vch.h, vch.n)
                            if oc == 0:
                                vch.inc(v.tensor_copy(
                                    zsum_sb[:, tt:tt + 1],
                                    stat_sb[:, 2 + i % 2:3 + i % 2]))
                            else:
                                vch.inc(v.tensor_add(
                                    zsum_sb[:, tt:tt + 1],
                                    zsum_sb[:, tt:tt + 1],
                                    stat_sb[:, 2 + i % 2:3 + i % 2]))
                            if i >= 1:
                                poc, ptt = jobs7[i - 1]
                                v.wait_ge(c2.h, ZSQ[jobs7[i - 1]])
                                v.wait_ge(vch.h, vch.n)
                                cell = stat_sb[:, 4 + (i - 1) % 2:
                                               5 + (i - 1) % 2]
                                if poc == 0:
                                    vch.inc(v.tensor_copy(
                                        zsq_sb[:, ptt:ptt + 1], cell))
                                else:
                                    vch.inc(v.tensor_add(
                                        zsq_sb[:, ptt:ptt + 1],
                                        zsq_sb[:, ptt:ptt + 1], cell))
                        i = len(jobs7)
                        poc, ptt = jobs7[-1]
                        v.wait_ge(c2.h, ZSQ[jobs7[-1]])
                        v.wait_ge(vch.h, vch.n)
                        vch.inc(v.tensor_add(
                            zsq_sb[:, ptt:ptt + 1],
                            zsq_sb[:, ptt:ptt + 1],
                            stat_sb[:, 4 + (i - 1) % 2:5 + (i - 1) % 2]))
                        endsec(v, c6)

                    @blk.scalar
                    def _(sc):
                        sc.wait_ge(ph.h, ph_in)
                        for jj in jobs7:
                            oc, tt = jj
                            i = j7[jj]
                            sc.wait_ge(c6.h, ZEV[jj])
                            if i >= 1:
                                sc.wait_ge(c2.h, ZSQ[jobs7[i - 1]])
                            c2.inc(sc.activation(
                                trs2[:, :],
                                ztb[oc % 2][:, tt * 256:(tt + 1) * 256],
                                AF.Square,
                                accum_out=stat_sb[:, 4 + i % 2:5 + i % 2]))
                        endsec(sc, c2)

                assert ph.n == ph_in + 4

        if _KPH <= 10:
            _fin()
            return nc

        with ExitStack() as P:
            zp0 = P.enter_context(nc.sbuf_tensor("zp0", [128, D], F32))
            zp1 = P.enter_context(nc.sbuf_tensor("zp1", [128, D], F32))
            oo0 = P.enter_context(nc.sbuf_tensor("oo0", [128, D], F32))
            oo1 = P.enter_context(nc.sbuf_tensor("oo1", [128, D], F32))
            zpb = [zp0, zp1]
            oob = [oo0, oo1]

            ph_in = ph.n
            ZL = {}
            OST = {}
            nzl = [dm[6].n, dm[7].n]
            nos = [dm[14].n, dm[15].n]
            for tt in range(16):
                nzl[tt % 2] += 16
                ZL[tt] = (dm[6 + tt % 2], nzl[tt % 2])
                nos[tt % 2] += 16
                OST[tt] = (dm[14 + tt % 2], nos[tt % 2])
            VA8 = {tt: c3.n + tt + 1 for tt in range(16)}
            RS8 = {tt: c2.n + tt + 1 for tt in range(16)}
            NM8 = {tt: c6.n + tt + 1 for tt in range(16)}

            with nc.Block() as blk:
                @blk.sync
                def _(sy):
                    sy.wait_ge(ph.h, ph_in)
                    for tt in range(16):
                        if tt >= 2:
                            sy.wait_ge(c6.h, NM8[tt - 2])
                        ZL[tt][0].inc(sy.dma_start(
                            zpb[tt % 2][:, :],
                            zpre.ap()[tt * 128:(tt + 1) * 128, :]), 16)
                        sy.wait_ge(c6.h, NM8[tt])
                        OST[tt][0].inc(sy.dma_start(
                            o.ap()[tt * 128:(tt + 1) * 128, :],
                            oob[tt % 2][:, :]), 16)
                    sy.wait_ge(dm[14].h, dm[14].n)
                    sy.wait_ge(dm[15].h, dm[15].n)
                    ph.inc(sy.nop())

                @blk.vector
                def _(v):
                    v.wait_ge(ph.h, ph_in)
                    for tt in range(16):
                        p = tt % 2
                        if tt >= 2:
                            v.wait_ge(c2.h, RS8[tt - 2])
                            v.wait_ge(c6.h, NM8[tt - 2])
                        v.wait_ge(vch.h, vch.n)
                        vch.inc(v.tensor_scalar_mul(
                            stat_sb[:, p:p + 1],
                            zsum_sb[:, tt:tt + 1], 1.0 / D))
                        v.wait_ge(vch.h, vch.n)
                        vch.inc(v.tensor_mul(stat_sb[:, 6 + p:7 + p],
                                             stat_sb[:, p:p + 1],
                                             stat_sb[:, p:p + 1]))
                        v.wait_ge(vch.h, vch.n)
                        c3.inc(v.scalar_tensor_tensor(
                            stat_sb[:, 2 + p:3 + p],
                            zsq_sb[:, tt:tt + 1], 1.0 / D,
                            stat_sb[:, 6 + p:7 + p],
                            ALU.mult, ALU.subtract))
                        if tt >= 1:
                            pp = (tt - 1) % 2
                            v.wait_ge(c2.h, RS8[tt - 1])
                            if tt - 1 >= 2:
                                wt(v, OST[tt - 3])
                            wt(v, ZL[tt - 1])
                            c6.inc(v.tensor_scalar(
                                oob[pp][:, :], zpb[pp][:, :],
                                stat_sb[:, pp:pp + 1],
                                stat_sb[:, 4 + pp:5 + pp],
                                ALU.subtract, ALU.mult))
                    pp = 15 % 2
                    v.wait_ge(c2.h, RS8[15])
                    wt(v, OST[13])
                    wt(v, ZL[15])
                    c6.inc(v.tensor_scalar(
                        oob[pp][:, :], zpb[pp][:, :],
                        stat_sb[:, pp:pp + 1],
                        stat_sb[:, 4 + pp:5 + pp],
                        ALU.subtract, ALU.mult))
                    endsec(v, c6)

                @blk.scalar
                def _(sc):
                    sc.wait_ge(ph.h, ph_in)
                    for tt in range(16):
                        p = tt % 2
                        sc.wait_ge(c3.h, VA8[tt])
                        sch.inc(sc.activation(stat_sb[:, 6 + p:7 + p],
                                              stat_sb[:, 2 + p:3 + p], AF.Ln,
                                              bias=EPS))
                        sc.wait_ge(sch.h, sch.n)
                        c2.inc(sc.activation(stat_sb[:, 4 + p:5 + p],
                                             stat_sb[:, 6 + p:7 + p],
                                             AF.Exp, scale=-0.5))
                    endsec(sc, c2)

    return nc


_NC_CACHE = []


def _get_program():
    if not _NC_CACHE:
        _NC_CACHE.append(build_program())
    return _NC_CACHE[0]


def _consts():
    inv_freq = (1.0 / (500000.0 **
                       (np.arange(0, HD, 2, dtype=np.float32) / HD))
                ).astype(np.float32)
    t = np.arange(T, dtype=np.float32)
    freqs = np.outer(t, inv_freq).astype(np.float32)
    cosT = np.cos(freqs).T.astype(np.float32)
    sinT = np.sin(freqs).T.astype(np.float32)
    csa_c = np.empty((128, T), dtype=ml_dtypes.bfloat16)
    csa_c[0:64] = cosT
    csa_c[64:128] = cosT
    csb_c = np.empty((128, T), dtype=ml_dtypes.bfloat16)
    csb_c[0:64] = -sinT
    csb_c[64:128] = sinT
    swp_c = np.zeros((128, 128), dtype=ml_dtypes.bfloat16)
    for m in range(128):
        swp_c[(m + 64) % 128, m] = 1.0
    ident_c = np.eye(128, dtype=ml_dtypes.bfloat16)
    ones_c = np.ones((128, 128), dtype=ml_dtypes.bfloat16)
    kt = np.arange(128)[:, None]
    qt = np.arange(512)[None, :]
    maskp_c = np.zeros((128, 2048), dtype=ml_dtypes.bfloat16)
    for j2 in range(2):
        for half in range(2):
            jj = 2 * j2 + half
            m = (jj * 128 + kt <= qt).astype(ml_dtypes.bfloat16)
            maskp_c[:, j2 * 1024 + half * 512:
                    j2 * 1024 + half * 512 + 512] = m
    return csa_c, csb_c, swp_c, ident_c, ones_c, maskp_c


def kernel(L0_attn_x, L0_attn_Wq, L0_attn_Wk, L0_attn_Wv, L0_attn_Wo,
           L0_ffn_w1q, L0_ffn_a1, L0_ffn_b1, L0_ffn_w3q, L0_ffn_a3,
           L0_ffn_b3, L0_ffn_w2q, L0_ffn_a2, L0_ffn_b2):
    xall = np.ascontiguousarray(np.asarray(L0_attn_x, dtype=np.float32))
    Wq = np.asarray(L0_attn_Wq, dtype=np.float32)
    Wk = np.asarray(L0_attn_Wk, dtype=np.float32)
    Wv = np.asarray(L0_attn_Wv, dtype=np.float32)
    Wo = np.ascontiguousarray(np.asarray(L0_attn_Wo, dtype=np.float32))
    w1q = np.ascontiguousarray(np.asarray(L0_ffn_w1q, dtype=np.float32))
    w3q = np.ascontiguousarray(np.asarray(L0_ffn_w3q, dtype=np.float32))
    w2q = np.ascontiguousarray(np.asarray(L0_ffn_w2q, dtype=np.float32))
    a1_ = np.ascontiguousarray(np.asarray(L0_ffn_a1, dtype=np.float32))
    b1_ = np.ascontiguousarray(np.asarray(L0_ffn_b1, dtype=np.float32))
    a3_ = np.ascontiguousarray(np.asarray(L0_ffn_a3, dtype=np.float32))
    b3_ = np.ascontiguousarray(np.asarray(L0_ffn_b3, dtype=np.float32))
    a2_ = np.ascontiguousarray(np.asarray(L0_ffn_a2, dtype=np.float32))
    b2_ = np.ascontiguousarray(np.asarray(L0_ffn_b2, dtype=np.float32))

    csa_c, csb_c, swp_c, ident_c, ones_c, maskp_c = _consts()

    in_maps = []
    for c in range(8):
        b = c // 2
        s = c % 2
        in_maps.append({
            "x": xall[b],
            "xh": np.ascontiguousarray(xall[b][s * TH:(s + 1) * TH]),
            "wq": np.ascontiguousarray(Wq[:, s * 2048:(s + 1) * 2048]),
            "wk": np.ascontiguousarray(Wk[:, s * 512:(s + 1) * 512]),
            "wv": np.ascontiguousarray(Wv[:, s * 512:(s + 1) * 512]),
            "wo": Wo,
            "w1": w1q, "w3": w3q, "w2": w2q,
            "a1": a1_, "b1": b1_, "a3": a3_, "b3": b3_,
            "a2": a2_, "b2": b2_,
            "csa": csa_c, "csb": csb_c, "swp": swp_c,
            "ident": ident_c, "onesb": ones_c,
            "maskp": maskp_c,
            "agoff": np.array([[s * D * TH]], dtype=np.int32),
        })

    nc = _get_program()
    res = run_bass_kernel_spmd(nc, in_maps, core_ids=list(range(8)))
    out = np.empty((B, T, D), dtype=np.float32)
    for c in range(8):
        b = c // 2
        s = c % 2
        out[b, s * TH:(s + 1) * TH] = np.asarray(res.results[c]["o"])
    return out


# version 06
# speedup vs baseline: 1.3036x; 1.0501x over previous
import math
import os
from contextlib import ExitStack

_KPH = int(os.environ.get("KPH", "99"))
_KP3G = int(os.environ.get("KP3G", "32"))
_KNOMASK = os.environ.get("KNOMASK", "0") == "1"
_KNOEXP2 = os.environ.get("KNOEXP2", "0") == "1"
_KNOEPI = os.environ.get("KNOEPI", "0") == "1"
_KNOROW = os.environ.get("KNOROW", "0") == "1"

import numpy as np
import ml_dtypes

import concourse.bass as bass
import concourse.mybir as mybir
from concourse.bass_utils import run_bass_kernel_spmd

F32 = mybir.dt.float32
BF16 = mybir.dt.bfloat16
I32 = mybir.dt.int32
AF = mybir.ActivationFunctionType
ALU = mybir.AluOpType
AX = mybir.AxisListType

B, T, D = 4, 4096, 4096
HD = 128
NHC = 16
NGC = 4
TH = T // 2
HF = 11008
DQ = 1024
HQ = 2752
HQP = 2816
SCALE = 1.0 / math.sqrt(HD)
EPS = 1e-5


class Ctr:
    def __init__(self, handle):
        self.h = handle
        self.n = 0

    def inc(self, instr, k=1):
        instr.then_inc(self.h, k)
        self.n += k
        return self.n


def sb3(sb_slice, tiles):
    return sb_slice.rearrange("p (t c) -> p t c", t=tiles)


def dr3(dr_slice):
    return dr_slice.rearrange("(t p) c -> p t c", p=128)


def build_program():
    nc = bass.Bass("TRN2", target_bir_lowering=False)

    x = nc.dram_tensor("x", [T, D], F32, kind="ExternalInput")
    xh = nc.dram_tensor("xh", [TH, D], F32, kind="ExternalInput")
    wq = nc.dram_tensor("wq", [D, NHC * HD], F32, kind="ExternalInput")
    wk = nc.dram_tensor("wk", [D, NGC * HD], F32, kind="ExternalInput")
    wv = nc.dram_tensor("wv", [D, NGC * HD], F32, kind="ExternalInput")
    wo = nc.dram_tensor("wo", [D, D], F32, kind="ExternalInput")
    w1 = nc.dram_tensor("w1", [HF, DQ], F32, kind="ExternalInput")
    w3 = nc.dram_tensor("w3", [HF, DQ], F32, kind="ExternalInput")
    w2 = nc.dram_tensor("w2", [D, HQ], F32, kind="ExternalInput")
    a1 = nc.dram_tensor("a1", [2, D], F32, kind="ExternalInput")
    b1 = nc.dram_tensor("b1", [HF, 2], F32, kind="ExternalInput")
    a3 = nc.dram_tensor("a3", [2, D], F32, kind="ExternalInput")
    b3 = nc.dram_tensor("b3", [HF, 2], F32, kind="ExternalInput")
    a2 = nc.dram_tensor("a2", [2, HF], F32, kind="ExternalInput")
    b2 = nc.dram_tensor("b2", [D, 2], F32, kind="ExternalInput")
    csa = nc.dram_tensor("csa", [128, T], BF16, kind="ExternalInput")
    csb = nc.dram_tensor("csb", [128, T], BF16, kind="ExternalInput")
    swp = nc.dram_tensor("swp", [128, 128], BF16, kind="ExternalInput")
    ident = nc.dram_tensor("ident", [128, 128], BF16, kind="ExternalInput")
    onesb = nc.dram_tensor("onesb", [128, 128], BF16, kind="ExternalInput")
    maskp = nc.dram_tensor("maskp", [128, 2048], BF16, kind="ExternalInput")
    agoff = nc.dram_tensor("agoff", [1, 1], I32, kind="ExternalInput")
    o = nc.dram_tensor("o", [TH, D], F32, kind="ExternalOutput")

    xbf = nc.dram_tensor("xbf", [T, D], BF16)
    w1bf = nc.dram_tensor("w1bf", [HF, DQ], BF16)
    w3bf = nc.dram_tensor("w3bf", [HF, DQ], BF16)
    w2bf = nc.dram_tensor("w2bf", [D, HQP], BF16)
    w1t = nc.dram_tensor("w1t", [DQ, HF], BF16)
    w3t = nc.dram_tensor("w3t", [DQ, HF], BF16)
    w2t = nc.dram_tensor("w2t", [HQP, D], BF16)
    b1td = nc.dram_tensor("b1td", [2, HF], BF16)
    b3td = nc.dram_tensor("b3td", [2, HF], BF16)
    b2td = nc.dram_tensor("b2td", [2, D], BF16)
    kT = nc.dram_tensor("kT", [NGC * HD, T], BF16)
    vS = nc.dram_tensor("vS", [T, NGC * HD], BF16)
    qT = nc.dram_tensor("qT", [NHC * HD, T], BF16)
    ctxq = [nc.dram_tensor(f"ctxq{q}", [NHC * HD, TH // 2], BF16)
            for q in range(4)]
    ctxAGh = [nc.dram_tensor(f"ctxAG{i}", [4 * NHC * HD, TH // 2], BF16)
              for i in range(2)]
    ctxFh = [nc.dram_tensor(f"ctxF{i}", [D, TH // 2], BF16) for i in range(2)]
    ypre = nc.dram_tensor("ypre", [TH, D], F32)
    y32 = nc.dram_tensor("y32", [TH, D], F32)
    ybf = nc.dram_tensor("ybf", [TH, D], BF16)
    uT = nc.dram_tensor("uT", [HF, TH], BF16)
    mid2d = nc.dram_tensor("mid2d", [2, TH], BF16)
    zpre = nc.dram_tensor("zpre", [TH, D], F32)

    with ExitStack() as CTX:
        E = CTX.enter_context

        ones_sb = E(nc.sbuf_tensor("ones_sb", [128, 128], BF16))
        onesf_sb = E(nc.sbuf_tensor("onesf_sb", [128, 32], F32))
        eps_sb = E(nc.sbuf_tensor("eps_sb", [128, 1], F32))
        nc.const_aps.register(F32, EPS, eps_sb[:, 0:1])
        mneg_sb = E(nc.sbuf_tensor("mneg_sb", [128, 128], F32))
        qmax2_sb = E(nc.sbuf_tensor("qmax2_sb", [128, 128], F32))
        kmax2_sb = E(nc.sbuf_tensor("kmax2_sb", [128, NGC], F32))
        stat_sb = E(nc.sbuf_tensor("stat_sb", [128, 8], F32))
        a1T_sb = E(nc.sbuf_tensor("a1T_sb", [128, 64], BF16))
        a3T_sb = E(nc.sbuf_tensor("a3T_sb", [128, 64], BF16))
        a2T_sb = E(nc.sbuf_tensor("a2T_sb", [128, 172], BF16))
        ysum_sb = E(nc.sbuf_tensor("ysum_sb", [128, 16], F32))
        ysq_sb = E(nc.sbuf_tensor("ysq_sb", [128, 16], F32))
        zsum_sb = E(nc.sbuf_tensor("zsum_sb", [128, 16], F32))
        zsq_sb = E(nc.sbuf_tensor("zsq_sb", [128, 16], F32))
        agoff_sb = E(nc.sbuf_tensor("agoff_sb", [1, 1], I32))

        ph = Ctr(E(nc.semaphore("ph")))
        dm = [Ctr(E(nc.semaphore(f"dm{i}"))) for i in range(22)]

        def wt(eng, ev):
            eng.wait_ge(ev[0].h, ev[1])
        c1 = Ctr(E(nc.semaphore("s1")))
        c2 = Ctr(E(nc.semaphore("s2")))
        c3 = Ctr(E(nc.semaphore("s3")))
        c4 = Ctr(E(nc.semaphore("s4")))
        c5 = Ctr(E(nc.semaphore("s5")))
        c6 = Ctr(E(nc.semaphore("s6")))
        cc1 = Ctr(E(nc.semaphore("cc1")))
        cc2 = Ctr(E(nc.semaphore("cc2")))
        cc3 = Ctr(E(nc.semaphore("cc3")))
        cc4 = Ctr(E(nc.semaphore("cc4")))
        vch = Ctr(E(nc.semaphore("vch")))
        sch = Ctr(E(nc.semaphore("sch")))

        def endsec(eng, own=None, val=None):
            if own is not None:
                eng.wait_ge(own.h, val if val is not None else own.n)
            ph.inc(eng.nop())

        def _fin():
            ph_f = ph.n
            with nc.Block() as blk:
                @blk.sync
                def _(sy):
                    sy.wait_ge(ph.h, ph_f)
                    dm[14].inc(sy.dma_start(o.ap(), x.ap()[0:TH, :]), 16)
                    sy.wait_ge(dm[14].h, dm[14].n)

        with ExitStack() as P:
            tp0 = P.enter_context(nc.sbuf_tensor("tp0", [128, HF], BF16))
            tp1 = P.enter_context(nc.sbuf_tensor("tp1", [128, HF], BF16))
            ab_sb = P.enter_context(nc.sbuf_tensor("ab_sb", [2, HF], BF16))
            bb_sb = P.enter_context(nc.sbuf_tensor("bb_sb", [128, 172], BF16))
            btT_sb = P.enter_context(nc.sbuf_tensor("btT_sb", [2, HF], BF16))
            id_sb = P.enter_context(nc.sbuf_tensor("id_sb", [128, 128], BF16))
            ps_a = P.enter_context(nc.psum_tensor("ps_a", [128, 256], BF16))
            ps_b = P.enter_context(nc.psum_tensor("ps_b", [128, 256], BF16))
            tp = [tp0, tp1]

            ph_in = ph.n
            GEV = {
                "xbf0": (dm[0], dm[0].n + 16),
                "xbf1": (dm[1], dm[1].n + 16),
                "wcasts": (dm[2], dm[2].n + 48),
                "ab_a1": (dm[3], dm[3].n + 16),
                "ab_a3": (dm[3], dm[3].n + 32),
                "ab_a2": (dm[3], dm[3].n + 48),
                "bb_b1": (dm[4], dm[4].n + 16),
                "bb_b3": (dm[4], dm[4].n + 32),
                "bb_b2": (dm[4], dm[4].n + 48),
                "consts": (dm[5], dm[5].n + 48),
            }
            AEV = {}
            n = c2.n
            n += 1; AEV["a1"] = n
            bev1 = {}
            for t in range(86):
                n += 1; bev1[t] = n
            AEV["b1"] = n
            n += 1; AEV["a3"] = n
            bev3 = {}
            for t in range(86):
                n += 1; bev3[t] = n
            AEV["b3"] = n
            n += 1; AEV["a2"] = n
            bev2 = {}
            for t in range(32):
                n += 1; bev2[t] = n
            AEV["b2"] = n
            TEV = {"b1g": {}, "b3g": {}, "b2g": {}}
            n = c1.n
            n += 1; TEV["a1"] = n
            for t in range(86):
                n += 1; TEV["b1g"][t] = n
            n += 1; TEV["a3"] = n
            for t in range(86):
                n += 1; TEV["b3g"][t] = n
            n += 1; TEV["a2"] = n
            for t in range(32):
                n += 1; TEV["b2g"][t] = n
            SEV = {}
            tpn = [dm[6].n, dm[7].n]
            for wid, ntile in (("w1", 8), ("w3", 8), ("w2", 22)):
                for wi in range(ntile):
                    pq = wi % 2
                    tpn[pq] += 16
                    SEV[(wid + "tp", wi)] = (dm[6 + pq], tpn[pq])
                    tpn[pq] += 16
                    SEV[(wid + "st", wi)] = (dm[6 + pq], tpn[pq])
            nb = dm[8].n
            for k in ["b1td", "b3td", "b2td"]:
                nb += 16
                SEV[k] = (dm[8], nb)

            with nc.Block() as blk:
                @blk.gpsimd
                def _(g):
                    dm[0].inc(g.dma_start(xbf.ap()[0:TH, :], x.ap()[0:TH, :]), 16)
                    dm[1].inc(g.dma_start(xbf.ap()[TH:T, :], x.ap()[TH:T, :]), 16)
                    dm[2].inc(g.dma_start(w1bf.ap(), w1.ap()), 16)
                    dm[2].inc(g.dma_start(w3bf.ap(), w3.ap()), 16)
                    dm[2].inc(g.dma_start(w2bf.ap()[:, 0:HQ], w2.ap()), 16)
                    dm[3].inc(g.dma_start(ab_sb[0:2, 0:D], a1.ap()), 16)
                    dm[4].inc(g.dma_start(sb3(bb_sb[:, 0:172], 86),
                                          dr3(b1.ap())), 16)
                    g.wait_ge(c1.h, TEV["b1g"][85])
                    g.wait_ge(c1.h, TEV["a1"])
                    dm[3].inc(g.dma_start(ab_sb[0:2, 0:D], a3.ap()), 16)
                    dm[4].inc(g.dma_start(sb3(bb_sb[:, 0:172], 86),
                                          dr3(b3.ap())), 16)
                    g.wait_ge(c1.h, TEV["b3g"][85])
                    g.wait_ge(c1.h, TEV["a3"])
                    dm[3].inc(g.dma_start(ab_sb[0:2, 0:HF], a2.ap()), 16)
                    dm[4].inc(g.dma_start(sb3(bb_sb[:, 0:64], 32),
                                          dr3(b2.ap())), 16)
                    for i_ in range(5):
                        g.wait_ge(dm[i_].h, dm[i_].n)
                    ph.inc(g.nop())

                @blk.sync
                def _(sy):
                    dm[5].inc(sy.dma_start(id_sb[:, :], ident.ap()), 16)
                    dm[5].inc(sy.dma_start(ones_sb[:, :], onesb.ap()), 16)
                    dm[5].inc(sy.dma_start(agoff_sb[:, :], agoff.ap()), 16)
                    last_st = [None, None]
                    for wt_, wbf, ntile, wid in (
                            (w1t, w1bf, 8, "w1"),
                            (w3t, w3bf, 8, "w3"),
                            (w2t, w2bf, 22, "w2")):
                        width = HF if wid != "w2" else D
                        for wi in range(ntile):
                            buf = tp[wi % 2]
                            wt(sy, GEV["wcasts"])
                            if last_st[wi % 2] is not None:
                                wt(sy, last_st[wi % 2])
                            SEV[(wid + "tp", wi)][0].inc(sy.dma_start(
                                buf[:, 0:width],
                                wbf.ap()[:, wi * 128:(wi + 1) * 128],
                                transpose=True), 16)
                            wt(sy, SEV[(wid + "tp", wi)])
                            SEV[(wid + "st", wi)][0].inc(sy.dma_start(
                                wt_.ap()[wi * 128:(wi + 1) * 128, :],
                                buf[:, 0:width]), 16)
                            last_st[wi % 2] = SEV[(wid + "st", wi)]
                    sy.wait_ge(c2.h, AEV["b1"])
                    dm[8].inc(sy.dma_start(b1td.ap(), btT_sb[0:2, :]), 16)
                    sy.wait_ge(c2.h, AEV["b3"])
                    dm[8].inc(sy.dma_start(b3td.ap(), btT_sb[0:2, :]), 16)
                    sy.wait_ge(c2.h, AEV["b2"])
                    dm[8].inc(sy.dma_start(b2td.ap(), btT_sb[0:2, 0:D]), 16)
                    for i_ in (5, 6, 7, 8):
                        sy.wait_ge(dm[i_].h, dm[i_].n)
                    ph.inc(sy.nop())

                @blk.tensor
                def _(t):
                    wt(t, GEV["ab_a1"])
                    wt(t, GEV["consts"])
                    for k in range(32):
                        mm = t.transpose(ps_a[:, 2 * k:2 * k + 2],
                                         ab_sb[0:2, 128 * k:128 * (k + 1)],
                                         id_sb[0:2, 0:2])
                    c1.inc(mm)
                    wt(t, GEV["bb_b1"])
                    for k in range(86):
                        if k >= 2:
                            t.wait_ge(c2.h, bev1[k - 1])
                        c1.inc(t.transpose(
                            ps_b[0:2, 128 * (k % 2):128 * (k % 2) + 128],
                            bb_sb[:, 2 * k:2 * k + 2], id_sb[:, :]))
                    wt(t, GEV["ab_a3"])
                    t.wait_ge(c2.h, AEV["a1"])
                    for k in range(32):
                        mm = t.transpose(ps_a[:, 2 * k:2 * k + 2],
                                         ab_sb[0:2, 128 * k:128 * (k + 1)],
                                         id_sb[0:2, 0:2])
                    c1.inc(mm)
                    wt(t, GEV["bb_b3"])
                    wt(t, SEV["b1td"])
                    for k in range(86):
                        if k >= 2:
                            t.wait_ge(c2.h, bev3[k - 1])
                        c1.inc(t.transpose(
                            ps_b[0:2, 128 * (k % 2):128 * (k % 2) + 128],
                            bb_sb[:, 2 * k:2 * k + 2], id_sb[:, :]))
                    wt(t, GEV["ab_a2"])
                    t.wait_ge(c2.h, AEV["a3"])
                    for k in range(86):
                        mm = t.transpose(ps_a[:, 2 * k:2 * k + 2],
                                         ab_sb[0:2, 128 * k:128 * (k + 1)],
                                         id_sb[0:2, 0:2])
                    c1.inc(mm)
                    wt(t, GEV["bb_b2"])
                    wt(t, SEV["b3td"])
                    for k in range(32):
                        if k >= 2:
                            t.wait_ge(c2.h, bev2[k - 1])
                        c1.inc(t.transpose(
                            ps_b[0:2, 128 * (k % 2):128 * (k % 2) + 128],
                            bb_sb[:, 2 * k:2 * k + 2], id_sb[:, :]))
                    endsec(t, c1)

                @blk.scalar
                def _(sc):
                    sc.wait_ge(c1.h, TEV["a1"])
                    c2.inc(sc.copy(a1T_sb[:, :], ps_a[:, 0:64]))
                    for k in range(86):
                        sc.wait_ge(c1.h, TEV["b1g"][k])
                        c2.inc(sc.copy(
                            btT_sb[0:2, 128 * k:128 * (k + 1)],
                            ps_b[0:2, 128 * (k % 2):128 * (k % 2) + 128]))
                    sc.wait_ge(c1.h, TEV["a3"])
                    c2.inc(sc.copy(a3T_sb[:, :], ps_a[:, 0:64]))
                    for k in range(86):
                        sc.wait_ge(c1.h, TEV["b3g"][k])
                        c2.inc(sc.copy(
                            btT_sb[0:2, 128 * k:128 * (k + 1)],
                            ps_b[0:2, 128 * (k % 2):128 * (k % 2) + 128]))
                    sc.wait_ge(c1.h, TEV["a2"])
                    c2.inc(sc.copy(a2T_sb[:, :], ps_a[:, 0:172]))
                    for k in range(32):
                        sc.wait_ge(c1.h, TEV["b2g"][k])
                        c2.inc(sc.copy(
                            btT_sb[0:2, 128 * k:128 * (k + 1)],
                            ps_b[0:2, 128 * (k % 2):128 * (k % 2) + 128]))
                    endsec(sc, c2)

                @blk.vector
                def _(v):
                    c6.inc(v.memset(eps_sb[:, :], EPS))
                    c6.inc(v.memset(onesf_sb[:, :], 1.0))
                    endsec(v, c6)

            assert c2.n == AEV["b2"]
            assert c1.n == TEV["b2g"][31]
            assert ph.n == ph_in + 5

        if _KPH <= 0:
            _fin()
            return nc

        with ExitStack() as P:
            csa_sb = P.enter_context(nc.sbuf_tensor("csa_sb", [128, T], BF16))
            csb_sb = P.enter_context(nc.sbuf_tensor("csb_sb", [128, T], BF16))
            swp_sb = P.enter_context(nc.sbuf_tensor("swp_sb", [128, 128], BF16))
            qe0 = P.enter_context(nc.sbuf_tensor("qe0", [128, 512], BF16))
            qe1 = P.enter_context(nc.sbuf_tensor("qe1", [128, 512], BF16))
            xT_sb = P.enter_context(nc.sbuf_tensor("xT_sb", [128, 32 * TH], BF16))
            wkq0 = P.enter_context(nc.sbuf_tensor("wkq0", [128, 32 * 128], BF16))
            wkq1 = P.enter_context(nc.sbuf_tensor("wkq1", [128, 32 * 128], BF16))
            wvb0 = P.enter_context(nc.sbuf_tensor("wvb0", [128, 32 * 256], BF16))
            wvb1 = P.enter_context(nc.sbuf_tensor("wvb1", [128, 32 * 256], BF16))
            rp0 = P.enter_context(nc.sbuf_tensor("rp0", [128, 1024], F32))
            sq0 = P.enter_context(nc.sbuf_tensor("sq0", [128, 512], BF16))
            sq1 = P.enter_context(nc.sbuf_tensor("sq1", [128, 512], BF16))
            ko0 = P.enter_context(nc.sbuf_tensor("ko0", [128, 512], BF16))
            ko1 = P.enter_context(nc.sbuf_tensor("ko1", [128, 512], BF16))
            vo0 = P.enter_context(nc.sbuf_tensor("vo0", [128, 256], BF16))
            vo1 = P.enter_context(nc.sbuf_tensor("vo1", [128, 256], BF16))
            kmc_sb = P.enter_context(nc.sbuf_tensor("kmc_sb", [128, 128], F32))
            qeb = [qe0, qe1]
            psw0 = P.enter_context(nc.psum_tensor("psw0", [128, 512], F32))
            psw1 = P.enter_context(nc.psum_tensor("psw1", [128, 512], F32))
            psk0 = P.enter_context(nc.psum_tensor("psk0", [128, 512], F32))
            psk1 = P.enter_context(nc.psum_tensor("psk1", [128, 512], F32))
            psk2 = P.enter_context(nc.psum_tensor("psk2", [128, 512], F32))
            pst0 = P.enter_context(nc.psum_tensor("pst0", [128, 512], F32))
            pst1 = P.enter_context(nc.psum_tensor("pst1", [128, 512], F32))
            wkq = [wkq0, wkq1]
            wvb = [wvb0, wvb1]
            sqb = [sq0, sq1]
            kob = [ko0, ko1]
            vob = [vo0, vo1]
            psk = [psk0, psk1, psk2]
            pst = [pst0, pst1]
            psw = [psw0, psw1]

            ph_in = ph.n

            jobs = []
            for H in (0, 1):
                for g in range(NGC):
                    for s4 in range(4):
                        jobs.append(("K", H, g, s4))
                for vc in range(2):
                    for tt in range(16):
                        jobs.append(("V", H, vc, tt))
                for h in range(NHC):
                    for s4 in range(4):
                        jobs.append(("Q", H, h, s4))
            jidx = {j: i for i, j in enumerate(jobs)}
            rjobs = [j for j in jobs if j[0] in ("K", "Q")]
            ridx = {j: i for i, j in enumerate(rjobs)}
            vjobs = [j for j in jobs if j[0] == "V"]
            vidx = {j: i for i, j in enumerate(vjobs)}

            kqloads = []
            vloads = []
            for H in (0, 1):
                for g in range(NGC):
                    kqloads.append(("K", H, g))
                for vc in range(2):
                    vloads.append(("V", H, vc))
                for h in range(NHC):
                    kqloads.append(("Q", H, h))
            kqli = {l: i for i, l in enumerate(kqloads)}
            vli = {l: i for i, l in enumerate(vloads)}

            def load_of(j):
                return (j[0], j[1], j[2])

            def last_job_of_load(l):
                kind, H, a = l
                return (kind, H, a, 3 if kind != "V" else 15)

            ldorder = []
            for H in (0, 1):
                for g in range(NGC):
                    ldorder.append(("K", H, g))
                for vc in range(2):
                    ldorder.append(("V", H, vc))
                for h in range(NHC):
                    ldorder.append(("Q", H, h))
            WEV = {}
            nkq = [dm[9].n, dm[10].n]
            nv = [dm[11].n, dm[12].n]
            for l in ldorder:
                if l[0] == "V":
                    i = vli[l]
                    nv[i % 2] += 16
                    WEV[l] = (dm[11 + i % 2], nv[i % 2])
                else:
                    i = kqli[l]
                    nkq[i % 2] += 16
                    WEV[l] = (dm[9 + i % 2], nkq[i % 2])
            CSL = (dm[5], dm[5].n + 48)
            XEV = {}
            OEV = {}
            nxt = dm[13].n
            nko = [dm[14].n, dm[15].n]
            nvo = [dm[16].n, dm[17].n]
            for H in (0, 1):
                nxt += 32 * 16
                XEV[H] = (dm[13], nxt)
                for j in [jj for jj in jobs if jj[1] == H]:
                    if j[0] == "V":
                        iv = vidx[j]
                        nvo[iv % 2] += 16
                        OEV[j] = (dm[16 + iv % 2], nvo[iv % 2])
                    else:
                        ir = ridx[j]
                        nko[ir % 2] += 16
                        OEV[j] = (dm[14 + ir % 2], nko[ir % 2])
            MEV = {}
            SEV1 = {}
            SWP1 = {}
            n = c1.n
            pendq = []
            for j in jobs:
                while pendq and (len(pendq) >= 2 or j[0] == "V"
                                 or j == ("K", 1, 0, 0)):
                    jp = pendq.pop(0)
                    n += 1
                    SWP1[jp] = n
                    n += 1
                    SEV1[jp] = n
                n += 1
                MEV[j] = n
                if j[0] in ("K", "Q"):
                    pendq.append(j)
            for jp in pendq:
                n += 1
                SWP1[jp] = n
                n += 1
                SEV1[jp] = n
            QEV = {}
            QE2 = {}
            n = c2.n
            for j in jobs:
                n += 1
                QEV[j] = n
                if j[0] in ("K", "Q"):
                    n += 1
                    QE2[j] = n
            REV = {}
            n = c3.n
            for j in rjobs:
                n += 1
                REV[j] = n
            PEV = {}
            n = c6.n
            for j in rjobs:
                n += 1
                PEV[j] = n

            def free_ev(j):
                if j[0] == "V":
                    return (c2, QEV[j])
                return (c6, PEV[j])

            def job_pos(j):
                kind, H, a, b = j
                if kind == "V":
                    return None
                return (H * TH + b * 512)

            with nc.Block() as blk:
                @blk.gpsimd
                def _(g):
                    g.wait_ge(ph.h, ph_in)
                    for l in ldorder:
                        kind, H, a = l
                        if kind == "V":
                            i = vli[l]
                            if i >= 2:
                                pl = vloads[i - 2]
                                g.wait_ge(c1.h, MEV[last_job_of_load(pl)])
                            srcap = wv.ap()[:, a * 256:(a + 1) * 256]
                            WEV[l][0].inc(g.dma_start(
                                sb3(wvb[i % 2][:, :], 32), dr3(srcap)), 16)
                        else:
                            i = kqli[l]
                            if i >= 2:
                                pl = kqloads[i - 2]
                                g.wait_ge(c1.h, MEV[last_job_of_load(pl)])
                            wsrc = wk if kind == "K" else wq
                            srcap = wsrc.ap()[:, a * 128:(a + 1) * 128]
                            WEV[l][0].inc(g.dma_start(
                                sb3(wkq[i % 2][:, :], 32), dr3(srcap)), 16)
                    for i_ in (9, 10, 11, 12):
                        g.wait_ge(dm[i_].h, dm[i_].n)
                    ph.inc(g.nop())

                @blk.sync
                def _(sy):
                    sy.wait_ge(ph.h, ph_in)
                    dm[5].inc(sy.dma_start(csa_sb[:, :], csa.ap()), 16)
                    dm[5].inc(sy.dma_start(csb_sb[:, :], csb.ap()), 16)
                    dm[5].inc(sy.dma_start(swp_sb[:, :], swp.ap()), 16)
                    for H in (0, 1):
                        if H == 1:
                            sy.wait_ge(c1.h, MEV[("Q", 0, NHC - 1, 3)])
                        wt(sy, GEV["xbf1" if H else "xbf0"])
                        for dc in range(32):
                            dm[13].inc(sy.dma_start(
                                xT_sb[:, dc * TH:(dc + 1) * TH],
                                xbf.ap()[H * TH:(H + 1) * TH,
                                         dc * 128:(dc + 1) * 128],
                                transpose=True), 16)
                        for j in [jj for jj in jobs if jj[1] == H]:
                            kind, _, a, b = j
                            if kind == "V":
                                iv = vidx[j]
                                sy.wait_ge(c2.h, QEV[j])
                                gt = H * TH + b * 128
                                OEV[j][0].inc(sy.dma_start(
                                    vS.ap()[gt:gt + 128,
                                            a * 256:(a + 1) * 256],
                                    vob[iv % 2][:, :]), 16)
                            else:
                                ir = ridx[j]
                                sy.wait_ge(c6.h, PEV[j])
                                ts = H * TH + b * 512
                                dst = kT if kind == "K" else qT
                                OEV[j][0].inc(sy.dma_start(
                                    dst.ap()[a * 128:(a + 1) * 128,
                                             ts:ts + 512],
                                    kob[ir % 2][:, :]), 16)
                    for i_ in (5, 13, 14, 15, 16, 17):
                        sy.wait_ge(dm[i_].h, dm[i_].n)
                    ph.inc(sy.nop())

                @blk.tensor
                def _(t):
                    t.wait_ge(ph.h, ph_in)
                    pendq = []

                    def flush_stats(t, force=False):
                        if not pendq or (not force and len(pendq) < 2):
                            return
                        jp = pendq.pop(0)
                        ip = ridx[jp]
                        wt(t, CSL)
                        t.wait_ge(c2.h, QE2[jp])
                        if ip >= 2:
                            t.wait_ge(c6.h, PEV[rjobs[ip - 2]])
                        c1.inc(t.matmul(psw[ip % 2][:, :], swp_sb[:, :],
                                        qeb[ip % 2][:, :],
                                        start=True, stop=True))
                        if ip >= 2:
                            t.wait_ge(c3.h, REV[rjobs[ip - 2]])
                        c1.inc(t.matmul(pst[ip % 2][:, :], ones_sb[:, :],
                                        sqb[ip % 2][:, :],
                                        start=True, stop=True))

                    for j in jobs:
                        kind, H, a, b = j
                        i = jidx[j]
                        drain = (kind == "V") or j == ("K", 1, 0, 0)
                        flush_stats(t, force=drain)
                        if drain:
                            flush_stats(t, force=True)
                        if (kind, b) in (("K", 0), ("V", 0), ("Q", 0)) and \
                                a == 0:
                            wt(t, XEV[H])
                        if b == 0:
                            wt(t, WEV[(kind, H, a)])
                        if i >= 3:
                            sem, val = free_ev(jobs[i - 3])
                            t.wait_ge(sem.h, val)
                        pg = psk[i % 3]
                        if kind == "V":
                            ld = vli[(kind, H, a)]
                            wb = wvb[ld % 2]
                            for dc in range(32):
                                mm = t.matmul(
                                    pg[:, 0:256],
                                    xT_sb[:, dc * TH + b * 128:
                                          dc * TH + b * 128 + 128],
                                    wb[:, dc * 256:(dc + 1) * 256],
                                    start=(dc == 0), stop=(dc == 31))
                            c1.inc(mm)
                        else:
                            ld = kqli[(kind, H, a)]
                            wb = wkq[ld % 2]
                            for dc in range(32):
                                mm = t.matmul(
                                    pg[:, :],
                                    wb[:, dc * 128:(dc + 1) * 128],
                                    xT_sb[:, dc * TH + b * 512:
                                          dc * TH + b * 512 + 512],
                                    start=(dc == 0), stop=(dc == 31))
                            c1.inc(mm)
                        if kind in ("K", "Q"):
                            pendq.append(j)
                    while pendq:
                        flush_stats(t, force=True)
                    endsec(t, c1)

                @blk.scalar
                def _(sc):
                    sc.wait_ge(ph.h, ph_in)
                    for j in jobs:
                        kind, H, a, b = j
                        i = jidx[j]
                        sc.wait_ge(c1.h, MEV[j])
                        pg = psk[i % 3]
                        if kind == "V":
                            iv = vidx[j]
                            if iv >= 2:
                                wt(sc, OEV[vjobs[iv - 2]])
                            c2.inc(sc.copy(vob[iv % 2][:, :], pg[:, 0:256]))
                        else:
                            ir = ridx[j]
                            if ir >= 2:
                                sc.wait_ge(c1.h, SEV1[rjobs[ir - 2]])
                            c2.inc(sc.activation(sqb[ir % 2][:, :], pg[:, :],
                                                 AF.Square))
                            if ir >= 2:
                                sc.wait_ge(c1.h, SWP1[rjobs[ir - 2]])
                            c2.inc(sc.copy(qeb[ir % 2][:, :], pg[:, :]))
                    endsec(sc, c2)

                @blk.vector
                def _(v):
                    v.wait_ge(ph.h, ph_in)
                    for j in rjobs:
                        kind, H, a, b = j
                        ir = ridx[j]
                        i = jidx[j]
                        pg = psk[i % 3]
                        v.wait_ge(c1.h, SEV1[j])
                        if kind == "Q":
                            col = a * 8 + H * 4 + b
                            c3.inc(v.reduce_max(qmax2_sb[:, col:col + 1],
                                                pst[ir % 2][:, :], axis=AX.X))
                        else:
                            strip = H * 4 + b
                            if strip == 0:
                                c3.inc(v.reduce_max(
                                    kmax2_sb[:, a:a + 1],
                                    pst[ir % 2][:, :], axis=AX.X))
                            else:
                                vch.inc(v.reduce_max(
                                    stat_sb[:, ir % 2:ir % 2 + 1],
                                    pst[ir % 2][:, :], axis=AX.X))
                                v.wait_ge(vch.h, vch.n)
                                c3.inc(v.tensor_max(
                                    kmax2_sb[:, a:a + 1],
                                    kmax2_sb[:, a:a + 1],
                                    stat_sb[:, ir % 2:ir % 2 + 1]))
                        wt(v, CSL)
                        v.wait_ge(c2.h, QEV[j])
                        if ir >= 2:
                            wt(v, OEV[rjobs[ir - 2]])
                        ts = H * TH + b * 512
                        r = rp0
                        ko = kob[ir % 2]
                        v.wait_ge(vch.h, vch.n)
                        v.wait_ge(c6.h, c6.n)
                        v.tensor_mul(r[:, 0:512], pg[:, :],
                                     csa_sb[:, ts:ts + 512])
                        vch.inc(v.tensor_mul(r[:, 512:1024], psw[ir % 2][:, :],
                                             csb_sb[:, ts:ts + 512]))
                        v.wait_ge(vch.h, vch.n)
                        c6.inc(v.tensor_add(ko[:, :], r[:, 0:512],
                                            r[:, 512:1024]))
                    v.wait_ge(vch.h, vch.n)
                    for gk in range(NGC):
                        mmv = v.tensor_scalar_mul(
                            kmc_sb[:, gk * 32:(gk + 1) * 32],
                            onesf_sb[:, 0:32], kmax2_sb[:, gk:gk + 1])
                    vch.inc(mmv)
                    v.wait_ge(vch.h, vch.n)
                    c6.inc(v.tensor_mul(kmc_sb[:, :], kmc_sb[:, :],
                                        qmax2_sb[:, :]))
                    endsec(v, c6)

            assert c2.n == QE2[jobs[-1]]
            assert ph.n == ph_in + 5

            ph_in2 = ph.n
            with nc.Block() as blk:
                @blk.scalar
                def _(sc):
                    sc.wait_ge(ph.h, ph_in2)
                    c2.inc(sc.activation(mneg_sb[:, :], kmc_sb[:, :],
                                         AF.Sqrt))
                    endsec(sc, c2)

                @blk.vector
                def _(v):
                    v.wait_ge(c2.h, c2.n)
                    c6.inc(v.tensor_scalar_mul(mneg_sb[:, :], mneg_sb[:, :],
                                               -SCALE))
                    endsec(v, c6)
            ph_p1 = ph.n

        if _KPH <= 1:
            _fin()
            return nc

        with ExitStack() as P:
            maskp_sb = P.enter_context(nc.sbuf_tensor("maskp_sb", [128, 2048], BF16))
            qs0 = P.enter_context(nc.sbuf_tensor("qs0", [128, NHC * 512], BF16))
            qs1 = P.enter_context(nc.sbuf_tensor("qs1", [128, NHC * 512], BF16))
            ks0 = P.enter_context(nc.sbuf_tensor("ks0", [128, T], BF16))
            ks1 = P.enter_context(nc.sbuf_tensor("ks1", [128, T], BF16))
            vs0 = P.enter_context(nc.sbuf_tensor("vs0", [128, T], BF16))
            vs1 = P.enter_context(nc.sbuf_tensor("vs1", [128, T], BF16))
            pb0 = P.enter_context(nc.sbuf_tensor("pb0", [128, 1024], BF16))
            pb1 = P.enter_context(nc.sbuf_tensor("pb1", [128, 1024], BF16))
            rln_sb = P.enter_context(nc.sbuf_tensor("rln_sb", [128, 512], F32))
            ri0 = P.enter_context(nc.sbuf_tensor("ri0", [128, 512], F32))
            ri1 = P.enter_context(nc.sbuf_tensor("ri1", [128, 512], F32))
            cx0 = P.enter_context(nc.sbuf_tensor("cx0", [128, 512], BF16))
            cx1 = P.enter_context(nc.sbuf_tensor("cx1", [128, 512], BF16))
            psS0 = P.enter_context(nc.psum_tensor("psS0", [128, 1024], F32))
            psS1 = P.enter_context(nc.psum_tensor("psS1", [128, 1024], F32))
            psc0 = P.enter_context(nc.psum_tensor("psc0", [128, 512], F32))
            psc1 = P.enter_context(nc.psum_tensor("psc1", [128, 512], F32))
            psr0 = P.enter_context(nc.psum_tensor("psr0", [128, 512], F32))
            psr1 = P.enter_context(nc.psum_tensor("psr1", [128, 512], F32))
            qsb = [qs0, qs1]
            ksb = [ks0, ks1]
            vsb = [vs0, vs1]
            pbb = [pb0, pb1]
            rib = [ri0, ri1]
            cxb = [cx0, cx1]
            psS = [psS0, psS1]
            psc = [psc0, psc1]
            psr = [psr0, psr1]

            ph_in = ph.n

            groups = [(st, g) for st in range(8) for g in range(NGC)][:_KP3G]
            giof = {sg: i for i, sg in enumerate(groups)}
            heads = []
            for (st, g) in groups:
                for hh in range(4):
                    heads.append((st, g, g * 4 + hh))
            pairs = []
            hinfo = []
            for m, (st, g, h) in enumerate(heads):
                npz = 2 * (st + 1)
                p0i = len(pairs)
                for pi in range(npz):
                    pairs.append(dict(
                        m=m, st=st, g=g, h=h, pi=pi, c0=2 * pi,
                        masked=(pi >= npz - 2), j2=pi - (npz - 2),
                        first=(pi == 0), last=(pi == npz - 1)))
                hinfo.append(dict(st=st, g=g, h=h, pfirst=p0i,
                                  plast=len(pairs) - 1))
            NP = len(pairs)

            MKL = (dm[5], dm[5].n + 16)
            QL = {}
            KL = {}
            VL = {}
            CST = {}
            nq = [dm[18].n, dm[19].n]
            nk = [dm[6].n, dm[7].n]
            nvv = [dm[16].n, dm[17].n]
            nct = [dm[14].n, dm[15].n]

            def _ql(st):
                nq[st % 2] += 16
                QL[st] = (dm[18 + st % 2], nq[st % 2])

            def _kvl(gi):
                nk[gi % 2] += 16
                KL[gi] = (dm[6 + gi % 2], nk[gi % 2])
                nvv[gi % 2] += 16
                VL[gi] = (dm[16 + gi % 2], nvv[gi % 2])

            _ql(0)
            _kvl(0)
            for gi, (st, g) in enumerate(groups):
                if gi + 1 < len(groups):
                    _kvl(gi + 1)
                if g == NGC - 1 and st + 1 < 8:
                    _ql(st + 1)
                for hh in range(4):
                    m = gi * 4 + hh
                    nct[m % 2] += 16
                    CST[m] = (dm[14 + m % 2], nct[m % 2])
            s1b, s4b = c1.n, c4.n
            SC = {p: s1b + p + 1 for p in range(NP)}
            CT = {p: s4b + p + 1 for p in range(NP)}
            s2b, s5b = c2.n, c5.n
            EX = {p: s2b + p + 1 for p in range(NP)}
            RI = {m: s5b + m + 1 for m in range(len(heads))}
            s3b, s6b = c3.n, c6.n
            MK = {}
            nmk = s3b
            for p, pr in enumerate(pairs):
                if pr["masked"] and not _KNOMASK:
                    nmk += 1
                    MK[p] = nmk
            EVC = {m: s6b + m + 1 for m in range(len(heads))}

            def span_of(st):
                return (st + 1) * 512

            with nc.Block() as blk:
                @blk.sync
                def _(sy):
                    sy.wait_ge(ph.h, ph_in)
                    dm[5].inc(sy.dma_start(maskp_sb[:, :], maskp.ap()), 16)
                    QL[0][0].inc(sy.dma_start(
                        sb3(qsb[0][:, :], NHC),
                        dr3(qT.ap()[:, 0:512])), 16)
                    KL[0][0].inc(sy.dma_start(ksb[0][:, 0:512],
                                              kT.ap()[0:128, 0:512]), 16)
                    VL[0][0].inc(sy.dma_start(
                        sb3(vsb[0][:, 0:512], 4),
                        vS.ap()[0:512, 0:128].rearrange(
                            "(c p) w -> p c w", p=128)), 16)
                    for gi, (st, g) in enumerate(groups):
                        if gi + 1 < len(groups):
                            st2, g2 = groups[gi + 1]
                            sp2 = span_of(st2)
                            if gi >= 1:
                                sy.wait_ge(c4.h, CT[hinfo[gi * 4 - 1]["plast"]])
                            KL[gi + 1][0].inc(sy.dma_start(
                                ksb[(gi + 1) % 2][:, 0:sp2],
                                kT.ap()[g2 * 128:(g2 + 1) * 128, 0:sp2]), 16)
                            VL[gi + 1][0].inc(sy.dma_start(
                                sb3(vsb[(gi + 1) % 2][:, 0:sp2], sp2 // 128),
                                vS.ap()[0:sp2, g2 * 128:(g2 + 1) * 128]
                                .rearrange("(c p) w -> p c w", p=128)), 16)
                        if g == NGC - 1 and st + 1 < 8:
                            if st >= 1:
                                last_m_prev_st = (giof[(st - 1, NGC - 1)]) * 4 + 3
                                sy.wait_ge(c4.h, CT[hinfo[last_m_prev_st]["plast"]])
                            QL[st + 1][0].inc(sy.dma_start(
                                sb3(qsb[(st + 1) % 2][:, :], NHC),
                                dr3(qT.ap()[:, (st + 1) * 512:(st + 2) * 512])),
                                16)
                        for hh in range(4):
                            m = gi * 4 + hh
                            st_, _, h_ = heads[m]
                            sy.wait_ge(c6.h, EVC[m])
                            dst = ctxq[st_ // 2]
                            cofs = (st_ % 2) * 512
                            CST[m][0].inc(sy.dma_start(
                                dst.ap()[h_ * 128:(h_ + 1) * 128,
                                         cofs:cofs + 512],
                                cxb[m % 2][:, :]), 16)
                    for i_ in (5, 6, 7, 14, 15, 16, 17, 18, 19):
                        sy.wait_ge(dm[i_].h, dm[i_].n)
                    ph.inc(sy.nop())

                @blk.gpsimd
                def _(g):
                    g.wait_ge(ph.h, ph_in)
                    ccq = [cc1, cc2, cc3, cc4]
                    for q in range(4):
                        wt(g, CST[q * 32 + 30])
                        wt(g, CST[q * 32 + 31])
                        ccq[q].inc(g.collective_compute(
                            "AllGather", ALU.bypass,
                            replica_groups=[[0, 1], [2, 3], [4, 5], [6, 7]],
                            ins=[ctxq[q].ap().opt()],
                            outs=[ctxAGh[q % 2].ap()
                                  [(q // 2) * 2 * NHC * HD:
                                   (q // 2 + 1) * 2 * NHC * HD, :].opt()]))
                    wt(g, GEV["consts"])
                    g.wait_ge(cc1.h, 1)
                    g.wait_ge(cc3.h, 1)
                    with g.register("ro") as ro:
                        g.reg_load(ro, agoff_sb[0:1, 0:1])
                        dm[20].inc(g.dma_start(
                            ctxFh[0].ap(),
                            bass.AP(ctxAGh[0], ro,
                                    [[TH // 2, D], [1, TH // 2]])), 16)
                    endsec(g)

                @blk.tensor
                def _(t):
                    t.wait_ge(ph.h, ph_in)

                    def consume(q):
                        pq = pairs[q]
                        mq = pq["m"]
                        gi = giof[(pq["st"], pq["g"])]
                        if pq["masked"] and not _KNOMASK:
                            t.wait_ge(c3.h, MK[q])
                        else:
                            t.wait_ge(c2.h, EX[q])
                        for half in (0, 1):
                            c = pq["c0"] + half
                            stt = pq["first"] and half == 0
                            spp = pq["last"] and half == 1
                            mm = t.matmul(
                                psc[mq % 2][:, :],
                                vsb[gi % 2][:, c * 128:(c + 1) * 128],
                                pbb[q % 2][:, half * 512:(half + 1) * 512],
                                start=stt, stop=spp)
                            if not _KNOROW:
                                mm = t.matmul(
                                    psr[mq % 2][:, :], ones_sb[:, :],
                                    pbb[q % 2][:, half * 512:(half + 1) * 512],
                                    start=stt, stop=spp)
                        c4.inc(mm)

                    for p, pr in enumerate(pairs):
                        m = pr["m"]
                        gi = giof[(pr["st"], pr["g"])]
                        if pr["first"]:
                            if pr["h"] == pr["g"] * 4:
                                wt(t, KL[gi])
                                wt(t, VL[gi])
                                if pr["g"] == 0:
                                    wt(t, QL[pr["st"]])
                            if m >= 2:
                                t.wait_ge(c6.h, EVC[m - 2])
                                t.wait_ge(c5.h, RI[m - 2])
                        for half in (0, 1):
                            c = pr["c0"] + half
                            mm = t.matmul(
                                psS[p % 2][:, half * 512:(half + 1) * 512],
                                ksb[gi % 2][:, c * 128:(c + 1) * 128],
                                qsb[pr["st"] % 2][:, pr["h"] * 512:
                                                  (pr["h"] + 1) * 512],
                                start=True, stop=True)
                        c1.inc(mm)
                        if p >= 1:
                            consume(p - 1)
                    consume(NP - 1)
                    endsec(t, c4)

                @blk.scalar
                def _(sc):
                    sc.wait_ge(ph.h, ph_in)
                    pend_epi = None
                    for p, pr in enumerate(pairs):
                        m = pr["m"]
                        col = pr["h"] * 8 + pr["st"]
                        sc.wait_ge(c1.h, SC[p])
                        if p >= 2:
                            sc.wait_ge(c4.h, CT[p - 2])
                        if _KNOEXP2:
                            sc.activation(pbb[p % 2][:, 0:512],
                                          psS[p % 2][:, 0:512], AF.Exp,
                                          bias=mneg_sb[:, col:col + 1],
                                          scale=SCALE)
                            c2.inc(sc.activation(pbb[p % 2][:, 512:1024],
                                                 psS[p % 2][:, 512:1024],
                                                 AF.Exp,
                                                 bias=mneg_sb[:, col:col + 1],
                                                 scale=SCALE))
                        else:
                            c2.inc(sc.activation(pbb[p % 2][:, :],
                                                 psS[p % 2][:, :], AF.Exp,
                                                 bias=mneg_sb[:, col:col + 1],
                                                 scale=SCALE))
                        if pend_epi is not None:
                            me, pe = pend_epi
                            pend_epi = None
                            sc.wait_ge(c4.h, CT[pe])
                            if me >= 2:
                                sc.wait_ge(c6.h, EVC[me - 2])
                                sc.wait_ge(c5.h, RI[me - 2])
                            if _KNOEPI:
                                c5.inc(sc.memzero(rib[me % 2][:, :]))
                            else:
                                sch.inc(sc.activation(rln_sb[:, :],
                                                      psr[me % 2][:, :],
                                                      AF.Ln))
                                sc.wait_ge(sch.h, sch.n)
                                c5.inc(sc.activation(
                                    rib[me % 2][:, :], rln_sb[:, :],
                                    AF.Exp, scale=-1.0))
                        if pr["last"]:
                            pend_epi = (m, p)
                    me, pe = pend_epi
                    sc.wait_ge(c4.h, CT[pe])
                    if me >= 2:
                        sc.wait_ge(c6.h, EVC[me - 2])
                        sc.wait_ge(c5.h, RI[me - 2])
                    if _KNOEPI:
                        c5.inc(sc.memzero(rib[me % 2][:, :]))
                    else:
                        sch.inc(sc.activation(rln_sb[:, :],
                                              psr[me % 2][:, :], AF.Ln))
                        sc.wait_ge(sch.h, sch.n)
                        c5.inc(sc.activation(rib[me % 2][:, :], rln_sb[:, :],
                                             AF.Exp, scale=-1.0))
                    endsec(sc, c5)

                @blk.vector
                def _(v):
                    v.wait_ge(ph.h, ph_in)
                    wt(v, MKL)
                    for p, pr in enumerate(pairs):
                        if pr["masked"] and not _KNOMASK:
                            v.wait_ge(c2.h, EX[p])
                            j2 = pr["j2"]
                            c3.inc(v.tensor_mul(
                                pbb[p % 2][:, :], pbb[p % 2][:, :],
                                maskp_sb[:, j2 * 1024:(j2 + 1) * 1024]))
                        if pr["last"]:
                            m = pr["m"]
                            v.wait_ge(c5.h, RI[m])
                            if m >= 2:
                                wt(v, CST[m - 2])
                            c6.inc(v.tensor_mul(cxb[m % 2][:, :],
                                                psc[m % 2][:, :],
                                                rib[m % 2][:, :]))
                    endsec(v, c6)

            assert c1.n == SC[NP - 1]
            assert c4.n == CT[NP - 1]
            assert c2.n == EX[NP - 1]
            assert ph.n == ph_in + 5
            cpA_ev = (dm[20], dm[20].n)

        if _KPH <= 3:
            _fin()
            return nc

        if _KPH <= 4:
            _fin()
            return nc

        with ExitStack() as P:
            cxs = P.enter_context(nc.sbuf_tensor("cxs", [128, 32 * 1024], BF16))
            woc0 = P.enter_context(nc.sbuf_tensor("woc0", [128, 32 * 512], BF16))
            woc1 = P.enter_context(nc.sbuf_tensor("woc1", [128, 32 * 512], BF16))
            xhb0 = P.enter_context(nc.sbuf_tensor("xhb0", [128, 8 * 512], F32))
            xhb1 = P.enter_context(nc.sbuf_tensor("xhb1", [128, 8 * 512], F32))
            stg0 = P.enter_context(nc.sbuf_tensor("stg0", [128, 8 * 512], F32))
            stg1 = P.enter_context(nc.sbuf_tensor("stg1", [128, 8 * 512], F32))
            trash_sb = P.enter_context(nc.sbuf_tensor("trash_sb", [128, 512], F32))
            pso0 = P.enter_context(nc.psum_tensor("pso0", [128, 512], F32))
            pso1 = P.enter_context(nc.psum_tensor("pso1", [128, 512], F32))
            pso2 = P.enter_context(nc.psum_tensor("pso2", [128, 512], F32))
            woc = [woc0, woc1]
            xhb = [xhb0, xhb1]
            stg = [stg0, stg1]
            pso = [pso0, pso1, pso2]

            ph_in = ph.n
            chunks = [(p_, oc) for p_ in (0, 1) for oc in range(8)]
            jobs4 = [(p_, oc, tt) for (p_, oc) in chunks for tt in range(8)]
            j4 = {jj: i for i, jj in enumerate(jobs4)}
            ci_of = {ch: i for i, ch in enumerate(chunks)}

            WOE = {}
            nwo = [dm[9].n, dm[10].n]
            for ch in chunks:
                ci = ci_of[ch]
                nwo[ci % 2] += 16
                WOE[ch] = (dm[9 + ci % 2], nwo[ci % 2])
            CXE = {}
            XHE = {}
            YPE = {}
            ncx = dm[13].n
            nxh = [dm[16].n, dm[17].n]
            nyp = [dm[14].n, dm[15].n]
            for p_ in (0, 1):
                ncx += 16
                CXE[p_] = (dm[13], ncx)
                for oc in range(8):
                    ci = ci_of[(p_, oc)]
                    nxh[ci % 2] += 16
                    XHE[(p_, oc)] = (dm[16 + ci % 2], nxh[ci % 2])
            for ch in chunks:
                ci = ci_of[ch]
                nyp[ci % 2] += 16
                YPE[ch] = (dm[14 + ci % 2], nyp[ci % 2])
            cpB_ev = (dm[21], dm[21].n + 16)
            MM4 = {}
            n = c1.n
            for jj in jobs4:
                n += 1
                MM4[jj] = n
            EV4 = {}
            n = c6.n
            for jj in jobs4:
                n += 1
                EV4[jj] = n
            SQ4 = {}
            n = c2.n
            for jj in jobs4:
                n += 1
                SQ4[jj] = n

            with nc.Block() as blk:
                @blk.gpsimd
                def _(g):
                    g.wait_ge(ph.h, ph_in)
                    for ch in chunks:
                        p_, oc = ch
                        ci = ci_of[ch]
                        if ci >= 2:
                            g.wait_ge(c1.h, MM4[chunks[ci - 2] + (7,)])
                        WOE[ch][0].inc(g.dma_start(
                            sb3(woc[ci % 2][:, :], 32),
                            dr3(wo.ap()[:, oc * 512:(oc + 1) * 512])), 16)
                        if ci == 3:
                            g.wait_ge(cc2.h, 1)
                            g.wait_ge(cc4.h, 1)
                            with g.register("ro2") as ro2:
                                g.reg_load(ro2, agoff_sb[0:1, 0:1])
                                dm[21].inc(g.dma_start(
                                    ctxFh[1].ap(),
                                    bass.AP(ctxAGh[1], ro2,
                                            [[TH // 2, D], [1, TH // 2]])),
                                    16)
                    for i_ in (9, 10):
                        g.wait_ge(dm[i_].h, dm[i_].n)
                    ph.inc(g.nop())

                @blk.sync
                def _(sy):
                    sy.wait_ge(ph.h, ph_in)
                    for p_ in (0, 1):
                        if p_ == 0:
                            wt(sy, cpA_ev)
                        else:
                            sy.wait_ge(c1.h, MM4[(0, 7, 7)])
                            wt(sy, cpB_ev)
                        dm[13].inc(sy.dma_start(
                            sb3(cxs[:, :], 32),
                            dr3(ctxFh[p_].ap())), 16)
                        for oc in range(8):
                            ci = ci_of[(p_, oc)]
                            if ci >= 2:
                                sy.wait_ge(c6.h, EV4[chunks[ci - 2] + (7,)])
                            XHE[(p_, oc)][0].inc(sy.dma_start(
                                sb3(xhb[ci % 2][:, :], 8),
                                dr3(xh.ap()[p_ * 1024:(p_ + 1) * 1024,
                                            oc * 512:(oc + 1) * 512])), 16)
                            if ci >= 1:
                                pch = chunks[ci - 1]
                                sy.wait_ge(c6.h, EV4[pch + (7,)])
                                YPE[pch][0].inc(sy.dma_start(
                                    dr3(ypre.ap()[pch[0] * 1024:
                                                  pch[0] * 1024 + 1024,
                                                  pch[1] * 512:
                                                  pch[1] * 512 + 512]),
                                    sb3(stg[(ci - 1) % 2][:, :], 8)), 16)
                    pch = chunks[-1]
                    sy.wait_ge(c6.h, EV4[pch + (7,)])
                    YPE[pch][0].inc(sy.dma_start(
                        dr3(ypre.ap()[pch[0] * 1024:pch[0] * 1024 + 1024,
                                      pch[1] * 512:pch[1] * 512 + 512]),
                        sb3(stg[(len(chunks) - 1) % 2][:, :], 8)), 16)
                    for i_ in (13, 14, 15, 16, 17):
                        sy.wait_ge(dm[i_].h, dm[i_].n)
                    ph.inc(sy.nop())

                @blk.tensor
                def _(t):
                    t.wait_ge(ph.h, ph_in)
                    for jj in jobs4:
                        p_, oc, tt = jj
                        ci = ci_of[(p_, oc)]
                        i = j4[jj]
                        if tt == 0:
                            wt(t, CXE[p_])
                            wt(t, WOE[(p_, oc)])
                        if i >= 3:
                            t.wait_ge(c6.h, EV4[jobs4[i - 3]])
                        for dc in range(32):
                            mm = t.matmul(
                                pso[i % 3][:, :],
                                cxs[:, dc * 1024 + tt * 128:
                                    dc * 1024 + tt * 128 + 128],
                                woc[ci % 2][:, dc * 512:(dc + 1) * 512],
                                start=(dc == 0), stop=(dc == 31))
                        c1.inc(mm)
                    endsec(t, c1)

                @blk.vector
                def _(v):
                    v.wait_ge(ph.h, ph_in)
                    for jj in jobs4:
                        p_, oc, tt = jj
                        ci = ci_of[(p_, oc)]
                        i = j4[jj]
                        v.wait_ge(c1.h, MM4[jj])
                        if tt == 0:
                            wt(v, XHE[(p_, oc)])
                            if ci >= 2:
                                wt(v, YPE[chunks[ci - 2]])
                                v.wait_ge(c2.h, SQ4[chunks[ci - 2] + (7,)])
                        c6.inc(v.scalar_tensor_tensor(
                            stg[ci % 2][:, tt * 512:(tt + 1) * 512],
                            pso[i % 3][:, :], 1.0,
                            xhb[ci % 2][:, tt * 512:(tt + 1) * 512],
                            ALU.mult, ALU.add,
                            accum_out=stat_sb[:, 2 + i % 2:3 + i % 2]))
                        col = p_ * 8 + tt
                        v.wait_ge(c6.h, EV4[jj])
                        v.wait_ge(vch.h, vch.n)
                        if oc == 0:
                            vch.inc(v.tensor_copy(
                                ysum_sb[:, col:col + 1],
                                stat_sb[:, 2 + i % 2:3 + i % 2]))
                        else:
                            vch.inc(v.tensor_add(
                                ysum_sb[:, col:col + 1],
                                ysum_sb[:, col:col + 1],
                                stat_sb[:, 2 + i % 2:3 + i % 2]))
                        if i >= 1:
                            pj = jobs4[i - 1]
                            pcol = pj[0] * 8 + pj[2]
                            v.wait_ge(c2.h, SQ4[pj])
                            v.wait_ge(vch.h, vch.n)
                            if pj[1] == 0:
                                vch.inc(v.tensor_copy(
                                    ysq_sb[:, pcol:pcol + 1],
                                    stat_sb[:, 4 + (i - 1) % 2:
                                            5 + (i - 1) % 2]))
                            else:
                                vch.inc(v.tensor_add(
                                    ysq_sb[:, pcol:pcol + 1],
                                    ysq_sb[:, pcol:pcol + 1],
                                    stat_sb[:, 4 + (i - 1) % 2:
                                            5 + (i - 1) % 2]))
                    pj = jobs4[-1]
                    i = len(jobs4)
                    pcol = pj[0] * 8 + pj[2]
                    v.wait_ge(c2.h, SQ4[pj])
                    v.wait_ge(vch.h, vch.n)
                    vch.inc(v.tensor_add(
                        ysq_sb[:, pcol:pcol + 1],
                        ysq_sb[:, pcol:pcol + 1],
                        stat_sb[:, 4 + (i - 1) % 2:5 + (i - 1) % 2]))
                    endsec(v, c6)

                @blk.scalar
                def _(sc):
                    sc.wait_ge(ph.h, ph_in)
                    for jj in jobs4:
                        p_, oc, tt = jj
                        ci = ci_of[(p_, oc)]
                        i = j4[jj]
                        sc.wait_ge(c6.h, EV4[jj])
                        if i >= 1:
                            sc.wait_ge(c2.h, SQ4[jobs4[i - 1]])
                        c2.inc(sc.activation(
                            trash_sb[:, :],
                            stg[ci % 2][:, tt * 512:(tt + 1) * 512],
                            AF.Square,
                            accum_out=stat_sb[:, 4 + i % 2:5 + i % 2]))
                    endsec(sc, c2)

            assert c1.n == MM4[jobs4[-1]]
            assert ph.n == ph_in + 5

        if _KPH <= 5:
            _fin()
            return nc

        with ExitStack() as P:
            yp0 = P.enter_context(nc.sbuf_tensor("yp0", [128, D], F32))
            yp1 = P.enter_context(nc.sbuf_tensor("yp1", [128, D], F32))
            yo0 = P.enter_context(nc.sbuf_tensor("yo0", [128, D], F32))
            yo1 = P.enter_context(nc.sbuf_tensor("yo1", [128, D], F32))
            ypb = [yp0, yp1]
            yob = [yo0, yo1]

            ph_in = ph.n
            LY = {}
            SY = {}
            GY = {}
            nly = [dm[6].n, dm[7].n]
            nsy = [dm[14].n, dm[15].n]
            ngy = [dm[20].n, dm[21].n]
            for tt in range(16):
                nly[tt % 2] += 16
                LY[tt] = (dm[6 + tt % 2], nly[tt % 2])
                nsy[tt % 2] += 16
                SY[tt] = (dm[14 + tt % 2], nsy[tt % 2])
                ngy[tt % 2] += 16
                GY[tt] = (dm[20 + tt % 2], ngy[tt % 2])
            VA = {tt: c3.n + tt + 1 for tt in range(16)}
            RS = {tt: c2.n + tt + 1 for tt in range(16)}
            NM = {tt: c6.n + tt + 1 for tt in range(16)}

            with nc.Block() as blk:
                @blk.sync
                def _(sy):
                    sy.wait_ge(ph.h, ph_in)
                    for tt in range(16):
                        if tt >= 2:
                            sy.wait_ge(c6.h, NM[tt - 2])
                        LY[tt][0].inc(sy.dma_start(
                            ypb[tt % 2][:, :],
                            ypre.ap()[tt * 128:(tt + 1) * 128, :]), 16)
                        sy.wait_ge(c6.h, NM[tt])
                        SY[tt][0].inc(sy.dma_start(
                            y32.ap()[tt * 128:(tt + 1) * 128, :],
                            yob[tt % 2][:, :]), 16)
                    for i_ in (6, 7, 14, 15):
                        sy.wait_ge(dm[i_].h, dm[i_].n)
                    ph.inc(sy.nop())

                @blk.gpsimd
                def _(g):
                    g.wait_ge(ph.h, ph_in)
                    for tt in range(16):
                        g.wait_ge(c6.h, NM[tt])
                        GY[tt][0].inc(g.dma_start(
                            ybf.ap()[tt * 128:(tt + 1) * 128, :],
                            yob[tt % 2][:, :]), 16)
                    for i_ in (20, 21):
                        g.wait_ge(dm[i_].h, dm[i_].n)
                    ph.inc(g.nop())

                @blk.vector
                def _(v):
                    v.wait_ge(ph.h, ph_in)
                    for tt in range(16):
                        p = tt % 2
                        if tt >= 2:
                            v.wait_ge(c2.h, RS[tt - 2])
                            v.wait_ge(c6.h, NM[tt - 2])
                        v.wait_ge(vch.h, vch.n)
                        vch.inc(v.tensor_scalar_mul(
                            stat_sb[:, p:p + 1],
                            ysum_sb[:, tt:tt + 1], 1.0 / D))
                        v.wait_ge(vch.h, vch.n)
                        vch.inc(v.tensor_mul(stat_sb[:, 6 + p:7 + p],
                                             stat_sb[:, p:p + 1],
                                             stat_sb[:, p:p + 1]))
                        v.wait_ge(vch.h, vch.n)
                        c3.inc(v.scalar_tensor_tensor(
                            stat_sb[:, 2 + p:3 + p],
                            ysq_sb[:, tt:tt + 1], 1.0 / D,
                            stat_sb[:, 6 + p:7 + p],
                            ALU.mult, ALU.subtract))
                        if tt >= 1:
                            pp = (tt - 1) % 2
                            v.wait_ge(c2.h, RS[tt - 1])
                            if tt - 1 >= 2:
                                wt(v, SY[tt - 3])
                                wt(v, GY[tt - 3])
                            wt(v, LY[tt - 1])
                            c6.inc(v.tensor_scalar(
                                yob[pp][:, :], ypb[pp][:, :],
                                stat_sb[:, pp:pp + 1],
                                stat_sb[:, 4 + pp:5 + pp],
                                ALU.subtract, ALU.mult))
                    pp = 15 % 2
                    v.wait_ge(c2.h, RS[15])
                    wt(v, SY[13])
                    wt(v, GY[13])
                    wt(v, LY[15])
                    c6.inc(v.tensor_scalar(
                        yob[pp][:, :], ypb[pp][:, :],
                        stat_sb[:, pp:pp + 1],
                        stat_sb[:, 4 + pp:5 + pp],
                        ALU.subtract, ALU.mult))
                    endsec(v, c6)

                @blk.scalar
                def _(sc):
                    sc.wait_ge(ph.h, ph_in)
                    for tt in range(16):
                        p = tt % 2
                        sc.wait_ge(c3.h, VA[tt])
                        sch.inc(sc.activation(stat_sb[:, 6 + p:7 + p],
                                              stat_sb[:, 2 + p:3 + p], AF.Ln,
                                              bias=EPS))
                        sc.wait_ge(sch.h, sch.n)
                        c2.inc(sc.activation(stat_sb[:, 4 + p:5 + p],
                                             stat_sb[:, 6 + p:7 + p],
                                             AF.Exp, scale=-0.5))
                    endsec(sc, c2)

            assert ph.n == ph_in + 4

        if _KPH <= 6:
            _fin()
            return nc

        with ExitStack() as P6:
            xs_sb = P6.enter_context(nc.sbuf_tensor("xs_sb", [128, 8 * TH], BF16))
            mid1f = P6.enter_context(nc.sbuf_tensor("mid1f", [2, TH], BF16))
            mid3f = P6.enter_context(nc.sbuf_tensor("mid3f", [2, TH], BF16))
            mid2f = P6.enter_context(nc.sbuf_tensor("mid2f", [2, TH], BF16))

            with ExitStack() as P:
                yT0 = P.enter_context(nc.sbuf_tensor("yT0", [128, 32 * 512], BF16))
                yT1 = P.enter_context(nc.sbuf_tensor("yT1", [128, 32 * 512], BF16))
                xtmp = P.enter_context(nc.sbuf_tensor("xtmp", [128, 512], F32))
                pm10 = P.enter_context(nc.psum_tensor("pm10", [2, 512], F32))
                pm11 = P.enter_context(nc.psum_tensor("pm11", [2, 512], F32))
                pm30 = P.enter_context(nc.psum_tensor("pm30", [2, 512], F32))
                pm31 = P.enter_context(nc.psum_tensor("pm31", [2, 512], F32))
                yTb = [yT0, yT1]
                pm1 = [pm10, pm11]
                pm3 = [pm30, pm31]

                ph_in = ph.n
                YT = {}
                nyt = [dm[6].n, dm[7].n]
                for s_ in range(4):
                    nyt[s_ % 2] += 32 * 16
                    YT[s_] = (dm[6 + s_ % 2], nyt[s_ % 2])
                MID = {}
                n = c1.n
                for s_ in range(4):
                    for l_ in (0, 1):
                        n += 1
                        MID[(s_, l_)] = n
                MIDE = {}
                n = c2.n
                for s_ in range(4):
                    for l_ in (0, 1):
                        n += 1
                        MIDE[(s_, l_)] = n
                XSD = {s_: c6.n + s_ + 1 for s_ in range(4)}

                with nc.Block() as blk:
                    @blk.sync
                    def _(sy):
                        sy.wait_ge(ph.h, ph_in)
                        for s_ in range(4):
                            if s_ >= 2:
                                sy.wait_ge(c6.h, XSD[s_ - 2])
                                sy.wait_ge(c1.h, MID[(s_ - 2, 1)])
                            for dc in range(32):
                                dm[6 + s_ % 2].inc(sy.dma_start(
                                    yTb[s_ % 2][:, dc * 512:(dc + 1) * 512],
                                    ybf.ap()[s_ * 512:(s_ + 1) * 512,
                                             dc * 128:(dc + 1) * 128],
                                    transpose=True), 16)
                        for i_ in (6, 7):
                            sy.wait_ge(dm[i_].h, dm[i_].n)
                        ph.inc(sy.nop())

                    @blk.tensor
                    def _(t):
                        t.wait_ge(ph.h, ph_in)
                        for s_ in range(4):
                            wt(t, YT[s_])
                            for l_, (aT, pm) in enumerate(
                                    ((a1T_sb, pm1), (a3T_sb, pm3))):
                                if s_ >= 2:
                                    t.wait_ge(c2.h, MIDE[(s_ - 2, l_)])
                                for dc in range(32):
                                    mm = t.matmul(
                                        pm[s_ % 2][:, :],
                                        aT[:, dc * 2:dc * 2 + 2],
                                        yTb[s_ % 2][:, dc * 512:(dc + 1) * 512],
                                        start=(dc == 0), stop=(dc == 31))
                                c1.inc(mm)
                        endsec(t, c1)

                    @blk.scalar
                    def _(sc):
                        sc.wait_ge(ph.h, ph_in)
                        for s_ in range(4):
                            for l_, (mf, pm) in enumerate(
                                    ((mid1f, pm1), (mid3f, pm3))):
                                sc.wait_ge(c1.h, MID[(s_, l_)])
                                c2.inc(sc.mul(
                                    mf[0:2, s_ * 512:(s_ + 1) * 512],
                                    pm[s_ % 2][:, :], 0.5))
                        endsec(sc, c2)

                    @blk.vector
                    def _(v):
                        v.wait_ge(ph.h, ph_in)
                        for s_ in range(4):
                            wt(v, YT[s_])
                            v.wait_ge(c6.h, c6.n)
                            for j in range(8):
                                yb = yTb[s_ % 2]
                                v.wait_ge(vch.h, vch.n)
                                vch.inc(v.tensor_add(
                                    xtmp[:, :],
                                    yb[:, j * 512:(j + 1) * 512],
                                    yb[:, (8 + j) * 512:(9 + j) * 512]))
                                v.wait_ge(vch.h, vch.n)
                                vch.inc(v.tensor_add(
                                    xtmp[:, :], xtmp[:, :],
                                    yb[:, (16 + j) * 512:(17 + j) * 512]))
                                v.wait_ge(vch.h, vch.n)
                                mm = v.tensor_add(
                                    xs_sb[:, j * TH + s_ * 512:
                                          j * TH + s_ * 512 + 512],
                                    xtmp[:, :],
                                    yb[:, (24 + j) * 512:(25 + j) * 512])
                                if j < 7:
                                    vch.inc(mm)
                            c6.inc(mm)
                        endsec(v, c6)

                assert ph.n == ph_in + 4

            if _KPH <= 7:
                _fin()
                return nc

            with ExitStack() as P:
                b1T_s = P.enter_context(nc.sbuf_tensor("b1T_s", [2, HF], BF16))
                b3T_s = P.enter_context(nc.sbuf_tensor("b3T_s", [2, HF], BF16))
                w1c0 = P.enter_context(nc.sbuf_tensor("w1c0", [128, 8 * 128], BF16))
                w1c1 = P.enter_context(nc.sbuf_tensor("w1c1", [128, 8 * 128], BF16))
                w3c0 = P.enter_context(nc.sbuf_tensor("w3c0", [128, 8 * 128], BF16))
                w3c1 = P.enter_context(nc.sbuf_tensor("w3c1", [128, 8 * 128], BF16))
                u0 = P.enter_context(nc.sbuf_tensor("u0", [128, TH], BF16))
                u1 = P.enter_context(nc.sbuf_tensor("u1", [128, TH], BF16))
                sil0 = P.enter_context(nc.sbuf_tensor("sil0", [128, 512], F32))
                sil1 = P.enter_context(nc.sbuf_tensor("sil1", [128, 512], F32))
                ph10 = P.enter_context(nc.psum_tensor("ph10", [128, 512], F32))
                ph11 = P.enter_context(nc.psum_tensor("ph11", [128, 512], F32))
                ph30 = P.enter_context(nc.psum_tensor("ph30", [128, 512], F32))
                ph31 = P.enter_context(nc.psum_tensor("ph31", [128, 512], F32))
                pq0 = P.enter_context(nc.psum_tensor("pq0", [2, 512], F32))
                pq1 = P.enter_context(nc.psum_tensor("pq1", [2, 512], F32))
                pq2 = P.enter_context(nc.psum_tensor("pq2", [2, 512], F32))
                pq3 = P.enter_context(nc.psum_tensor("pq3", [2, 512], F32))
                w1cb = [w1c0, w1c1]
                w3cb = [w3c0, w3c1]
                ub = [u0, u1]
                silb = [sil0, sil1]
                ph1 = [ph10, ph11]
                ph3 = [ph30, ph31]
                pqm = [pq0, pq1, pq2, pq3]

                ph_in = ph.n
                js_of = lambda hc, s_: hc * 4 + s_
                BL = (dm[13], dm[13].n + 32)
                W1L = {}
                W3L = {}
                UST = {}
                nw1 = [dm[6].n, dm[7].n]
                nw3 = [dm[18].n, dm[19].n]
                nus = [dm[14].n, dm[15].n]
                for hc in range(86):
                    nw1[hc % 2] += 16
                    W1L[hc] = (dm[6 + hc % 2], nw1[hc % 2])
                    nw3[hc % 2] += 16
                    W3L[hc] = (dm[18 + hc % 2], nw3[hc % 2])
                for hc in range(86):
                    nus[hc % 2] += 16
                    UST[hc] = (dm[14 + hc % 2], nus[hc % 2])
                H1 = {}
                H3 = {}
                MID2 = {}
                n = c1.n
                pq6 = []
                for hc in range(86):
                    for s_ in range(4):
                        if len(pq6) >= 2:
                            jp6 = pq6.pop(0)
                            n += 1
                            MID2[jp6] = n
                        n += 1
                        H1[(hc, s_)] = n
                        n += 1
                        H3[(hc, s_)] = n
                        pq6.append((hc, s_))
                for jp6 in pq6:
                    n += 1
                    MID2[jp6] = n
                SIL = {}
                n = c2.n
                for hc in range(86):
                    for s_ in range(4):
                        n += 1
                        SIL[(hc, s_)] = n
                M2E = {}
                for s_ in range(4):
                    n += 1
                    M2E[s_] = n
                UD = {}
                n = c3.n
                for hc in range(86):
                    for s_ in range(4):
                        n += 1
                        UD[(hc, s_)] = n

                with nc.Block() as blk:
                    @blk.sync
                    def _(sy):
                        sy.wait_ge(ph.h, ph_in)
                        dm[13].inc(sy.dma_start(b1T_s[0:2, :], b1td.ap()), 16)
                        dm[13].inc(sy.dma_start(b3T_s[0:2, :], b3td.ap()), 16)
                        for hc in range(86):
                            if hc >= 2:
                                sy.wait_ge(c1.h, H3[(hc - 2, 3)])
                            W1L[hc][0].inc(sy.dma_start(
                                sb3(w1cb[hc % 2][:, :], 8),
                                dr3(w1t.ap()[:, hc * 128:(hc + 1) * 128])), 16)
                            W3L[hc][0].inc(sy.dma_start(
                                sb3(w3cb[hc % 2][:, :], 8),
                                dr3(w3t.ap()[:, hc * 128:(hc + 1) * 128])), 16)
                            if hc >= 1:
                                sy.wait_ge(c3.h, UD[(hc - 1, 3)])
                                UST[hc - 1][0].inc(sy.dma_start(
                                    uT.ap()[(hc - 1) * 128:hc * 128, :],
                                    ub[(hc - 1) % 2][:, :]), 16)
                        sy.wait_ge(c3.h, UD[(85, 3)])
                        UST[85][0].inc(sy.dma_start(
                            uT.ap()[85 * 128:86 * 128, :],
                            ub[85 % 2][:, :]), 16)
                        sy.wait_ge(c2.h, M2E[3])
                        dm[13].inc(sy.dma_start(mid2d.ap(), mid2f[0:2, :]), 16)
                        for i_ in (6, 7, 13, 14, 15, 18, 19):
                            sy.wait_ge(dm[i_].h, dm[i_].n)
                        ph.inc(sy.nop())

                    @blk.tensor
                    def _(t):
                        t.wait_ge(ph.h, ph_in)
                        pq6 = []

                        def flush6(t, force=False):
                            if not pq6 or (not force and len(pq6) < 2):
                                return
                            hp, sp = pq6.pop(0)
                            t.wait_ge(c3.h, UD[(hp, sp)])
                            c1.inc(t.matmul(
                                pqm[sp][:, :],
                                a2T_sb[:, hp * 2:hp * 2 + 2],
                                ub[hp % 2][:, sp * 512:(sp + 1) * 512],
                                start=(hp == 0), stop=(hp == 85)))

                        for hc in range(86):
                            for s_ in range(4):
                                js = js_of(hc, s_)
                                flush6(t)
                                if s_ == 0:
                                    wt(t, W1L[hc])
                                    wt(t, W3L[hc])
                                    if hc == 0:
                                        wt(t, BL)
                                if js >= 2:
                                    pj = (js - 2) // 4, (js - 2) % 4
                                    t.wait_ge(c3.h, UD[pj])
                                for wcb, bT, mf, php in (
                                        (w1cb, b1T_s, mid1f, ph1),
                                        (w3cb, b3T_s, mid3f, ph3)):
                                    for dc in range(8):
                                        t.matmul(
                                            php[s_ % 2][:, :],
                                            wcb[hc % 2][:, dc * 128:(dc + 1) * 128],
                                            xs_sb[:, dc * TH + s_ * 512:
                                                  dc * TH + s_ * 512 + 512],
                                            start=(dc == 0), stop=False)
                                    mm = t.matmul(
                                        php[s_ % 2][:, :],
                                        bT[0:2, hc * 128:(hc + 1) * 128],
                                        mf[0:2, s_ * 512:(s_ + 1) * 512],
                                        start=False, stop=True)
                                    c1.inc(mm)
                                pq6.append((hc, s_))
                        while pq6:
                            flush6(t, force=True)
                        endsec(t, c1)

                    @blk.scalar
                    def _(sc):
                        sc.wait_ge(ph.h, ph_in)
                        for hc in range(86):
                            for s_ in range(4):
                                js = js_of(hc, s_)
                                sc.wait_ge(c1.h, H1[(hc, s_)])
                                if js >= 2:
                                    sc.wait_ge(c3.h, UD[((js - 2) // 4,
                                                        (js - 2) % 4)])
                                c2.inc(sc.activation(
                                    silb[js % 2][:, :],
                                    ph1[s_ % 2][:, :], AF.Silu))
                        sc.wait_ge(c1.h, MID2[(85, 3)])
                        for s_ in range(4):
                            c2.inc(sc.mul(mid2f[0:2, s_ * 512:(s_ + 1) * 512],
                                          pqm[s_][:, :], 0.5))
                        endsec(sc, c2)

                    @blk.vector
                    def _(v):
                        v.wait_ge(ph.h, ph_in)
                        for hc in range(86):
                            for s_ in range(4):
                                js = js_of(hc, s_)
                                v.wait_ge(c2.h, SIL[(hc, s_)])
                                v.wait_ge(c1.h, H3[(hc, s_)])
                                if s_ == 0 and hc >= 2:
                                    wt(v, UST[hc - 2])
                                    v.wait_ge(c1.h, MID2[(hc - 2, 3)])
                                c3.inc(v.tensor_mul(
                                    ub[hc % 2][:, s_ * 512:(s_ + 1) * 512],
                                    silb[js % 2][:, :],
                                    ph3[s_ % 2][:, :]))
                        endsec(v, c3)

                assert ph.n == ph_in + 4

        if _KPH <= 8:
            _fin()
            return nc

        with ExitStack() as P:
            us_sb = P.enter_context(nc.sbuf_tensor("us_sb", [128, 22 * TH], BF16))
            with ExitStack() as Pa:
                ul0 = Pa.enter_context(nc.sbuf_tensor("ul0", [128, 4 * TH], BF16))
                ul1 = Pa.enter_context(nc.sbuf_tensor("ul1", [128, 4 * TH], BF16))
                utmp = Pa.enter_context(nc.sbuf_tensor("utmp", [128, TH], F32))
                ulb = [ul0, ul1]

                ph_in = ph.n
                UL = {}
                nul = [dm[6].n, dm[7].n]
                for j in range(22):
                    nul[j % 2] += 4 * 16
                    UL[j] = (dm[6 + j % 2], nul[j % 2])
                FD = {j: c6.n + j + 1 for j in range(22)}

                with nc.Block() as blk:
                    @blk.sync
                    def _(sy):
                        sy.wait_ge(ph.h, ph_in)
                        for j in range(22):
                            rows = 128 if j < 21 else 64
                            if j >= 2:
                                sy.wait_ge(c6.h, FD[j - 2])
                            for c_ in range(4):
                                r0 = c_ * HQ + j * 128
                                dm[6 + j % 2].inc(sy.dma_start(
                                    ulb[j % 2][0:rows, c_ * TH:c_ * TH + TH],
                                    uT.ap()[r0:r0 + rows, :]), 16)
                        for i_ in (6, 7):
                            sy.wait_ge(dm[i_].h, dm[i_].n)
                        ph.inc(sy.nop())

                    @blk.vector
                    def _(v):
                        v.wait_ge(ph.h, ph_in)
                        for j in range(22):
                            rows = 128 if j < 21 else 64
                            wt(v, UL[j])
                            ub_ = ulb[j % 2]
                            v.wait_ge(vch.h, vch.n)
                            v.wait_ge(c6.h, c6.n)
                            vch.inc(v.tensor_add(utmp[0:rows, :],
                                                 ub_[0:rows, 0:TH],
                                                 ub_[0:rows, TH:2 * TH]))
                            v.wait_ge(vch.h, vch.n)
                            vch.inc(v.tensor_add(utmp[0:rows, :],
                                                 utmp[0:rows, :],
                                                 ub_[0:rows, 2 * TH:3 * TH]))
                            v.wait_ge(vch.h, vch.n)
                            c6.inc(v.tensor_add(
                                us_sb[0:rows, j * TH:(j + 1) * TH],
                                utmp[0:rows, :],
                                ub_[0:rows, 3 * TH:4 * TH]))
                        endsec(v, c6)

                assert ph.n == ph_in + 2

            with ExitStack() as P:
                w2c0 = P.enter_context(nc.sbuf_tensor("w2c0", [128, 22 * 256], BF16))
                w2c1 = P.enter_context(nc.sbuf_tensor("w2c1", [128, 22 * 256], BF16))
                yr0 = P.enter_context(nc.sbuf_tensor("yr0", [128, 16 * 256], F32))
                yr1 = P.enter_context(nc.sbuf_tensor("yr1", [128, 16 * 256], F32))
                zt0 = P.enter_context(nc.sbuf_tensor("zt0", [128, 16 * 256], F32))
                zt1 = P.enter_context(nc.sbuf_tensor("zt1", [128, 16 * 256], F32))
                b2T_s = P.enter_context(nc.sbuf_tensor("b2T_s", [2, D], BF16))
                mid2s = P.enter_context(nc.sbuf_tensor("mid2s", [2, TH], BF16))
                trs2 = P.enter_context(nc.sbuf_tensor("trs2", [128, 256], F32))
                pz0 = P.enter_context(nc.psum_tensor("pz0", [128, 256], F32))
                pz1 = P.enter_context(nc.psum_tensor("pz1", [128, 256], F32))
                pz2 = P.enter_context(nc.psum_tensor("pz2", [128, 256], F32))
                w2cb = [w2c0, w2c1]
                yrb = [yr0, yr1]
                ztb = [zt0, zt1]
                pz = [pz0, pz1, pz2]

                ph_in = ph.n
                jobs7 = [(oc, tt) for oc in range(16) for tt in range(16)]
                j7 = {jj: i for i, jj in enumerate(jobs7)}
                BM2L = (dm[13], dm[13].n + 32)
                W2L = {}
                YRL = {}
                ZST = {}
                nw2 = [dm[6].n, dm[7].n]
                nyr = [dm[18].n, dm[19].n]
                nzs = [dm[14].n, dm[15].n]
                for oc in range(16):
                    nw2[oc % 2] += 16
                    W2L[oc] = (dm[6 + oc % 2], nw2[oc % 2])
                    nyr[oc % 2] += 16
                    YRL[oc] = (dm[18 + oc % 2], nyr[oc % 2])
                for oc in range(16):
                    nzs[oc % 2] += 16
                    ZST[oc] = (dm[14 + oc % 2], nzs[oc % 2])
                ZM = {}
                n = c1.n
                for jj in jobs7:
                    n += 1
                    ZM[jj] = n
                ZEV = {}
                n = c6.n
                for jj in jobs7:
                    n += 1
                    ZEV[jj] = n
                ZSQ = {}
                n = c2.n
                for jj in jobs7:
                    n += 1
                    ZSQ[jj] = n

                with nc.Block() as blk:
                    @blk.sync
                    def _(sy):
                        sy.wait_ge(ph.h, ph_in)
                        dm[13].inc(sy.dma_start(b2T_s[0:2, :], b2td.ap()), 16)
                        dm[13].inc(sy.dma_start(mid2s[0:2, :], mid2d.ap()), 16)
                        for oc in range(16):
                            if oc >= 2:
                                sy.wait_ge(c1.h, ZM[(oc - 2, 15)])
                            W2L[oc][0].inc(sy.dma_start(
                                sb3(w2cb[oc % 2][:, :], 22),
                                dr3(w2t.ap()[:, oc * 256:(oc + 1) * 256])), 16)
                            if oc >= 2:
                                sy.wait_ge(c6.h, ZEV[(oc - 2, 15)])
                            YRL[oc][0].inc(sy.dma_start(
                                sb3(yrb[oc % 2][:, :], 16),
                                dr3(y32.ap()[:, oc * 256:(oc + 1) * 256])), 16)
                            if oc >= 1:
                                sy.wait_ge(c6.h, ZEV[(oc - 1, 15)])
                                ZST[oc - 1][0].inc(sy.dma_start(
                                    dr3(zpre.ap()[:, (oc - 1) * 256:
                                                  oc * 256]),
                                    sb3(ztb[(oc - 1) % 2][:, :], 16)), 16)
                        sy.wait_ge(c6.h, ZEV[(15, 15)])
                        ZST[15][0].inc(sy.dma_start(
                            dr3(zpre.ap()[:, 15 * 256:16 * 256]),
                            sb3(ztb[15 % 2][:, :], 16)), 16)
                        for i_ in (6, 7, 13, 14, 15, 18, 19):
                            sy.wait_ge(dm[i_].h, dm[i_].n)
                        ph.inc(sy.nop())

                    @blk.tensor
                    def _(t):
                        t.wait_ge(ph.h, ph_in)
                        for jj in jobs7:
                            oc, tt = jj
                            i = j7[jj]
                            if tt == 0:
                                wt(t, W2L[oc])
                                if oc == 0:
                                    wt(t, BM2L)
                            if i >= 3:
                                t.wait_ge(c6.h, ZEV[jobs7[i - 3]])
                            for j in range(22):
                                rows = 128 if j < 21 else 64
                                t.matmul(
                                    pz[i % 3][:, :],
                                    us_sb[0:rows, j * TH + tt * 128:
                                          j * TH + tt * 128 + 128],
                                    w2cb[oc % 2][0:rows,
                                                 j * 256:(j + 1) * 256],
                                    start=(j == 0), stop=False)
                            mm = t.matmul(
                                pz[i % 3][:, :],
                                mid2s[0:2, tt * 128:(tt + 1) * 128],
                                b2T_s[0:2, oc * 256:(oc + 1) * 256],
                                start=False, stop=True)
                            c1.inc(mm)
                        endsec(t, c1)

                    @blk.vector
                    def _(v):
                        v.wait_ge(ph.h, ph_in)
                        for jj in jobs7:
                            oc, tt = jj
                            i = j7[jj]
                            v.wait_ge(c1.h, ZM[jj])
                            if tt == 0:
                                wt(v, YRL[oc])
                                if oc >= 2:
                                    wt(v, ZST[oc - 2])
                                    v.wait_ge(c2.h, ZSQ[(oc - 2, 15)])
                            c6.inc(v.scalar_tensor_tensor(
                                ztb[oc % 2][:, tt * 256:(tt + 1) * 256],
                                pz[i % 3][:, :], 1.0,
                                yrb[oc % 2][:, tt * 256:(tt + 1) * 256],
                                ALU.mult, ALU.add,
                                accum_out=stat_sb[:, 2 + i % 2:3 + i % 2]))
                            v.wait_ge(c6.h, ZEV[jj])
                            v.wait_ge(vch.h, vch.n)
                            if oc == 0:
                                vch.inc(v.tensor_copy(
                                    zsum_sb[:, tt:tt + 1],
                                    stat_sb[:, 2 + i % 2:3 + i % 2]))
                            else:
                                vch.inc(v.tensor_add(
                                    zsum_sb[:, tt:tt + 1],
                                    zsum_sb[:, tt:tt + 1],
                                    stat_sb[:, 2 + i % 2:3 + i % 2]))
                            if i >= 1:
                                poc, ptt = jobs7[i - 1]
                                v.wait_ge(c2.h, ZSQ[jobs7[i - 1]])
                                v.wait_ge(vch.h, vch.n)
                                cell = stat_sb[:, 4 + (i - 1) % 2:
                                               5 + (i - 1) % 2]
                                if poc == 0:
                                    vch.inc(v.tensor_copy(
                                        zsq_sb[:, ptt:ptt + 1], cell))
                                else:
                                    vch.inc(v.tensor_add(
                                        zsq_sb[:, ptt:ptt + 1],
                                        zsq_sb[:, ptt:ptt + 1], cell))
                        i = len(jobs7)
                        poc, ptt = jobs7[-1]
                        v.wait_ge(c2.h, ZSQ[jobs7[-1]])
                        v.wait_ge(vch.h, vch.n)
                        vch.inc(v.tensor_add(
                            zsq_sb[:, ptt:ptt + 1],
                            zsq_sb[:, ptt:ptt + 1],
                            stat_sb[:, 4 + (i - 1) % 2:5 + (i - 1) % 2]))
                        endsec(v, c6)

                    @blk.scalar
                    def _(sc):
                        sc.wait_ge(ph.h, ph_in)
                        for jj in jobs7:
                            oc, tt = jj
                            i = j7[jj]
                            sc.wait_ge(c6.h, ZEV[jj])
                            if i >= 1:
                                sc.wait_ge(c2.h, ZSQ[jobs7[i - 1]])
                            c2.inc(sc.activation(
                                trs2[:, :],
                                ztb[oc % 2][:, tt * 256:(tt + 1) * 256],
                                AF.Square,
                                accum_out=stat_sb[:, 4 + i % 2:5 + i % 2]))
                        endsec(sc, c2)

                assert ph.n == ph_in + 4

        if _KPH <= 10:
            _fin()
            return nc

        with ExitStack() as P:
            zp0 = P.enter_context(nc.sbuf_tensor("zp0", [128, D], F32))
            zp1 = P.enter_context(nc.sbuf_tensor("zp1", [128, D], F32))
            oo0 = P.enter_context(nc.sbuf_tensor("oo0", [128, D], F32))
            oo1 = P.enter_context(nc.sbuf_tensor("oo1", [128, D], F32))
            zpb = [zp0, zp1]
            oob = [oo0, oo1]

            ph_in = ph.n
            ZL = {}
            OST = {}
            nzl = [dm[6].n, dm[7].n]
            nos = [dm[14].n, dm[15].n]
            for tt in range(16):
                nzl[tt % 2] += 16
                ZL[tt] = (dm[6 + tt % 2], nzl[tt % 2])
                nos[tt % 2] += 16
                OST[tt] = (dm[14 + tt % 2], nos[tt % 2])
            VA8 = {tt: c3.n + tt + 1 for tt in range(16)}
            RS8 = {tt: c2.n + tt + 1 for tt in range(16)}
            NM8 = {tt: c6.n + tt + 1 for tt in range(16)}

            with nc.Block() as blk:
                @blk.sync
                def _(sy):
                    sy.wait_ge(ph.h, ph_in)
                    for tt in range(16):
                        if tt >= 2:
                            sy.wait_ge(c6.h, NM8[tt - 2])
                        ZL[tt][0].inc(sy.dma_start(
                            zpb[tt % 2][:, :],
                            zpre.ap()[tt * 128:(tt + 1) * 128, :]), 16)
                        sy.wait_ge(c6.h, NM8[tt])
                        OST[tt][0].inc(sy.dma_start(
                            o.ap()[tt * 128:(tt + 1) * 128, :],
                            oob[tt % 2][:, :]), 16)
                    sy.wait_ge(dm[14].h, dm[14].n)
                    sy.wait_ge(dm[15].h, dm[15].n)
                    ph.inc(sy.nop())

                @blk.vector
                def _(v):
                    v.wait_ge(ph.h, ph_in)
                    for tt in range(16):
                        p = tt % 2
                        if tt >= 2:
                            v.wait_ge(c2.h, RS8[tt - 2])
                            v.wait_ge(c6.h, NM8[tt - 2])
                        v.wait_ge(vch.h, vch.n)
                        vch.inc(v.tensor_scalar_mul(
                            stat_sb[:, p:p + 1],
                            zsum_sb[:, tt:tt + 1], 1.0 / D))
                        v.wait_ge(vch.h, vch.n)
                        vch.inc(v.tensor_mul(stat_sb[:, 6 + p:7 + p],
                                             stat_sb[:, p:p + 1],
                                             stat_sb[:, p:p + 1]))
                        v.wait_ge(vch.h, vch.n)
                        c3.inc(v.scalar_tensor_tensor(
                            stat_sb[:, 2 + p:3 + p],
                            zsq_sb[:, tt:tt + 1], 1.0 / D,
                            stat_sb[:, 6 + p:7 + p],
                            ALU.mult, ALU.subtract))
                        if tt >= 1:
                            pp = (tt - 1) % 2
                            v.wait_ge(c2.h, RS8[tt - 1])
                            if tt - 1 >= 2:
                                wt(v, OST[tt - 3])
                            wt(v, ZL[tt - 1])
                            c6.inc(v.tensor_scalar(
                                oob[pp][:, :], zpb[pp][:, :],
                                stat_sb[:, pp:pp + 1],
                                stat_sb[:, 4 + pp:5 + pp],
                                ALU.subtract, ALU.mult))
                    pp = 15 % 2
                    v.wait_ge(c2.h, RS8[15])
                    wt(v, OST[13])
                    wt(v, ZL[15])
                    c6.inc(v.tensor_scalar(
                        oob[pp][:, :], zpb[pp][:, :],
                        stat_sb[:, pp:pp + 1],
                        stat_sb[:, 4 + pp:5 + pp],
                        ALU.subtract, ALU.mult))
                    endsec(v, c6)

                @blk.scalar
                def _(sc):
                    sc.wait_ge(ph.h, ph_in)
                    for tt in range(16):
                        p = tt % 2
                        sc.wait_ge(c3.h, VA8[tt])
                        sch.inc(sc.activation(stat_sb[:, 6 + p:7 + p],
                                              stat_sb[:, 2 + p:3 + p], AF.Ln,
                                              bias=EPS))
                        sc.wait_ge(sch.h, sch.n)
                        c2.inc(sc.activation(stat_sb[:, 4 + p:5 + p],
                                             stat_sb[:, 6 + p:7 + p],
                                             AF.Exp, scale=-0.5))
                    endsec(sc, c2)

    return nc


_NC_CACHE = []


def _get_program():
    if not _NC_CACHE:
        _NC_CACHE.append(build_program())
    return _NC_CACHE[0]


def _consts():
    inv_freq = (1.0 / (500000.0 **
                       (np.arange(0, HD, 2, dtype=np.float32) / HD))
                ).astype(np.float32)
    t = np.arange(T, dtype=np.float32)
    freqs = np.outer(t, inv_freq).astype(np.float32)
    cosT = np.cos(freqs).T.astype(np.float32)
    sinT = np.sin(freqs).T.astype(np.float32)
    csa_c = np.empty((128, T), dtype=ml_dtypes.bfloat16)
    csa_c[0:64] = cosT
    csa_c[64:128] = cosT
    csb_c = np.empty((128, T), dtype=ml_dtypes.bfloat16)
    csb_c[0:64] = -sinT
    csb_c[64:128] = sinT
    swp_c = np.zeros((128, 128), dtype=ml_dtypes.bfloat16)
    for m in range(128):
        swp_c[(m + 64) % 128, m] = 1.0
    ident_c = np.eye(128, dtype=ml_dtypes.bfloat16)
    ones_c = np.ones((128, 128), dtype=ml_dtypes.bfloat16)
    kt = np.arange(128)[:, None]
    qt = np.arange(512)[None, :]
    maskp_c = np.zeros((128, 2048), dtype=ml_dtypes.bfloat16)
    for j2 in range(2):
        for half in range(2):
            jj = 2 * j2 + half
            m = (jj * 128 + kt <= qt).astype(ml_dtypes.bfloat16)
            maskp_c[:, j2 * 1024 + half * 512:
                    j2 * 1024 + half * 512 + 512] = m
    return csa_c, csb_c, swp_c, ident_c, ones_c, maskp_c


def kernel(L0_attn_x, L0_attn_Wq, L0_attn_Wk, L0_attn_Wv, L0_attn_Wo,
           L0_ffn_w1q, L0_ffn_a1, L0_ffn_b1, L0_ffn_w3q, L0_ffn_a3,
           L0_ffn_b3, L0_ffn_w2q, L0_ffn_a2, L0_ffn_b2):
    xall = np.ascontiguousarray(np.asarray(L0_attn_x, dtype=np.float32))
    Wq = np.asarray(L0_attn_Wq, dtype=np.float32)
    Wk = np.asarray(L0_attn_Wk, dtype=np.float32)
    Wv = np.asarray(L0_attn_Wv, dtype=np.float32)
    Wo = np.ascontiguousarray(np.asarray(L0_attn_Wo, dtype=np.float32))
    w1q = np.ascontiguousarray(np.asarray(L0_ffn_w1q, dtype=np.float32))
    w3q = np.ascontiguousarray(np.asarray(L0_ffn_w3q, dtype=np.float32))
    w2q = np.ascontiguousarray(np.asarray(L0_ffn_w2q, dtype=np.float32))
    a1_ = np.ascontiguousarray(np.asarray(L0_ffn_a1, dtype=np.float32))
    b1_ = np.ascontiguousarray(np.asarray(L0_ffn_b1, dtype=np.float32))
    a3_ = np.ascontiguousarray(np.asarray(L0_ffn_a3, dtype=np.float32))
    b3_ = np.ascontiguousarray(np.asarray(L0_ffn_b3, dtype=np.float32))
    a2_ = np.ascontiguousarray(np.asarray(L0_ffn_a2, dtype=np.float32))
    b2_ = np.ascontiguousarray(np.asarray(L0_ffn_b2, dtype=np.float32))

    csa_c, csb_c, swp_c, ident_c, ones_c, maskp_c = _consts()

    in_maps = []
    for c in range(8):
        b = c // 2
        s = c % 2
        in_maps.append({
            "x": xall[b],
            "xh": np.ascontiguousarray(xall[b][s * TH:(s + 1) * TH]),
            "wq": np.ascontiguousarray(Wq[:, s * 2048:(s + 1) * 2048]),
            "wk": np.ascontiguousarray(Wk[:, s * 512:(s + 1) * 512]),
            "wv": np.ascontiguousarray(Wv[:, s * 512:(s + 1) * 512]),
            "wo": Wo,
            "w1": w1q, "w3": w3q, "w2": w2q,
            "a1": a1_, "b1": b1_, "a3": a3_, "b3": b3_,
            "a2": a2_, "b2": b2_,
            "csa": csa_c, "csb": csb_c, "swp": swp_c,
            "ident": ident_c, "onesb": ones_c,
            "maskp": maskp_c,
            "agoff": np.array([[s * D * (TH // 2)]], dtype=np.int32),
        })

    nc = _get_program()
    res = run_bass_kernel_spmd(nc, in_maps, core_ids=list(range(8)))
    out = np.empty((B, T, D), dtype=np.float32)
    for c in range(8):
        b = c // 2
        s = c % 2
        out[b, s * TH:(s + 1) * TH] = np.asarray(res.results[c]["o"])
    return out
